# version 02
# baseline (speedup 1.0000x reference)
import numpy as np
from contextlib import ExitStack

import concourse.bass as bass
import concourse.mybir as mybir
from concourse.bass_utils import run_bass_kernel_spmd

F32 = mybir.dt.float32
BF16 = mybir.dt.bfloat16
U32 = mybir.dt.uint32
AF = mybir.ActivationFunctionType
ALU = mybir.AluOpType
AX = mybir.AxisListType
IDX = bass.IndirectOffsetOnAxis

T = 4096
NTC = T // 128
DM = 1024
NEG = -1.0e30
C1 = 512
C0 = 1408

ENGINES = ("sync", "gpsimd", "tensor", "vector", "scalar")


class Prog:

    def __init__(self, nc, ctx):
        self.nc = nc
        self.ctx = ctx
        self.sems = {}
        self.cnt = {}
        self.dma_like = set()
        self.q = None
        self.hwm = None

    def semh(self, name):
        if name not in self.sems:
            self.sems[name] = self.ctx.enter_context(self.nc.semaphore(name))
            self.cnt[name] = 0
        return self.sems[name]

    def phase_begin(self):
        self.q = {e: [] for e in ENGINES}
        self.hwm = {e: {} for e in ENGINES}

    def op(self, eng, fn, waits=(), inc=None, n=1, drain=True):
        wmax = {}
        for item in waits:
            if item is None:
                continue
            s, v = item[0], item[1]
            if s not in self.dma_like and len(item) > 2 and item[2] == eng:
                continue
            if v > wmax.get(s, -1):
                wmax[s] = v
        w = []
        for s, v in wmax.items():
            if self.hwm[eng].get(s, 0) >= v:
                continue
            self.hwm[eng][s] = v
            w.append((s, v))
        mark = None
        if inc is not None:
            self.semh(inc)
            self.cnt[inc] += n
            mark = (inc, self.cnt[inc], eng)
        do_drain = drain and eng in ("vector", "scalar", "gpsimd")
        self.q[eng].append((w, fn, inc, n, do_drain))
        return mark

    def dma(self, eng, out, in_, waits=(), inc="dma", **kw):
        self.dma_like.add(inc)
        return self.op(eng, lambda e, o=out, i=in_, k=dict(kw): e.dma_start(out=o, in_=i, **k),
                       waits=waits, inc=inc, n=16, drain=(eng == "gpsimd"))

    def mark(self, inc):
        self.semh(inc)
        return (inc, self.cnt[inc], None)

    def phase_end(self):
        nc, q, sems = self.nc, self.q, self.sems
        with nc.Block() as block:
            for eng in ENGINES:
                ops = q[eng]
                if not ops:
                    continue

                def body(ectx, ops=ops):
                    for (w, fn, inc, n, drain) in ops:
                        for (s, v) in w:
                            ectx.wait_ge(sems[s], v)
                        instr = fn(ectx)
                        if inc is not None:
                            instr.then_inc(sems[inc], n)
                        if drain:
                            ectx.drain()

                getattr(block, eng)(body)
        self.q = None



def host_consts():
    c = {}
    HEAD_DIM = 64
    inv_freq = 1.0 / (10000.0 ** (np.arange(0, HEAD_DIM, 2, dtype=np.float64) / HEAD_DIM))
    pos = np.arange(T, dtype=np.float64)
    freqs = np.outer(pos, inv_freq)
    emb = np.concatenate([freqs, freqs], -1)
    cosT = np.cos(emb).T.astype(np.float32)
    sinT = np.sin(emb).T.astype(np.float32)
    c["cosT"] = np.ascontiguousarray(np.concatenate([cosT, cosT], 0))
    c["sinT"] = np.ascontiguousarray(np.concatenate([sinT, sinT], 0))
    R = np.zeros((64, 64), np.float32)
    for i in range(32):
        R[2 * i, 2 * i + 1] = -1.0
        R[2 * i + 1, 2 * i] = 1.0
    R2T = np.zeros((128, 128), np.float32)
    R2T[0:64, 0:64] = R.T
    R2T[64:128, 64:128] = R.T
    c["R2T"] = R2T
    D = 512
    div = np.exp(np.arange(0, D, 2, dtype=np.float64) * (-np.log(10000.0) / D))
    ang = pos[:, None] * div[None, :]
    pe = np.stack([np.sin(ang), np.cos(ang)], axis=-1).reshape(T, D)
    c["peT"] = np.ascontiguousarray(pe.T.astype(np.float32))
    ii = np.arange(128)[:, None]
    jj = np.arange(128)[None, :]
    c["mask_causal"] = np.where(jj <= ii, 0.0, NEG).astype(np.float32)
    c["mask_upper"] = np.where(ii <= jj, 0.0, NEG).astype(np.float32)
    c["mask_glob"] = np.ascontiguousarray(
        np.broadcast_to(np.where(jj < 4, 0.0, NEG), (128, 128)).astype(np.float32))
    c["triuS"] = np.triu(np.ones((128, 128), np.float32), 1)
    c["ident"] = np.eye(128, dtype=np.float32)
    return c


def host_weights(inp):
    asc = np.ascontiguousarray
    w = {}
    w["WqkvT0"] = asc(inp["L0_attn_qkv_w"].T)
    w["WoT0"] = asc(inp["L0_attn_out_w"].T)
    w["WgT"] = asc(inp["L0_ffn_w_gate"].T)
    w["W1eT"] = asc(inp["L0_ffn_w1_e"].transpose(0, 2, 1))
    w["W3eT"] = asc(inp["L0_ffn_w3_e"].transpose(0, 2, 1))
    w["W2eT"] = asc(inp["L0_ffn_w2_e"].transpose(0, 2, 1))
    w["W1sT"] = asc(inp["L0_ffn_w1_s"].T)
    w["W3sT"] = asc(inp["L0_ffn_w3_s"].T)
    w["W2sT"] = asc(inp["L0_ffn_w2_s"].T)
    w["WqkvT1"] = asc(inp["L1_attn_Wqkv"].T)
    w["WoT1"] = asc(inp["L1_attn_Wo"].T)
    w["Wr"] = asc(inp["L1_ffn_router_w"])
    w["Wrb"] = asc(inp["L1_ffn_router_b"].reshape(1, 8))
    w["W1x"] = asc(inp["L1_ffn_w1"])
    w["B1x"] = asc(inp["L1_ffn_b1"].reshape(8, 32, 128).transpose(0, 2, 1))
    w["W2x"] = asc(inp["L1_ffn_w2"])
    w["B2x"] = asc(inp["L1_ffn_b2"].reshape(8, 1, 1024))
    return w


WSHAPES = {
    "WqkvT0": [1024, 3072], "WoT0": [1024, 1024], "WgT": [1024, 16],
    "W1eT": [16, 1024, 512], "W3eT": [16, 1024, 512], "W2eT": [16, 512, 1024],
    "W1sT": [1024, 1024], "W3sT": [1024, 1024], "W2sT": [1024, 1024],
    "WqkvT1": [1024, 12288], "WoT1": [4096, 1024],
    "Wr": [1024, 8], "Wrb": [1, 8], "W1x": [8, 1024, 4096], "B1x": [8, 128, 32],
    "W2x": [8, 4096, 1024], "B2x": [8, 1, 1024],
    "cosT": [128, T], "sinT": [128, T], "R2T": [128, 128], "peT": [512, T],
    "mask_causal": [128, 128], "mask_upper": [128, 128], "mask_glob": [128, 128],
    "triuS": [128, 128],
    "ident": [128, 128],
}


class BC:

    def __init__(self, nc, ctx):
        self.nc = nc
        self.ctx = ctx
        self.ext = {}
        self.p = Prog(nc, ctx)
        self.tp_war = [[], []]

    def ein(self, name):
        if name not in self.ext:
            self.ext[name] = self.nc.dram_tensor(name, WSHAPES[name], F32,
                                                 kind="ExternalInput").ap()
        return self.ext[name]

    def sb(self, name, shape, dt):
        scope = getattr(self, "scope", None) or self.ctx
        return scope.enter_context(self.nc.sbuf_tensor(name, shape, dt))

    def ps(self, name, shape, dt=F32):
        scope = getattr(self, "scope", None) or self.ctx
        return scope.enter_context(self.nc.psum_tensor(name, shape, dt))



def emit_ln(bc, pref, x_src, identb, tp_ps, xnT_dram=None, xn_dram=None,
            xnT_sb=None, fp32_cb=None, wait0=()):
    p = bc.p
    xbuf = [bc.sb(pref + "xb%d" % i, [128, DM], F32) for i in range(2)]
    xnb = [bc.sb(pref + "xnb%d" % i, [128, DM], BF16) for i in range(2)]
    xn32 = [bc.sb(pref + "xn32%d" % i, [128, DM], F32) for i in range(2)] \
        if fp32_cb is not None else None
    sq = bc.sb(pref + "sq", [128, DM], BF16)
    st = [bc.sb(pref + "st%d" % i, [128, 4], F32) for i in range(2)]
    stg = [bc.sb(pref + "stg%d" % i, [128, DM], BF16) for i in range(2)] \
        if xnT_dram is not None else None

    free = [list(wait0), list(wait0)]
    stg_free = [[], []]
    last = []
    for tc in range(NTC):
        par = tc & 1
        s = st[par]
        ld = p.dma("sync", xbuf[par][:], x_src[tc * 128:(tc + 1) * 128, :],
                   waits=free[par], inc="x%d" % par)
        p.op("vector", lambda v, a=xbuf[par], s=s: v.reduce_sum(
            out=s[:, 0:1], in_=a[:], axis=AX.X), waits=[ld], inc="sv")
        msq = p.op("scalar", lambda sc, a=xbuf[par], s=s: sc.activation(
            out=sq[:], in_=a[:], func=AF.Square, accum_out=s[:, 1:2]),
            waits=[ld], inc="sa")
        p.op("vector", lambda v, s=s: v.tensor_scalar(
            out=s[:, 2:3], in0=s[:, 0:1], scalar1=1.0 / DM, scalar2=None,
            op0=ALU.mult), inc="sv")
        p.op("vector", lambda v, s=s: v.tensor_scalar(
            out=s[:, 1:2], in0=s[:, 1:2], scalar1=1.0 / DM, scalar2=1e-5,
            op0=ALU.mult, op1=ALU.add), waits=[msq], inc="sv")
        p.op("vector", lambda v, s=s: v.tensor_tensor(
            out=s[:, 3:4], in0=s[:, 2:3], in1=s[:, 2:3], op=ALU.mult), inc="sv")
        p.op("vector", lambda v, s=s: v.tensor_tensor(
            out=s[:, 1:2], in0=s[:, 1:2], in1=s[:, 3:4], op=ALU.subtract), inc="sv")
        mv = p.op("vector", lambda v, s=s: v.reciprocal(
            out=s[:, 1:2], in_=s[:, 1:2]), inc="sv")
        mr = p.op("scalar", lambda sc, s=s: sc.activation(
            out=s[:, 3:4], in_=s[:, 1:2], func=AF.Sqrt), waits=[mv], inc="sa")
        mb = p.op("vector", lambda v, a=xbuf[par], o=xnb[par], s=s: v.tensor_scalar(
            out=o[:], in0=a[:], scalar1=s[:, 2:3], scalar2=s[:, 3:4],
            op0=ALU.subtract, op1=ALU.mult), waits=[mr], inc="sv")
        last_use = [mb]
        if fp32_cb is not None:
            m32 = p.op("vector", lambda v, a=xbuf[par], o=xn32[par], s=s: v.tensor_scalar(
                out=o[:], in0=a[:], scalar1=s[:, 2:3], scalar2=s[:, 3:4],
                op0=ALU.subtract, op1=ALU.mult), inc="sv")
            cbm = fp32_cb(tc, xn32[par], m32)
            last_use.extend(cbm)
        if xn_dram is not None:
            md = p.dma("gpsimd", xn_dram[tc * 128:(tc + 1) * 128, :], xnb[par][:],
                       waits=[mb], inc="n%d" % par)
            last_use.append(md)
            last.append(md)
        if xnT_sb is not None:
            for fc in range(8):
                tpar = fc & 1
                mt = p.op("tensor", lambda t, a=xnb[par], f=fc, pp=tp_ps[tpar]:
                          t.transpose(out=pp[:], in_=a[:, f * 128:(f + 1) * 128],
                                      identity=identb[:]),
                          waits=[mb] + bc.tp_war[tpar], inc="sp")
                me = p.op("vector", lambda v, d=xnT_sb[fc], tcc=tc, pp=tp_ps[tpar]:
                          v.tensor_copy(out=d[:, tcc * 128:(tcc + 1) * 128], in_=pp[:]),
                          waits=[mt], inc="sv")
                bc.tp_war[tpar] = [me]
                last_use.append(me)
                last.append(me)
        if xnT_dram is not None:
            evs = []
            for fc in range(8):
                tpar = fc & 1
                mt = p.op("tensor", lambda t, a=xnb[par], f=fc, pp=tp_ps[tpar]:
                          t.transpose(out=pp[:], in_=a[:, f * 128:(f + 1) * 128],
                                      identity=identb[:]),
                          waits=[mb] + stg_free[par] + bc.tp_war[tpar], inc="sp")
                me = p.op("vector", lambda v, d=stg[par], f=fc, pp=tp_ps[tpar]:
                          v.tensor_copy(out=d[:, f * 128:(f + 1) * 128], in_=pp[:]),
                          waits=[mt], inc="sv")
                bc.tp_war[tpar] = [me]
                evs.append(me)
            for fc in range(8):
                md = p.dma("gpsimd", xnT_dram[fc * 128:(fc + 1) * 128,
                                              tc * 128:(tc + 1) * 128],
                           stg[par][:, fc * 128:(fc + 1) * 128],
                           waits=evs, inc="s%d" % par)
            stg_free[par] = [md]
            last_use.append(md)
            last.append(md)
        free[par] = last_use
    return last



def emit_A(bc, x_src, xo_dram, xnT_dram, wait0=()):
    import os
    A_PAIRS = int(os.environ.get("A_PAIRS", "8"))
    A_SECT = os.environ.get("A_SECT", "all")
    p = bc.p
    SCALE = 0.125

    identb = bc.sb("A_identb", [128, 128], BF16)
    cosT = bc.sb("A_cosT", [128, T], F32)
    sinT = bc.sb("A_sinT", [128, T], F32)
    r2t = bc.sb("A_r2t", [128, 128], BF16)
    mcau = bc.sb("A_mcau", [128, 128], F32)
    mupp = bc.sb("A_mupp", [128, 128], F32)
    mglob = bc.sb("A_mglob", [128, 128], F32)
    wo0 = [bc.sb("A_wo0_%d" % i, [128, 1024], BF16) for i in range(8)]
    tp_ps = [bc.ps("A_tp%d" % i, [128, 128], BF16) for i in range(2)]

    p.phase_begin()
    p.dma("gpsimd", identb[:], bc.ein("ident"), inc="cgA")
    LVL = int(os.environ.get("A_CONSTLVL", "4"))
    if LVL >= 1:
        p.dma("sync", cosT[:], bc.ein("cosT"), inc="cstA")
        p.dma("sync", sinT[:], bc.ein("sinT"), inc="cstA")
    if LVL >= 2:
        p.dma("sync", mcau[:], bc.ein("mask_causal"), inc="cstA")
        p.dma("sync", mupp[:], bc.ein("mask_upper"), inc="cstA")
        p.dma("sync", mglob[:], bc.ein("mask_glob"), inc="cstA")
    if LVL >= 3:
        p.dma("gpsimd", r2t[:], bc.ein("R2T"), inc="cgA")
    if LVL >= 4:
        for i in range(8):
            p.dma("gpsimd", wo0[i][:], bc.ein("WoT0")[i * 128:(i + 1) * 128, :],
                  inc="cgA")
    cst = [p.mark("cstA"), p.mark("cgA")]
    m_r2t = m_cos = m_sin = m_mc = m_mu = cst[0]
    m_wo = [cst[1]] * 8
    cst_w = cst

    emit_ln(bc, "Aln", x_src, identb, tp_ps, xnT_dram=xnT_dram,
            wait0=list(wait0) + cst_w)
    ln_done = [p.mark("s0"), p.mark("s1"), p.mark("sv"), p.mark("sp")]

    wq = [bc.sb("A_wq%d" % i, [128, 128], BF16) for i in range(8)]
    wk = [bc.sb("A_wk%d" % i, [128, 128], BF16) for i in range(8)]
    wv = [bc.sb("A_wv%d" % i, [128, 128], BF16) for i in range(8)]
    xnt_s = [bc.sb("A_xnt%d" % i, [128, 512], BF16) for i in range(2)]
    qraw = bc.sb("A_qraw", [128, T], BF16)
    kraw = bc.sb("A_kraw", [128, T], BF16)
    qfin = bc.sb("A_qfin", [128, T], BF16)
    kf01 = [bc.sb("A_kf0", [128, T], BF16), bc.sb("A_kf1", [128, T], BF16)]
    vT = bc.sb("A_vT", [128, T], BF16)
    vtok = bc.sb("A_vtok", [128, T], BF16)
    t1b = bc.sb("A_t1", [128, 512], F32)
    t2b = bc.sb("A_t2", [128, 512], F32)
    attnT = [bc.sb("A_attnT%d" % i, [128, T], BF16) for i in range(8)]
    prow = bc.sb("A_prow", [128, 512], BF16)
    ptt = [bc.sb("A_ptt%d" % i, [128, 128], BF16) for i in range(3)]
    rmax = bc.sb("A_rmax", [128, 1], F32)
    nbias = bc.sb("A_nbias", [128, 1], F32)
    rsum = bc.sb("A_rsum", [128, 1], F32)

    qps = bc.ps("A_qps", [128, 512])
    kps = bc.ps("A_kps", [128, 512])
    vps = bc.ps("A_vps", [128, 512])
    rps = bc.ps("A_rps", [128, 512])
    sps = bc.ps("A_sps", [128, 512])
    aps = bc.ps("A_aps", [128, 128])

    if A_SECT == "ln":
        p.phase_end()
        return []
    mz0 = p.op("vector", lambda v: v.memset(kf01[0][64:128, :], 0.0), inc="sv")
    mz1 = p.op("vector", lambda v: v.memset(kf01[1][0:64, :], 0.0), inc="sv")

    wload_free = []
    xnt_free = [[], []]
    ev_free = {"q": [], "k": [], "v": []}
    rope_free = []
    vtok_free = ln_done
    s_free = []
    exp_free = []
    pt_free = [[], [], []]
    ae_free = []
    pair_done = []

    for pp in range(A_PAIRS):
        qc, kc, vc = pp * 128, 1024 + pp * 128, 2048 + pp * 128
        wl = []
        for ic in range(8):
            wl.append(p.dma("gpsimd", wq[ic][:],
                            bc.ein("WqkvT0")[ic * 128:(ic + 1) * 128, qc:qc + 128],
                            waits=wload_free, inc="wlA"))
            wl.append(p.dma("gpsimd", wk[ic][:],
                            bc.ein("WqkvT0")[ic * 128:(ic + 1) * 128, kc:kc + 128],
                            inc="wlA"))
            wl.append(p.dma("gpsimd", wv[ic][:],
                            bc.ein("WqkvT0")[ic * 128:(ic + 1) * 128, vc:vc + 128],
                            inc="wlA"))
        wload_free = []
        mm_marks = []
        for tcb in range(8):
            sl = slice(tcb * 512, (tcb + 1) * 512)
            mq = mk_ = mv_ = None
            for ic in range(8):
                par = ic & 1
                lx = p.dma("sync", xnt_s[par][:],
                           xnT_dram[ic * 128:(ic + 1) * 128, sl],
                           waits=xnt_free[par] + ln_done, inc="xt%d" % par)
                st_ = (ic == 0)
                en_ = (ic == 7)
                mq = p.op("tensor", lambda t, w=wq[ic], x=xnt_s[par], s=st_, e=en_:
                          t.matmul(out=qps[:], lhsT=w[:], rhs=x[:], start=s, stop=e),
                          waits=[lx, wl[-1]] + (ev_free["q"] if ic == 0 else []),
                          inc="sp")
                mk_ = p.op("tensor", lambda t, w=wk[ic], x=xnt_s[par], s=st_, e=en_:
                           t.matmul(out=kps[:], lhsT=w[:], rhs=x[:], start=s, stop=e),
                           waits=[wl[-1]] + (ev_free["k"] if ic == 0 else []),
                           inc="sp")
                mv_ = p.op("tensor", lambda t, w=wv[ic], x=xnt_s[par], s=st_, e=en_:
                           t.matmul(out=vps[:], lhsT=w[:], rhs=x[:], start=s, stop=e),
                           waits=[wl[-1]] + (ev_free["v"] if ic == 0 else []),
                           inc="sp")
                xnt_free[par] = [mv_]
            eq = p.op("vector", lambda v, s=sl: v.tensor_copy(out=qraw[:, s], in_=qps[:]),
                      waits=[mq] + rope_free, inc="sv")
            ek = p.op("scalar", lambda sc, s=sl: sc.copy(out=kraw[:, s], in_=kps[:]),
                      waits=[mk_] + rope_free, inc="sa")
            ev = p.op("vector", lambda v, s=sl: v.tensor_copy(out=vT[:, s], in_=vps[:]),
                      waits=[mv_] + vtok_free, inc="sv")
            ev_free = {"q": [eq], "k": [ek], "v": [ev]}
            mm_marks.append((eq, ek, ev))
        rp_marks = []
        rps_free = []
        for tcb in range(8):
            sl = slice(tcb * 512, (tcb + 1) * 512)
            eq, ek, ev = mm_marks[tcb]
            for which, (raw, mk2) in enumerate(((qraw, eq), (kraw, ek))):
                mr = p.op("tensor", lambda t, r=raw, s=sl: t.matmul(
                    out=rps[:], lhsT=r2t[:], rhs=r[:, s], start=True, stop=True),
                    waits=[mk2] + cst_w + rps_free, inc="sp")
                m1 = p.op("gpsimd", lambda g, r=raw, s=sl: g.tensor_tensor(
                    out=t1b[:], in0=r[:, s], in1=cosT[:, s], op=ALU.mult),
                    waits=[mk2] + cst_w + rp_marks[-1:], inc="gv")
                m2 = p.op("vector", lambda v, s=sl: v.tensor_tensor(
                    out=t2b[:], in0=rps[:], in1=sinT[:, s], op=ALU.mult),
                    waits=[mr] + cst_w, inc="sv")
                rps_free = [m2]
                if which == 0:
                    m3 = p.op("vector", lambda v, s=sl: v.tensor_tensor(
                        out=qfin[:, s], in0=t1b[:], in1=t2b[:], op=ALU.add),
                        waits=[m1, m2], inc="sv")
                    rp_marks.append(m3)
                else:
                    p.op("vector", lambda v, s=sl: v.tensor_tensor(
                        out=kf01[0][0:64, s], in0=t1b[0:64, :], in1=t2b[0:64, :],
                        op=ALU.add), waits=[m1, m2, mz0], inc="sv")
                    m3 = p.op("vector", lambda v, s=sl: v.tensor_tensor(
                        out=kf01[1][64:128, s], in0=t1b[64:128, :], in1=t2b[64:128, :],
                        op=ALU.add), waits=[mz1], inc="sv")
                    rp_marks.append(m3)
        rope_free = rp_marks[-2:]
        rope_all = rp_marks[-1]
        if A_SECT == "qkv":
            rope_free = []
            pair_done.append(mm_marks[-1][0])
            continue
        vt_last = None
        for kt in range(NTC):
            ev = mm_marks[kt // 4][2]
            tpar = kt & 1
            mt = p.op("tensor", lambda t, k=kt, pp_=tp_ps[tpar]: t.transpose(
                out=pp_[:], in_=vT[:, k * 128:(k + 1) * 128], identity=identb[:]),
                waits=[ev] + vtok_free + bc.tp_war[tpar], inc="sp")
            vt_last = p.op("vector", lambda v, k=kt, pp_=tp_ps[tpar]: v.tensor_copy(
                out=vtok[:, k * 128:(k + 1) * 128], in_=pp_[:]), waits=[mt], inc="sv")
            bc.tp_war[tpar] = [vt_last]
        v_all = vt_last
        if A_SECT in ("rope", "vt"):
            pair_done.append(vt_last)
            continue
        for qt in range(NTC):
            pieces = [(0, qt, mcau)]
            if qt >= 1:
                pieces.append((128, qt - 1, mupp if (qt % 2 == 0) else None))
            if qt >= 2:
                pieces.append((256, 0, mglob))
            width = 128 * len(pieces)
            for hh in range(2):
                kfh = kf01[hh]
                sm = []
                for (c0, kt, _msk) in pieces:
                    sm.append(p.op("tensor", lambda t, c=c0, k=kt, f=kfh, q=qt:
                              t.matmul(out=sps[:, c:c + 128],
                                       lhsT=qfin[:, q * 128:(q + 1) * 128],
                                       rhs=f[:, k * 128:(k + 1) * 128],
                                       start=True, stop=True),
                              waits=[rope_all] + s_free, inc="sp"))
                mk3 = sm[-1]
                if os.environ.get("A_DBG") and pp == 0 and qt == 0 and hh == 0:
                    dbgS = bc.nc.dram_tensor("dbgS", [128, 128], F32).ap()
                    dbgM = bc.nc.dram_tensor("dbgM", [128, 128], F32).ap()
                    dcp = p.op("vector", lambda v: v.tensor_copy(out=t1b[:, 0:128], in_=sps[:, 0:128]), waits=sm, inc="sv")
                    d1 = p.dma("gpsimd", dbgS, t1b[:, 0:128], waits=[dcp], inc="n1")
                    d2 = p.dma("gpsimd", dbgM, mcau[:], waits=cst_w, inc="n1")
                    sm = sm + [d1, d2, dcp]
                for i, (c0, kt, msk) in enumerate(pieces):
                    if msk is not None:
                        mk3 = p.op("vector", lambda v, c=c0, m=msk: v.tensor_tensor(
                            out=sps[:, c:c + 128], in0=sps[:, c:c + 128], in1=m[:],
                            op=ALU.add), waits=sm + cst_w, inc="sv")
                m1 = p.op("vector", lambda v, w=width: v.reduce_max(
                    out=rmax[:], in_=sps[:, 0:w], axis=AX.X), waits=[mk3] + sm, inc="sv")
                m2 = p.op("vector", lambda v: v.tensor_scalar(
                    out=nbias[:], in0=rmax[:], scalar1=-SCALE, scalar2=None,
                    op0=ALU.mult), inc="sv")
                m3 = p.op("scalar", lambda sc, w=width: sc.activation(
                    out=prow[:, 0:w], in_=sps[:, 0:w], func=AF.Exp, scale=SCALE,
                    bias=nbias[:], accum_out=rsum[:]), waits=[m2] + exp_free, inc="sa")
                m4 = p.op("vector", lambda v: v.reciprocal(out=rsum[:], in_=rsum[:]),
                          waits=[m3], inc="sv")
                m5 = p.op("vector", lambda v, w=width: v.tensor_scalar(
                    out=prow[:, 0:w], in0=prow[:, 0:w], scalar1=rsum[:], scalar2=None,
                    op0=ALU.mult), waits=[m4], inc="sv")
                s_free = [m5]
                exp_free = [m5]
                pev = []
                for i, (c0, kt, _msk) in enumerate(pieces):
                    tpar = i & 1
                    mt = p.op("tensor", lambda t, c=c0, pp_=tp_ps[tpar]: t.transpose(
                        out=pp_[:], in_=prow[:, c:c + 128], identity=identb[:]),
                        waits=[m5] + pt_free[i] + bc.tp_war[tpar], inc="sp")
                    me = p.op("vector", lambda v, i_=i, pp_=tp_ps[tpar]: v.tensor_copy(
                        out=ptt[i_][:], in_=pp_[:]), waits=[mt], inc="sv")
                    bc.tp_war[tpar] = [me]
                    pev.append(me)
                am = None
                for j, (c0, kt, _msk) in enumerate(pieces):
                    am = p.op("tensor", lambda t, k=kt, i_=j,
                              s=(j == 0), e=(j == len(pieces) - 1):
                              t.matmul(out=aps[:],
                                       lhsT=vtok[:, k * 128:(k + 1) * 128],
                                       rhs=ptt[i_][:], start=s, stop=e),
                              waits=pev + [v_all] + (ae_free if j == 0 else []),
                              inc="sp")
                for i in range(len(pieces)):
                    pt_free[i] = [am]
                ae = p.op("vector", lambda v, d=attnT[pp], q=qt, b=hh * 64: v.tensor_copy(
                    out=d[b:b + 64, q * 128:(q + 1) * 128], in_=aps[b:b + 64, :]),
                    waits=[am], inc="sv")
                ae_free = [ae]
                if os.environ.get("A_DBG") and pp == 0 and qt == 0 and hh == 0:
                    dbgP = bc.nc.dram_tensor("dbgP", [128, 128], F32).ap()
                    dbgR = bc.nc.dram_tensor("dbgR", [128, 2], F32).ap()
                    d3 = p.dma("gpsimd", dbgP, prow[:, 0:128], waits=[ae], inc="n1")
                    d4 = p.dma("gpsimd", dbgR[:, 0:1], rmax[:], inc="n1", allow_slow_non_contiguous=True)
                    d5 = p.dma("gpsimd", dbgR[:, 1:2], rsum[:], inc="n1", allow_slow_non_contiguous=True)
                    ae_free = ae_free + [d3, d4, d5]
                    s_free = s_free + [d3, d4, d5]
                    exp_free = exp_free + [d3, d4, d5]
        pair_done.append(ae_free[0])
        vtok_free = [ae_free[0]]

    if A_SECT != "all" or A_PAIRS < 8:
        p.phase_end()
        return []
    ost = [bc.sb("A_ost0", [128, 1024], F32)]
    opsum = [(qps, kps), (vps, rps)]
    ev_last = [[], []]
    acc_marks = []
    for qt in range(NTC):
        par = qt & 1
        ps0, ps1 = opsum[par]
        mm0 = mm1 = None
        for pc in range(8):
            st_ = (pc == 0)
            en_ = (pc == 7)
            w0 = [pair_done[pc]] if qt == 0 else []
            mm0 = p.op("tensor", lambda t, pcc=pc, q=qt, s=st_, e=en_, d=ps0: t.matmul(
                out=d[:], lhsT=attnT[pcc][:, q * 128:(q + 1) * 128],
                rhs=wo0[pcc][:, 0:512], start=s, stop=e),
                waits=w0 + cst_w + (ev_last[par] if pc == 0 else []), inc="sp")
            mm1 = p.op("tensor", lambda t, pcc=pc, q=qt, s=st_, e=en_, d=ps1: t.matmul(
                out=d[:], lhsT=attnT[pcc][:, q * 128:(q + 1) * 128],
                rhs=wo0[pcc][:, 512:1024], start=s, stop=e), inc="sp")
        war = acc_marks[-1:]
        e0 = p.op("vector", lambda v, d=ost[0], sp_=ps0: v.tensor_copy(
            out=d[:, 0:512], in_=sp_[:]), waits=[mm1] + war, inc="sv")
        e1 = p.op("scalar", lambda s, d=ost[0], sp_=ps1: s.copy(
            out=d[:, 512:1024], in_=sp_[:]), waits=[mm1] + war, inc="sa")
        ev_last[par] = [e0, e1]
        md = p.dma("gpsimd", xo_dram[qt * 128:(qt + 1) * 128, :], ost[0][:],
                   waits=list(wait0) + [e0, e1], inc="ac%d" % par, accum_op=ALU.add)
        acc_marks.append(md)
    p.phase_end()
    return [p.mark("ac0"), p.mark("ac1")]




def emit_B(bc, x_src, xo_dram, wait0=()):
    p = bc.p
    identb = bc.sb("B_identb", [128, 128], BF16)
    tp_ps = [bc.ps("B_tp%d" % i, [128, 128], BF16) for i in range(2)]
    xnT = [bc.sb("B_xnT%d" % i, [128, T], BF16) for i in range(8)]
    wg = [bc.sb("B_wg%d" % i, [128, 16], BF16) for i in range(8)]

    p.phase_begin()
    p.dma("gpsimd", identb[:], bc.ein("ident"), waits=list(wait0), inc="cgB")
    for i in range(8):
        p.dma("gpsimd", wg[i][:], bc.ein("WgT")[i * 128:(i + 1) * 128, :],
              waits=list(wait0), inc="cgB")
    cst = p.mark("cgB")

    emit_ln(bc, "Bln", x_src, identb, tp_ps, xnT_sb=xnT,
            wait0=list(wait0) + [cst])
    ln_done = [p.mark("sv"), p.mark("sp")]

    sc = bc.sb("B_sc", [128, 16], F32)
    ga = bc.sb("B_ga", [128, 4], F32)
    gb = bc.sb("B_gb", [128, 4], F32)
    gc = bc.sb("B_gc", [128, 4], F32)
    gd = bc.sb("B_gd", [128, 4], F32)
    ge = bc.sb("B_ge", [128, 4], F32)
    gf = bc.sb("B_gf", [128, 4], F32)
    top8 = bc.sb("B_top8", [128, 8], F32)
    msk = bc.sb("B_msk", [128, 16], F32)
    msk2 = bc.sb("B_msk2", [128, 16], F32)
    t16 = bc.sb("B_t16", [128, 16], F32)
    w1v = bc.sb("B_w1v", [128, 1], F32)
    comb = bc.sb("B_comb", [128, 16 * NTC], F32)
    lgps = bc.ps("B_lgps", [128, 16])

    TT = ALU
    gate_done = []
    lg_free = []
    for tc in range(NTC):
        tsl = slice(tc * 128, (tc + 1) * 128)
        mlg = None
        for hc in range(8):
            mlg = p.op("tensor", lambda t, h=hc, s=tsl, st=(hc == 0), en=(hc == 7):
                       t.matmul(out=lgps[:], lhsT=xnT[h][:, s], rhs=wg[h][:],
                                start=st, stop=en),
                       waits=ln_done + [cst] + (lg_free if hc == 0 else []), inc="sp")
        msig = p.op("scalar", lambda s: s.activation(out=sc[:], in_=lgps[:],
                    func=AF.Sigmoid), waits=[mlg] + gate_done[-1:], inc="sa")
        lg_free = [msig]
        scv = sc[:].rearrange("p (g e) -> p g e", e=4)
        ops = [
            lambda v: v.tensor_tensor(out=ga[:], in0=scv[:, :, 0:1], in1=scv[:, :, 1:2], op=TT.max),
            lambda v: v.tensor_tensor(out=gb[:], in0=scv[:, :, 0:1], in1=scv[:, :, 1:2], op=TT.min),
            lambda v: v.tensor_tensor(out=gc[:], in0=scv[:, :, 2:3], in1=scv[:, :, 3:4], op=TT.max),
            lambda v: v.tensor_tensor(out=gd[:], in0=scv[:, :, 2:3], in1=scv[:, :, 3:4], op=TT.min),
            lambda v: v.tensor_tensor(out=ge[:], in0=ga[:], in1=gc[:], op=TT.min),
            lambda v: v.tensor_tensor(out=gf[:], in0=gb[:], in1=gd[:], op=TT.max),
            lambda v: v.tensor_tensor(out=ga[:], in0=ga[:], in1=gc[:], op=TT.max),
            lambda v: v.tensor_tensor(out=gb[:], in0=ge[:], in1=gf[:], op=TT.max),
            lambda v: v.tensor_tensor(out=ga[:], in0=ga[:], in1=gb[:], op=TT.add),
            lambda v: v.reduce_max(out=w1v[:], in_=ga[:], axis=AX.X),
            lambda v: v.tensor_scalar(out=gc[:], in0=ga[:], scalar1=w1v[:], scalar2=None, op0=TT.is_equal),
            lambda v: v.memset(gd[:, 0:1], 0.0),
            lambda v: v.tensor_copy(out=gd[:, 1:4], in_=gc[:, 0:3]),
            lambda v: v.tensor_tensor(out=gd[:, 2:4], in0=gd[:, 2:4], in1=gd[:, 0:2], op=TT.add),
            lambda v: v.tensor_scalar(out=ge[:], in0=gd[:], scalar1=0.5, scalar2=None, op0=TT.is_lt),
            lambda v: v.tensor_tensor(out=gc[:], in0=gc[:], in1=ge[:], op=TT.mult),
            lambda v: v.tensor_scalar(out=ge[:], in0=gc[:], scalar1=NEG, scalar2=None, op0=TT.mult),
            lambda v: v.tensor_tensor(out=gd[:], in0=ga[:], in1=ge[:], op=TT.add),
            lambda v: v.reduce_max(out=w1v[:], in_=gd[:], axis=AX.X),
            lambda v: v.tensor_scalar(out=ge[:], in0=gd[:], scalar1=w1v[:], scalar2=None, op0=TT.is_equal),
            lambda v: v.memset(gf[:, 0:1], 0.0),
            lambda v: v.tensor_copy(out=gf[:, 1:4], in_=ge[:, 0:3]),
            lambda v: v.tensor_tensor(out=gf[:, 2:4], in0=gf[:, 2:4], in1=gf[:, 0:2], op=TT.add),
            lambda v: v.tensor_scalar(out=gf[:], in0=gf[:], scalar1=0.5, scalar2=None, op0=TT.is_lt),
            lambda v: v.tensor_tensor(out=ge[:], in0=ge[:], in1=gf[:], op=TT.mult),
            lambda v: v.tensor_tensor(out=ga[:], in0=gc[:], in1=ge[:], op=TT.add),
            lambda v: v.tensor_scalar(out=gb[:], in0=ga[:], scalar1=1.0, scalar2=-NEG, op0=TT.subtract, op1=TT.mult),
        ]
        m = None
        for i, f in enumerate(ops):
            m = p.op("vector", f, waits=[msig] if i == 0 else [], inc="sv")
        for g in range(4):
            m = p.op("vector", lambda v, g_=g: v.tensor_scalar(
                out=msk[:, 4 * g_:4 * g_ + 4], in0=sc[:, 4 * g_:4 * g_ + 4],
                scalar1=gb[:, g_:g_ + 1], scalar2=None, op0=TT.add), inc="sv")
        ops2 = [
            lambda v: v.max(out=top8[:], in_=msk[:]),
            lambda v: v.memset(top8[:, 4:8], NEG),
            lambda v: v.match_replace(out=msk2[:], in_to_replace=top8[:], in_values=msk[:], imm_value=NEG),
            lambda v: v.tensor_tensor(out=t16[:], in0=msk[:], in1=msk2[:], op=TT.is_gt),
            lambda v: v.reduce_sum(out=w1v[:], in_=top8[:, 0:4], axis=AX.X),
            lambda v: v.tensor_scalar(out=w1v[:], in0=w1v[:], scalar1=1e-6, scalar2=None, op0=TT.add),
            lambda v: v.reciprocal(out=w1v[:], in_=w1v[:]),
            lambda v: v.tensor_tensor(out=t16[:], in0=t16[:], in1=sc[:], op=TT.mult),
        ]
        for f in ops2:
            m = p.op("vector", f, inc="sv")
        m = p.op("vector", lambda v, c=tc: v.tensor_scalar(
            out=comb[:, c * 16:(c + 1) * 16], in0=t16[:], scalar1=w1v[:],
            scalar2=None, op0=TT.mult), inc="sv")
        gate_done.append(m)
    gate_all = gate_done[-1]

    w1t = [[bc.sb("B_w1t%d_%d" % (sl, i), [128, 1024], BF16) for i in range(8)] for sl in range(1)]
    w3t = [[bc.sb("B_w3t%d_%d" % (sl, i), [128, 1024], BF16) for i in range(8)] for sl in range(1)]
    w2t = [[bc.sb("B_w2t%d_%d" % (sl, i), [128, 1024], BF16) for i in range(8)] for sl in range(1)]
    gf32 = bc.sb("B_gf32", [128, 512], F32)
    uf32 = bc.sb("B_uf32", [128, 512], F32)
    guT = [bc.sb("B_guT%d" % i, [128, 512], BF16) for i in range(8)]
    yacc = [bc.sb("B_yacc%d" % i, [128, 1024], F32) for i in range(4)]
    etmp = bc.sb("B_etmp", [128, 1024], F32)
    gps = bc.ps("B_gps", [128, 512])
    ups = bc.ps("B_ups", [128, 512])
    eps0 = bc.ps("B_eps0", [128, 512])
    eps1 = bc.ps("B_eps1", [128, 512])

    groups = [("S",)] + [(2 * i, 2 * i + 1) for i in range(8)]
    slot_free = [[], []]
    NSLOT = 1
    gu_free = [[] for _ in range(8)]
    silu_free = []
    uev_free = []
    yacc_free = [[] for _ in range(4)]
    yacc_dma = [[] for _ in range(4)]
    etmp_free = []
    eps_free = [[], []]
    acc_n = 0
    for gi, grp in enumerate(groups):
        sl = gi % NSLOT
        wl = []
        if grp[0] == "S":
            for hc in range(8):
                wl.append(p.dma("gpsimd", w1t[sl][hc][:], bc.ein("W1sT")[hc * 128:(hc + 1) * 128, :], waits=slot_free[sl], inc="wlB"))
                wl.append(p.dma("gpsimd", w3t[sl][hc][:], bc.ein("W3sT")[hc * 128:(hc + 1) * 128, :], inc="wlB"))
                wl.append(p.dma("gpsimd", w2t[sl][hc][:], bc.ein("W2sT")[hc * 128:(hc + 1) * 128, :], inc="wlB"))
        else:
            for li, e in enumerate(grp):
                for hc in range(8):
                    wl.append(p.dma("gpsimd", w1t[sl][hc][:, li * 512:(li + 1) * 512], bc.ein("W1eT")[e, hc * 128:(hc + 1) * 128, :], waits=slot_free[sl], inc="wlB"))
                    wl.append(p.dma("gpsimd", w3t[sl][hc][:, li * 512:(li + 1) * 512], bc.ein("W3eT")[e, hc * 128:(hc + 1) * 128, :], inc="wlB"))
                for fc in range(4):
                    wl.append(p.dma("gpsimd", w2t[sl][li * 4 + fc][:], bc.ein("W2eT")[e, fc * 128:(fc + 1) * 128, :], inc="wlB"))
        wld = wl[-1]
        last_mm = []
        for tcb in range(8):
            tsl = slice(tcb * 512, (tcb + 1) * 512)
            for li in range(len(grp)):
                shared = grp[0] == "S"
                nfc = 8 if shared else 4
                fo = 0 if shared else li * 512
                w2b = 0 if shared else li * 4
                gu_marks = []
                for fc in range(nfc):
                    mg = mu = None
                    for hc in range(8):
                        st = (hc == 0)
                        en = (hc == 7)
                        mg = p.op("tensor", lambda t, h=hc, f=fc, o=fo, s=tsl, sl_=sl, st_=st, en_=en:
                                  t.matmul(out=gps[:], lhsT=w1t[sl_][h][:, o + f * 128:o + (f + 1) * 128],
                                           rhs=xnT[h][:, s], start=st_, stop=en_),
                                  waits=[wld, gate_all] + (silu_free if hc == 0 else []), inc="sp")
                        mu = p.op("tensor", lambda t, h=hc, f=fc, o=fo, s=tsl, sl_=sl, st_=st, en_=en:
                                  t.matmul(out=ups[:], lhsT=w3t[sl_][h][:, o + f * 128:o + (f + 1) * 128],
                                           rhs=xnT[h][:, s], start=st_, stop=en_),
                                  waits=(uev_free if hc == 0 else []), inc="sp")
                    ms = p.op("scalar", lambda s: s.activation(out=gf32[:], in_=gps[:],
                              func=AF.Silu), waits=[mg], inc="sa")
                    mu2 = p.op("vector", lambda v: v.tensor_copy(out=uf32[:], in_=ups[:]),
                               waits=[mu], inc="sv")
                    mgu = p.op("vector", lambda v, f=fc: v.tensor_tensor(
                        out=guT[f][:], in0=gf32[:], in1=uf32[:], op=TT.mult),
                        waits=[ms, mu2] + gu_free[fc], inc="sv")
                    silu_free = [mgu]
                    uev_free = [mgu]
                    gu_marks.append(mgu)
                    gu_free[fc] = []
                for ts in range(4):
                    me0 = me1 = None
                    for fc in range(nfc):
                        st = (fc == 0)
                        en = (fc == nfc - 1)
                        me0 = p.op("tensor", lambda t, f=fc, ts_=ts, b=w2b, sl_=sl, st_=st, en_=en:
                                   t.matmul(out=eps0[:], lhsT=guT[f][:, ts_ * 128:(ts_ + 1) * 128],
                                            rhs=w2t[sl_][b + f][:, 0:512], start=st_, stop=en_),
                                   waits=gu_marks + (eps_free[0] if fc == 0 else []), inc="sp")
                        me1 = p.op("tensor", lambda t, f=fc, ts_=ts, b=w2b, sl_=sl, st_=st, en_=en:
                                   t.matmul(out=eps1[:], lhsT=guT[f][:, ts_ * 128:(ts_ + 1) * 128],
                                            rhs=w2t[sl_][b + f][:, 512:1024], start=st_, stop=en_),
                                   waits=(eps_free[1] if fc == 0 else []), inc="sp")
                        gu_free[fc].append(me1)
                    tci = tcb * 4 + ts
                    if shared:
                        ev0 = p.op("vector", lambda v, y=yacc[ts]: v.tensor_copy(
                            out=y[:, 0:512], in_=eps0[:]), waits=[me1] + yacc_dma[ts], inc="sv")
                        ev1 = p.op("scalar", lambda s, y=yacc[ts]: s.copy(
                            out=y[:, 512:1024], in_=eps1[:]), waits=[me1] + yacc_dma[ts], inc="sa")
                    elif li == 0:
                        ev0 = p.op("scalar", lambda s, y=yacc[ts], t_=tci, e=grp[0]: s.activation(
                            out=y[:, 0:512], in_=eps0[:], func=AF.Copy,
                            scale=comb[:, t_ * 16 + e:t_ * 16 + e + 1]),
                            waits=[me1] + yacc_dma[ts], inc="sa")
                        ev1 = p.op("scalar", lambda s, y=yacc[ts], t_=tci, e=grp[0]: s.activation(
                            out=y[:, 512:1024], in_=eps1[:], func=AF.Copy,
                            scale=comb[:, t_ * 16 + e:t_ * 16 + e + 1]), inc="sa")
                    else:
                        et0 = p.op("scalar", lambda s, t_=tci, e=grp[1]: s.activation(
                            out=etmp[:, 0:512], in_=eps0[:], func=AF.Copy,
                            scale=comb[:, t_ * 16 + e:t_ * 16 + e + 1]),
                            waits=[me1] + etmp_free, inc="sa")
                        et1 = p.op("scalar", lambda s, t_=tci, e=grp[1]: s.activation(
                            out=etmp[:, 512:1024], in_=eps1[:], func=AF.Copy,
                            scale=comb[:, t_ * 16 + e:t_ * 16 + e + 1]), inc="sa")
                        ev0 = p.op("vector", lambda v, y=yacc[ts]: v.tensor_tensor(
                            out=y[:], in0=y[:], in1=etmp[:], op=TT.add),
                            waits=[et0, et1], inc="sv")
                        ev1 = ev0
                        etmp_free = [ev0]
                    eps_free = [[ev0], [ev1]]
                    yacc_free[ts] = [ev0, ev1]
                    if li == len(grp) - 1:
                        md = p.dma("gpsimd", xo_dram[tci * 128:(tci + 1) * 128, :],
                                   yacc[ts][:], waits=list(wait0) + [ev0, ev1],
                                   inc="acB%d" % ts, accum_op=ALU.add)
                        yacc_dma[ts] = [md]
                        acc_n += 1
            last_mm = [me1]
        slot_free[sl] = last_mm
    p.phase_end()
    return [p.mark("acB%d" % i) for i in range(4)]




def emit_C(bc, x_src, xo_dram, xnT_dram, wait0=()):
    p = bc.p
    SC = 1.0 / np.sqrt(512.0)
    identb = bc.sb("C_identb", [128, 128], BF16)
    mcau = bc.sb("C_mcau", [128, 128], F32)
    tp_ps = [bc.ps("C_tp%d" % i, [128, 128], BF16) for i in range(2)]

    p.phase_begin()
    p.dma("gpsimd", identb[:], bc.ein("ident"), waits=list(wait0), inc="cgC")
    p.dma("sync", mcau[:], bc.ein("mask_causal"), waits=list(wait0), inc="cstC")
    cst = [p.mark("cstC"), p.mark("cgC")]

    emit_ln(bc, "Cln", x_src, identb, tp_ps, xnT_dram=xnT_dram,
            wait0=list(wait0) + cst)
    ln_done = [p.mark("s0"), p.mark("s1"), p.mark("sv"), p.mark("sp")]

    wq = [bc.sb("C_wq%d" % i, [128, 512], BF16) for i in range(8)]
    wk = [bc.sb("C_wk%d" % i, [128, 512], BF16) for i in range(8)]
    wv = [bc.sb("C_wv%d" % i, [128, 512], BF16) for i in range(8)]
    woc = [bc.sb("C_wo%d" % i, [128, 1024], BF16) for i in range(4)]
    xnt_f = [bc.sb("C_xnt%d" % i, [128, 512], BF16) for i in range(8)]
    qT = [bc.sb("C_qT%d" % i, [128, T], BF16) for i in range(4)]
    kT = [bc.sb("C_kT%d" % i, [128, T], BF16) for i in range(4)]
    vC = bc.sb("C_v", [128, 32 * 512], BF16)
    peb = [bc.sb("C_peb%d" % i, [128, 512], F32) for i in range(2)]
    srow = bc.sb("C_srow", [128, T], F32)
    prow = bc.sb("C_prow", [128, T], BF16)
    ptC = bc.sb("C_ptC", [128, T], BF16)
    atm = bc.sb("C_atm", [128, 512], BF16)
    attnT = [bc.sb("C_attnT%d" % i, [128, 128], BF16) for i in range(4)]
    ost = [bc.sb("C_ost%d" % i, [128, 1024], F32) for i in range(2)]
    rmax = bc.sb("C_rmax", [128, 1], F32)
    nbias = bc.sb("C_nbias", [128, 1], F32)
    rsum = bc.sb("C_rsum", [128, 1], F32)

    cps = [bc.ps("C_cps%d" % i, [128, 512]) for i in range(2)]
    sps = bc.ps("C_sps", [128, 512])
    ops = [bc.ps("C_ops%d" % i, [128, 512]) for i in range(2)]

    wl_free = []
    xnt_free = [[] for _ in range(8)]
    pe_free = [[], []]
    qkv_free = {}
    cps_free = [[], []]
    s_free = []
    exp_free = []
    pt_free = []
    atm_free = []
    at_free = [[] for _ in range(4)]
    ost_free = [[], []]
    ops_free = []
    acc_last = [[], []]
    head_prev_total = []
    for h in range(8):
        base = h * 1536
        wl = []
        for ic in range(8):
            wl.append(p.dma("gpsimd", wq[ic][:],
                            bc.ein("WqkvT1")[ic * 128:(ic + 1) * 128, base:base + 512],
                            waits=wl_free, inc="wlC"))
            wl.append(p.dma("gpsimd", wk[ic][:],
                            bc.ein("WqkvT1")[ic * 128:(ic + 1) * 128, base + 512:base + 1024],
                            inc="wlC"))
            wl.append(p.dma("gpsimd", wv[ic][:],
                            bc.ein("WqkvT1")[ic * 128:(ic + 1) * 128, base + 1024:base + 1536],
                            inc="wlC"))
        for dc in range(4):
            wl.append(p.dma("gpsimd", woc[dc][:],
                            bc.ein("WoT1")[h * 512 + dc * 128:h * 512 + (dc + 1) * 128, :],
                            inc="wlC"))
        wld = wl[-1]
        for tcb in range(8):
            tsl = slice(tcb * 512, (tcb + 1) * 512)
            lds = []
            for ic in range(8):
                lds.append(p.dma("sync", xnt_f[ic][:],
                                 xnT_dram[ic * 128:(ic + 1) * 128, tsl],
                                 waits=xnt_free[ic] + ln_done, inc="xtC%d" % ic))
            for which, (wt, dst) in enumerate(((wq, qT), (wk, kT))):
                for oc in range(4):
                    par = oc & 1
                    pl = p.dma("sync", peb[par][:],
                               bc.ein("peT")[oc * 128:(oc + 1) * 128, tsl],
                               waits=pe_free[par], inc="peC%d" % par)
                    mm = None
                    for ic in range(8):
                        mm = p.op("tensor", lambda t, i=ic, o=oc, s=tsl, w=wt, par_=par,
                                  st=(ic == 0), en=(ic == 7):
                                  t.matmul(out=cps[par_][:], lhsT=w[i][:, o * 128:(o + 1) * 128],
                                           rhs=xnt_f[i][:], start=st, stop=en),
                                  waits=[wld, lds[ic]] + (cps_free[par] if ic == 0 else []),
                                  inc="sp")
                        xnt_free[ic] = [mm]
                    ev = p.op("vector", lambda v, d=dst[oc], s=tsl, par_=par: v.tensor_tensor(
                        out=d[:, s], in0=cps[par_][:], in1=peb[par_][:], op=ALU.add),
                        waits=[mm, pl], inc="sv")
                    pe_free[par] = [ev]
                    cps_free[par] = [ev]
                    qkv_free[(which, oc)] = ev
            for j in range(4):
                par = j & 1
                mm = None
                for ic in range(8):
                    mm = p.op("tensor", lambda t, i=ic, j_=j, st=(ic == 0), en=(ic == 7), par_=par:
                              t.matmul(out=cps[par_][:], lhsT=xnt_f[i][:, j_ * 128:(j_ + 1) * 128],
                                       rhs=wv[i][:], start=st, stop=en),
                              waits=(cps_free[par] if ic == 0 else []), inc="sp")
                    xnt_free[ic] = [mm]
                tj = tcb * 4 + j
                ev = p.op("vector", lambda v, t_=tj, par_=par: v.tensor_copy(
                    out=vC[:, t_ * 512:(t_ + 1) * 512], in_=cps[par_][:]),
                    waits=[mm], inc="sv")
                cps_free[par] = [ev]
                qkv_free[("v", tj)] = ev
        qkv_all = [qkv_free[("v", 31)], qkv_free[(0, 3)], qkv_free[(1, 3)]]
        for qt in range(NTC):
            kext = (qt + 1) * 128
            nkcb = (kext + 511) // 512
            sevs = []
            for kcb in range(nkcb):
                wdt = min(512, kext - kcb * 512)
                mm = None
                for dc in range(4):
                    mm = p.op("tensor", lambda t, d=dc, q=qt, k=kcb, w=wdt,
                              st=(dc == 0), en=(dc == 3):
                              t.matmul(out=sps[:, 0:w], lhsT=qT[d][:, q * 128:(q + 1) * 128],
                                       rhs=kT[d][:, k * 512:k * 512 + w], start=st, stop=en),
                              waits=qkv_all + s_free, inc="sp")
                ev = p.op("scalar", lambda s, k=kcb, w=wdt: s.copy(
                    out=srow[:, k * 512:k * 512 + w], in_=sps[:, 0:w]),
                    waits=[mm] + exp_free, inc="sa")
                s_free = [ev]
                sevs.append(ev)
            mmask = p.op("vector", lambda v, q=qt: v.tensor_tensor(
                out=srow[:, q * 128:(q + 1) * 128], in0=srow[:, q * 128:(q + 1) * 128],
                in1=mcau[:], op=ALU.add), waits=sevs + cst, inc="sv")
            m1 = p.op("vector", lambda v, w=kext: v.reduce_max(
                out=rmax[:], in_=srow[:, 0:w], axis=AX.X), waits=[mmask] + sevs, inc="sv")
            m2 = p.op("vector", lambda v: v.tensor_scalar(
                out=nbias[:], in0=rmax[:], scalar1=-SC, scalar2=None, op0=ALU.mult), inc="sv")
            m3 = p.op("scalar", lambda s, w=kext: s.activation(
                out=prow[:, 0:w], in_=srow[:, 0:w], func=AF.Exp, scale=SC,
                bias=nbias[:], accum_out=rsum[:]), waits=[m2] + pt_free, inc="sa")
            m4 = p.op("vector", lambda v: v.reciprocal(out=rsum[:], in_=rsum[:]),
                      waits=[m3], inc="sv")
            m5 = p.op("vector", lambda v, w=kext: v.tensor_scalar(
                out=prow[:, 0:w], in0=prow[:, 0:w], scalar1=rsum[:], scalar2=None,
                op0=ALU.mult), waits=[m4], inc="sv")
            exp_free = [m5]
            ptevs = []
            for kt in range(qt + 1):
                tpar = kt & 1
                mt = p.op("tensor", lambda t, k=kt, pp_=tp_ps[tpar]: t.transpose(
                    out=pp_[:], in_=prow[:, k * 128:(k + 1) * 128], identity=identb[:]),
                    waits=[m5] + bc.tp_war[tpar], inc="sp")
                me = p.op("vector", lambda v, k=kt, pp_=tp_ps[tpar]: v.tensor_copy(
                    out=ptC[:, k * 128:(k + 1) * 128], in_=pp_[:]), waits=[mt], inc="sv")
                bc.tp_war[tpar] = [me]
                ptevs.append(me)
            pt_free = [ptevs[-1]]
            mm = None
            for kt in range(qt + 1):
                mm = p.op("tensor", lambda t, k=kt, st=(kt == 0), en=(kt == qt):
                          t.matmul(out=cps[0][:], lhsT=ptC[:, k * 128:(k + 1) * 128],
                                   rhs=vC[:, k * 512:(k + 1) * 512], start=st, stop=en),
                          waits=ptevs + ([qkv_free[("v", 31)]] if True else []) +
                                (cps_free[0] if kt == 0 else []), inc="sp")
            mev = p.op("vector", lambda v: v.tensor_copy(out=atm[:], in_=cps[0][:]),
                       waits=[mm] + atm_free, inc="sv")
            cps_free[0] = [mev]
            aevs = []
            for dc in range(4):
                tpar = dc & 1
                mt = p.op("tensor", lambda t, d=dc, pp_=tp_ps[tpar]: t.transpose(
                    out=pp_[:], in_=atm[:, d * 128:(d + 1) * 128], identity=identb[:]),
                    waits=[mev] + bc.tp_war[tpar], inc="sp")
                me = p.op("vector", lambda v, d=dc, pp_=tp_ps[tpar]: v.tensor_copy(
                    out=attnT[d][:], in_=pp_[:]), waits=[mt] + at_free[dc], inc="sv")
                bc.tp_war[tpar] = [me]
                aevs.append(me)
            atm_free = [aevs[-1]]
            par = qt & 1
            mo = [None, None]
            for half in range(2):
                for dc in range(4):
                    mo[half] = p.op("tensor", lambda t, d=dc, hf=half, st=(dc == 0), en=(dc == 3):
                                    t.matmul(out=ops[hf][:], lhsT=attnT[d][:],
                                             rhs=woc[d][:, hf * 512:(hf + 1) * 512],
                                             start=st, stop=en),
                                    waits=aevs + [wld] + (ops_free if dc == 0 else []),
                                    inc="sp")
            for dc in range(4):
                at_free[dc] = [mo[1]]
            e0 = p.op("vector", lambda v, d=ost[par]: v.tensor_copy(
                out=d[:, 0:512], in_=ops[0][:]), waits=[mo[0], mo[1]] + acc_last[par], inc="sv")
            e1 = p.op("scalar", lambda s, d=ost[par]: s.copy(
                out=d[:, 512:1024], in_=ops[1][:]), waits=[mo[0], mo[1]] + acc_last[par], inc="sa")
            ost_free[par] = [e0, e1]
            ops_free = [e0, e1]
            md = p.dma("gpsimd", xo_dram[qt * 128:(qt + 1) * 128, :], ost[par][:],
                       waits=list(wait0) + [e0, e1] + head_prev_total,
                       inc="acC%d" % par, accum_op=ALU.add)
            acc_last[par] = [md]
        wl_free = [mo[1]]
        head_prev_total = [p.mark("acC0"), p.mark("acC1")]
    p.phase_end()
    return [p.mark("acC0"), p.mark("acC1")]




def emit_D(bc, x_src, xo_dram, wait0=()):
    p = bc.p
    nc = bc.nc
    K1 = 2048
    for _nm in ("saD", "scpD", "gaD0", "gaD1", "gaD2", "gaD3", "cc"):
        p.dma_like.add(_nm)

    xn_dram = nc.dram_tensor("D_xn", [T, DM], BF16).ap()
    sc_loc = nc.dram_tensor("D_scloc", [T, 8], F32).ap()
    sc_full = nc.dram_tensor("D_scfull", [8 * T, 8], F32).ap()
    pairbuf = nc.dram_tensor("D_pairs", [8 * C1, 2], F32).ap()

    identb = bc.sb("D_identb", [128, 128], BF16)
    identf = bc.sb("D_identf", [128, 128], F32)
    triu = bc.sb("D_triu", [128, 128], F32)
    wr = [bc.sb("D_wr%d" % i, [128, 8], F32) for i in range(8)]
    wrb = bc.sb("D_wrb", [1, 8], F32)
    onesf = bc.sb("D_onesf", [1, 128], F32)
    ones128 = bc.sb("D_ones128", [128, 1], F32)
    tp_ps = [bc.ps("D_tp%d" % i, [128, 128], BF16) for i in range(2)]
    tpf = bc.ps("D_tpf", [128, 128], F32)
    psmall = bc.ps("D_psmall", [128, 512])
    h1ps = bc.ps("D_h1ps", [128, 512])
    eps0 = bc.ps("D_eps0", [128, 512])
    eps1 = bc.ps("D_eps1", [128, 512])
    rps = psmall[:, 0:8]
    tbps = psmall[:, 8:16]
    rkp = psmall[:, 16:48]
    redp = psmall[0:1, 48:88]

    xf32 = [bc.sb("D_xf32_%d" % i, [128, 128], F32) for i in range(8)]
    lgs = bc.sb("D_lgs", [128, 8], F32)
    rmax = bc.sb("D_rmax", [128, 1], F32)
    nbias = bc.sb("D_nbias", [128, 1], F32)
    rsum = bc.sb("D_rsum", [128, 1], F32)
    scores_sb = bc.sb("D_scores", [128, 8 * NTC], F32)

    p.phase_begin()
    p.dma("gpsimd", identb[:], bc.ein("ident"), waits=list(wait0), inc="cgD")
    p.dma("sync", identf[:], bc.ein("ident"), waits=list(wait0), inc="cstD")
    p.dma("sync", triu[:], bc.ein("triuS"), inc="cstD")
    for i in range(8):
        p.dma("sync", wr[i][:], bc.ein("Wr")[i * 128:(i + 1) * 128, :], inc="cstD")
    p.dma("sync", wrb[:], bc.ein("Wrb"), inc="cstD")
    mo1 = p.op("vector", lambda v: v.memset(onesf[:], 1.0), waits=list(wait0), inc="sv")
    mo2 = p.op("vector", lambda v: v.memset(ones128[:], 1.0), inc="sv")
    bc.breg = bc.ctx.enter_context(nc.gpsimd.register("D_bnd"))
    mreg = p.op("gpsimd", lambda g: g.reg_mov(bc.breg, 8 * C1 - 1), inc="gv")
    cst = [p.mark("cstD"), p.mark("cgD"), mo1, mo2]

    rt_state = {"tpf_free": [], "xf_free": [[] for _ in range(8)], "rps_free": [],
                "lg_free": [], "sc_marks": []}

    def router_cb(tc, xn32, mark):
        mes = []
        for hc in range(8):
            mt = p.op("tensor", lambda t, h=hc, a=xn32: t.transpose(
                out=tpf[:], in_=a[:, h * 128:(h + 1) * 128], identity=identf[:]),
                waits=[mark] + cst + rt_state["tpf_free"], inc="sp")
            me = p.op("vector", lambda v, x=xf32[hc]: v.tensor_copy(
                out=x[:], in_=tpf[:]), waits=[mt] + rt_state["xf_free"][hc], inc="sv")
            rt_state["tpf_free"] = [me]
            mes.append(me)
        for hc in range(8):
            mmr = p.op("tensor", lambda t, h=hc, x=xf32[hc], st=(hc == 0):
                       t.matmul(out=rps, lhsT=x[:], rhs=wr[h][:], start=st, stop=False),
                       waits=mes + (rt_state["rps_free"] if hc == 0 else []), inc="sp")
            rt_state["xf_free"][hc] = [mmr]
        mmb = p.op("tensor", lambda t: t.matmul(out=rps, lhsT=onesf[:], rhs=wrb[:],
                   start=False, stop=True), inc="sp")
        ml = p.op("vector", lambda v: v.tensor_copy(out=lgs[:], in_=rps),
                  waits=[mmb] + rt_state["lg_free"], inc="sv")
        rt_state["rps_free"] = [ml]
        m1 = p.op("vector", lambda v: v.reduce_max(out=rmax[:], in_=lgs[:], axis=AX.X), inc="sv")
        m2 = p.op("vector", lambda v: v.tensor_scalar(out=nbias[:], in0=rmax[:],
                  scalar1=-1.0, scalar2=None, op0=ALU.mult), inc="sv")
        m3 = p.op("scalar", lambda s: s.activation(out=lgs[:], in_=lgs[:], func=AF.Exp,
                  bias=nbias[:], accum_out=rsum[:]), waits=[m2], inc="sa")
        m4 = p.op("vector", lambda v: v.reciprocal(out=rsum[:], in_=rsum[:]),
                  waits=[m3], inc="sv")
        m5 = p.op("vector", lambda v, c=tc: v.tensor_scalar(
            out=scores_sb[:, c * 8:(c + 1) * 8], in0=lgs[:], scalar1=rsum[:],
            scalar2=None, op0=ALU.mult), waits=[m4], inc="sv")
        rt_state["lg_free"] = [m5]
        md = p.dma("gpsimd", sc_loc[tc * 128:(tc + 1) * 128, :],
                   scores_sb[:, tc * 8:(tc + 1) * 8], waits=[m5], inc="scD%d" % (tc & 1))
        rt_state["sc_marks"].append(md)
        return [m5]

    emit_ln(bc, "Dln", x_src, identb, tp_ps, xn_dram=xn_dram,
            fp32_cb=router_cb, wait0=list(wait0) + cst)
    sc_done = [p.mark("scD0"), p.mark("scD1")]
    xn_done = [p.mark("n0"), p.mark("n1")]

    mcc = p.op("gpsimd", lambda g: g.collective_compute(
        "AllGather", ALU.bypass, replica_groups=[list(range(8))],
        ins=[sc_loc.opt()], outs=[sc_full.opt()]), waits=sc_done, inc="cc")

    sfb = bc.sb("D_sfb", [128, 2048], F32)
    msf = p.dma("sync", sfb[:], sc_full.rearrange("(p t) e -> p (t e)", p=128),
                waits=[mcc], inc="sfD")

    tlo = bc.sb("D_tlo", [1, 8], F32)
    thi = bc.sb("D_thi", [1, 8], F32)
    tmid = bc.sb("D_tmid", [1, 8], F32)
    ge8 = bc.sb("D_ge8", [1, 8], U32)
    gei = bc.sb("D_gei", [1, 8], U32)
    cnt8 = bc.sb("D_cnt8", [128, 8], F32)
    cmpb = bc.sb("D_cmpb", [128, 256], F32)
    sfv = sfb[:].rearrange("p (t e) -> p t e", e=8)
    mi1 = p.op("vector", lambda v: v.memset(tlo[:], 0.0), inc="sv")
    mi2 = p.op("vector", lambda v: v.memset(thi[:], 2.0), inc="sv")
    mprev = [mi1, mi2, msf]
    cnt_free = []
    for it in range(42):
        ma = p.op("vector", lambda v: v.tensor_tensor(out=tmid[:], in0=tlo[:],
                  in1=thi[:], op=ALU.add), waits=mprev, inc="sv")
        mb = p.op("vector", lambda v: v.tensor_scalar(out=tmid[:], in0=tmid[:],
                  scalar1=0.5, scalar2=None, op0=ALU.mult), inc="sv")
        mt = p.op("tensor", lambda t: t.matmul(out=tbps, lhsT=onesf[:], rhs=tmid[:],
                  start=True, stop=True), waits=[mb] + cst + cnt_free, inc="sp")
        mcs = []
        for e in range(8):
            mcs.append(p.op("vector", lambda v, e_=e: v.tensor_scalar(
                out=cmpb[:], in0=sfv[:, :, e_:e_ + 1], scalar1=tbps[:, e_:e_ + 1],
                scalar2=0.0, op0=ALU.is_ge, op1=ALU.add,
                accum_out=cnt8[:, e_:e_ + 1]), waits=[mt, msf], inc="sv"))
        mrd = p.op("tensor", lambda t: t.matmul(out=redp[:, 0:8], lhsT=ones128[:],
                   rhs=cnt8[:], start=True, stop=True), waits=mcs + cst, inc="sp")
        cnt_free = [mrd]
        mg = p.op("vector", lambda v: v.tensor_scalar(out=ge8[:], in0=redp[0:1, 0:8],
                  scalar1=float(K1) - 0.5, scalar2=None, op0=ALU.is_ge),
                  waits=[mrd], inc="sv")
        ml1 = p.op("vector", lambda v: v.select(out=tlo[:], mask=ge8[:],
                   on_true=tmid[:], on_false=tlo[:], add_drain=True), inc="sv")
        mgi = p.op("vector", lambda v: v.tensor_scalar(out=gei[:], in0=ge8[:],
                   scalar1=0.5, scalar2=None, op0=ALU.is_lt), inc="sv")
        ml2 = p.op("vector", lambda v: v.select(out=thi[:], mask=gei[:],
                   on_true=tmid[:], on_false=thi[:], add_drain=True), inc="sv")
        mprev = [ml1, ml2]
    thr_done = mprev + [p.op("vector", lambda v: v.engine_nop(), waits=mprev, inc="sv")]

    toktab = bc.sb("D_toktab", [128, NTC], F32)
    bigt = bc.sb("D_big", [128, NTC], F32)
    me_ = bc.sb("D_me", [128, NTC], F32)
    minv = bc.sb("D_minv", [128, NTC], U32)
    csl = bc.sb("D_csl", [1, NTC], F32)
    pfa = bc.sb("D_pfa", [1, NTC], F32)
    pfb = bc.sb("D_pfb", [1, NTC], F32)
    rkb = bc.sb("D_rkb", [128, NTC], F32)
    slotu = bc.sb("D_slotu", [128, NTC], U32)
    pft = bc.sb("D_pft", [128, 2 * NTC], F32)
    psrc = [bc.sb("D_psrc%d" % i, [128, 2], F32) for i in range(2)]
    mtk = p.op("gpsimd", lambda g: g.iota(toktab[:], pattern=[[128, NTC]], base=0,
               channel_multiplier=1, allow_small_or_imprecise_dtypes=True), inc="gv")
    mbg = p.op("vector", lambda v: v.memset(bigt[:], 1.0e9), inc="sv")
    pfv = pft[:].rearrange("p (s t) -> p s t", t=2)
    mp1 = p.op("vector", lambda v: v.memset(pfv[:, :, 0:1], float(T)), inc="sv")
    mp2 = p.op("vector", lambda v: v.memset(pfv[:, :, 1:2], 0.0), inc="sv")
    mpf = p.dma("gpsimd", pairbuf.rearrange("(p s) t -> p s t", p=128), pft[:],
                waits=[mp1, mp2], inc="pfD")
    scv = scores_sb[:].rearrange("p (t e) -> p t e", e=8)
    scat_marks = [mpf]
    sl_free = []
    for e in range(8):
        mtb = p.op("tensor", lambda t, e_=e: t.matmul(out=tbps[:, 0:1], lhsT=onesf[:],
                   rhs=tlo[0:1, e_:e_ + 1], start=True, stop=True),
                   waits=thr_done + cnt_free, inc="sp")
        mm1 = p.op("vector", lambda v, e_=e: v.tensor_scalar(
            out=me_[:], in0=scv[:, :, e_:e_ + 1], scalar1=tbps[:, 0:1],
            scalar2=None, op0=ALU.is_ge), waits=[mtb] + sl_free, inc="sv")
        mcs1 = p.op("tensor", lambda t: t.matmul(out=redp[:, 8:40], lhsT=ones128[:],
                    rhs=me_[:], start=True, stop=True), waits=[mm1], inc="sp")
        mcv = p.op("vector", lambda v: v.tensor_copy(out=csl[:], in_=redp[0:1, 8:40]),
                   waits=[mcs1], inc="sv")
        pft_ = [pfa, pfb]
        cur = 0
        mpa = p.op("vector", lambda v, d=pft_[0]: v.memset(d[:, 0:1], 0.0), inc="sv")
        mpb = p.op("vector", lambda v, d=pft_[0]: v.tensor_copy(
            out=d[:, 1:NTC], in_=csl[:, 0:NTC - 1]), inc="sv")
        mst = None
        for k in (1, 2, 4, 8, 16):
            src, dst = pft_[cur], pft_[1 - cur]
            mst = p.op("vector", lambda v, k_=k, s_=src, d_=dst: v.tensor_copy(
                out=d_[:, 0:k_], in_=s_[:, 0:k_]), inc="sv")
            mst = p.op("vector", lambda v, k_=k, s_=src, d_=dst: v.tensor_tensor(
                out=d_[:, k_:NTC], in0=s_[:, k_:NTC], in1=s_[:, 0:NTC - k_],
                op=ALU.add), inc="sv")
            cur = 1 - cur
        mr1 = p.op("tensor", lambda t: t.matmul(out=rkp, lhsT=triu[:], rhs=me_[:],
                   start=True, stop=False), waits=[mm1, mst] + cst, inc="sp")
        mr2 = p.op("tensor", lambda t, a=pft_[cur]: t.matmul(out=rkp, lhsT=onesf[:], rhs=a[:],
                   start=False, stop=True), inc="sp")
        ms1 = p.op("vector", lambda v, e_=e: v.tensor_scalar(
            out=rkb[:], in0=rkp, scalar1=float(e_ * C1), scalar2=None, op0=ALU.add),
            waits=[mr2], inc="sv")
        msi = p.op("vector", lambda v: v.tensor_scalar(out=minv[:], in0=me_[:],
                   scalar1=0.5, scalar2=None, op0=ALU.is_lt), inc="sv")
        ms2 = p.op("vector", lambda v: v.select(out=rkb[:], mask=minv[:],
                   on_true=bigt[:], on_false=rkb[:], add_drain=True), waits=[mbg], inc="sv")
        msl = p.op("vector", lambda v: v.tensor_scalar(out=slotu[:], in0=rkb[:],
                   scalar1=0.0, scalar2=None, op0=ALU.add), inc="sv")
        sl_free = []
        for tc in range(NTC):
            par = tc & 1
            mc1 = p.op("vector", lambda v, c=tc, d=psrc[par]: v.tensor_copy(
                out=d[:, 0:1], in_=toktab[:, c:c + 1]),
                waits=[msl, mtk] + (scat_marks[-2:-1] if len(scat_marks) >= 2 else []), inc="sv")
            mc2 = p.op("vector", lambda v, c=tc, e_=e, d=psrc[par]: v.tensor_copy(
                out=d[:, 1:2], in_=scv[:, c:c + 1, e_:e_ + 1]), inc="sv")
            mdd = p.op("gpsimd", lambda g, c=tc, d=psrc[par]: g.indirect_dma_start(
                out=pairbuf, out_offset=IDX(slotu[:, c:c + 1], 0), in_=d[:],
                in_offset=None, bounds_check=bc.breg, oob_is_err=False),
                waits=[mc1, mc2, mpf, mreg] + scat_marks[-1:],
                inc="scpD", n=16, drain=True)
            scat_marks.append(mdd)
        sl_free = scat_marks[-2:]
    scat_all = [p.mark("scpD")]

    btT = [bc.sb("D_btT%d" % i, [128, 512], BF16) for i in range(8)]
    gbuf = [bc.sb("D_gbuf%d" % i, [128, DM], BF16) for i in range(4)]
    prs = [bc.sb("D_prs%d" % i, [128, 2], F32) for i in range(4)]
    tki = [bc.sb("D_tki%d" % i, [128, 1], U32) for i in range(4)]
    h1T = [bc.sb("D_h1T%d" % i, [128, 512], BF16) for i in range(32)]
    w1s = [bc.sb("D_w1s%d" % i, [128, 1024], BF16) for i in range(2)]
    w2s = [bc.sb("D_w2s%d" % i, [128, 1024], BF16) for i in range(32)]
    b1t = bc.sb("D_b1t", [128, 32], F32)
    b2r = bc.sb("D_b2r", [1, 1024], F32)
    ctb = [bc.sb("D_ctb%d" % i, [128, DM], F32) for i in range(2)]

    mz = [p.op("vector", lambda v, g=gbuf[j]: v.memset(g[:], 0.0), inc="sv")
          for j in range(4)]
    W1 = bc.ein("W1x")
    W2 = bc.ein("W2x")
    gb_free = [[] for _ in range(4)]
    bt_free = [[] for _ in range(8)]
    h1_free = [[] for _ in range(32)]
    w1_free = [[], []]
    h1ps_free = []
    eps_fr = [[], []]
    w2_free = [[] for _ in range(32)]
    b_free = []
    ctb_free = [[], []]
    sa_last = []
    prs_free = [[] for _ in range(4)]
    for e in range(8):
        mb1 = p.dma("sync", b1t[:], bc.ein("B1x")[e], waits=b_free, inc="b1D")
        mb2 = p.dma("sync", b2r[:], bc.ein("B2x")[e], waits=b_free, inc="b2D")
        gevs = []
        for j in range(4):
            mpr = p.dma("sync", prs[j][:],
                        pairbuf[e * C1 + j * 128:e * C1 + (j + 1) * 128, :],
                        waits=scat_all + prs_free[j], inc="prD%d" % j)
            mti = p.op("vector", lambda v, j_=j: v.tensor_scalar(out=tki[j_][:],
                       in0=prs[j_][:, 0:1], scalar1=0.0, scalar2=None,
                       op0=ALU.add), waits=[mpr], inc="sv")
            assert T - 1 == 8 * C1 - 1
            mga = p.op("gpsimd", lambda g, j_=j: g.indirect_dma_start(
                out=gbuf[j_][:], out_offset=None, in_=xn_dram,
                in_offset=IDX(tki[j_][:, 0:1], 0), bounds_check=bc.breg,
                oob_is_err=False), waits=[mti, mz[j], mreg] + xn_done + gb_free[j],
                inc="gaD%d" % j, n=16, drain=True)
            for hc in range(8):
                tpar = hc & 1
                mt = p.op("tensor", lambda t, j_=j, h=hc, pp_=tp_ps[tpar]: t.transpose(
                    out=pp_[:], in_=gbuf[j_][:, h * 128:(h + 1) * 128],
                    identity=identb[:]), waits=[mga] + bc.tp_war[tpar], inc="sp")
                me2 = p.op("vector", lambda v, j_=j, h=hc, pp_=tp_ps[tpar]: v.tensor_copy(
                    out=btT[h][:, j_ * 128:(j_ + 1) * 128], in_=pp_[:]),
                    waits=[mt] + bt_free[hc], inc="sv")
                bc.tp_war[tpar] = [me2]
                gevs.append(me2)
        for fc in range(32):
            par = fc & 1
            dram_ap = bass.AP(W1.tensor, W1.offset + e * (1024 * 4096) + fc * 128,
                              [[4096, 128], [128 * 4096, 8], [1, 128]])
            mw = p.dma("gpsimd", w1s[par][:].rearrange("p (c f) -> p c f", c=8),
                       dram_ap, waits=w1_free[par], inc="w1D%d" % par)
            mm = None
            for hc in range(8):
                mm = p.op("tensor", lambda t, h=hc, par_=par, st=(hc == 0), en=(hc == 7):
                          t.matmul(out=h1ps[:], lhsT=w1s[par_][:, h * 128:(h + 1) * 128],
                                   rhs=btT[h][:], start=st, stop=en),
                          waits=[mw] + gevs + ((h1_free[fc] + h1ps_free) if hc == 0 else []),
                          inc="sp")
            w1_free[par] = [mm]
            for hc in range(8):
                bt_free[hc] = [mm]
            mh = p.op("scalar", lambda s, f=fc: s.activation(
                out=h1T[f][:], in_=h1ps[:], func=AF.Relu, bias=b1t[:, f:f + 1]),
                waits=[mm, mb1], inc="sa")
            h1_free[fc] = [mh]
            h1ps_free = [mh]
        w2l = []
        for fc in range(32):
            w2l.append(p.dma("gpsimd", w2s[fc][:],
                             W2[e * 4096 + fc * 128:e * 4096 + (fc + 1) * 128, :] if False else
                             bass.AP(W2.tensor, W2.offset + e * (4096 * 1024) + fc * 128 * 1024,
                                     [[1024, 128], [1, 1024]]),
                             waits=w2_free[fc], inc="w2D"))
        w2_all = p.mark("w2D")
        for ts in range(4):
            cpar = ts & 1
            me0 = me1 = None
            for half, epp in ((0, eps0), (1, eps1)):
                mm = None
                for fc in range(32):
                    mm = p.op("tensor", lambda t, f=fc, ts_=ts, hf=half, ep=epp,
                              st=(fc == 0):
                              t.matmul(out=ep[:], lhsT=h1T[f][:, ts_ * 128:(ts_ + 1) * 128],
                                       rhs=w2s[f][:, hf * 512:(hf + 1) * 512],
                                       start=st, stop=False),
                              waits=(h1_free[fc] or [None]) + [w2_all] +
                                    ((ctb_free[cpar] + eps_fr[half]) if fc == 0 else []),
                              inc="sp")
                mm = p.op("tensor", lambda t, hf=half, ep=epp: t.matmul(
                    out=ep[:], lhsT=onesf[:], rhs=b2r[0:1, hf * 512:(hf + 1) * 512],
                    start=False, stop=True), waits=[mb2], inc="sp")
                if half == 0:
                    me0 = mm
                else:
                    me1 = mm
            ct0 = p.op("scalar", lambda s, ts_=ts, c=ctb[cpar]: s.activation(
                out=c[:, 0:512], in_=eps0[:], func=AF.Copy,
                scale=prs[ts_][:, 1:2]), waits=[me0, me1], inc="sa")
            ct1 = p.op("scalar", lambda s, ts_=ts, c=ctb[cpar]: s.activation(
                out=c[:, 512:1024], in_=eps1[:], func=AF.Copy,
                scale=prs[ts_][:, 1:2]), inc="sa")
            eps_fr = [[ct0], [ct1]]

            msa = p.op("gpsimd", lambda g, ts_=ts, c=ctb[cpar]: g.indirect_dma_start(
                out=xo_dram, out_offset=IDX(tki[ts_][:, 0:1], 0), in_=c[:],
                in_offset=None, compute_op=ALU.add),
                waits=list(wait0) + [ct0, ct1] + sa_last, inc="saD", n=16, drain=True)
            sa_last = [msa]
            ctb_free[cpar] = [msa]
            gb_free[ts] = [msa]
            prs_free[ts] = [msa]
        b_free = [mh, ct1]
        for fc in range(32):
            w2_free[fc] = [me1]
    p.phase_end()
    return [p.mark("saD")]



def build_program(phases):
    nc = bass.Bass("TRN2", target_bir_lowering=False)
    with ExitStack() as ctx:
        bc = BC(nc, ctx)
        p = bc.p
        x_in = nc.dram_tensor("x_in", [T, DM], F32, kind="ExternalInput").ap()
        out = nc.dram_tensor("out", [T, DM], F32, kind="ExternalOutput").ap()
        x_dram = nc.dram_tensor("x_work", [T + 1, DM], F32).ap()
        xnT_dram = nc.dram_tensor("xnT", [DM, T], BF16).ap()

        p.phase_begin()
        m0 = p.dma("sync", x_dram[0:T, :], x_in, inc="dma")
        p.phase_end()

        prev = [m0]
        for ph in phases:
            bc.scope = ExitStack()
            if ph == "A":
                prev = emit_A(bc, x_dram[0:T, :], x_dram, xnT_dram, wait0=prev)
            elif ph == "B":
                prev = emit_B(bc, x_dram[0:T, :], x_dram, wait0=prev)
            elif ph == "C":
                prev = emit_C(bc, x_dram[0:T, :], x_dram, xnT_dram, wait0=prev)
            elif ph == "D":
                prev = emit_D(bc, x_dram[0:T, :], x_dram, wait0=prev)
            else:
                raise ValueError(ph)
            bc.scope.close()
            bc.scope = None

        p.phase_begin()
        waits = [p.mark(sm) for sm in list(p.cnt)]
        fin = p.dma("sync", out, x_dram[0:T, :], waits=waits, inc="dma")
        p.op("sync", lambda e: e.nop(), waits=[fin])
        p.phase_end()
    return bc


_CACHE = {}


def _get_program(phases):
    key = tuple(phases)
    if key not in _CACHE:
        _CACHE[key] = build_program(phases)
    return _CACHE[key]


def run_phases(phases, x_full, host_w, host_c):
    bc = _get_program(phases)
    tables = dict(host_w)
    tables.update(host_c)
    in_maps = []
    for c in range(8):
        m = {"x_in": np.ascontiguousarray(x_full[c])}
        for name in bc.ext:
            m[name] = tables[name]
        in_maps.append(m)
    res = run_bass_kernel_spmd(bc.nc, in_maps, core_ids=list(range(8)))
    return np.stack([r["out"] for r in res.results], 0)


def kernel(**inp):
    host_w = host_weights(inp)
    host_c = host_consts()
    x = np.ascontiguousarray(inp["L0_attn_x"].astype(np.float32, copy=False))
    import os
    groups = os.environ.get("K_GROUPS", "ABCD")
    for group in groups.split(","):
        x = run_phases(list(group), x, host_w, host_c)
    return x


# version 03
# speedup vs baseline: 1.3573x; 1.3573x over previous
import numpy as np
from contextlib import ExitStack

import concourse.bass as bass
import concourse.mybir as mybir
from concourse.bass_utils import run_bass_kernel_spmd

F32 = mybir.dt.float32
BF16 = mybir.dt.bfloat16
U32 = mybir.dt.uint32
AF = mybir.ActivationFunctionType
ALU = mybir.AluOpType
AX = mybir.AxisListType
IDX = bass.IndirectOffsetOnAxis

T = 4096
NTC = T // 128
DM = 1024
NEG = -1.0e30
C1 = 512
C0 = 1408

ENGINES = ("sync", "gpsimd", "tensor", "vector", "scalar")


class Prog:

    def __init__(self, nc, ctx):
        self.nc = nc
        self.ctx = ctx
        self.sems = {}
        self.cnt = {}
        self.dma_like = set()
        self.q = None
        self.hwm = None

    def semh(self, name):
        if name not in self.sems:
            self.sems[name] = self.ctx.enter_context(self.nc.semaphore(name))
            self.cnt[name] = 0
        return self.sems[name]

    def phase_begin(self):
        self.q = {e: [] for e in ENGINES}
        self.hwm = {e: {} for e in ENGINES}

    def op(self, eng, fn, waits=(), inc=None, n=1, drain=True):
        wmax = {}
        for item in waits:
            if item is None:
                continue
            s, v = item[0], item[1]
            if s not in self.dma_like and len(item) > 2 and item[2] == eng:
                continue
            if v > wmax.get(s, -1):
                wmax[s] = v
        w = []
        for s, v in wmax.items():
            if self.hwm[eng].get(s, 0) >= v:
                continue
            self.hwm[eng][s] = v
            w.append((s, v))
        mark = None
        if inc is not None:
            self.semh(inc)
            self.cnt[inc] += n
            mark = (inc, self.cnt[inc], eng)
        do_drain = drain and eng in ("vector", "scalar", "gpsimd")
        self.q[eng].append((w, fn, inc, n, do_drain))
        return mark

    def dma(self, eng, out, in_, waits=(), inc="dma", **kw):
        self.dma_like.add(inc)
        return self.op(eng, lambda e, o=out, i=in_, k=dict(kw): e.dma_start(out=o, in_=i, **k),
                       waits=waits, inc=inc, n=16, drain=False)

    def mark(self, inc):
        self.semh(inc)
        return (inc, self.cnt[inc], None)

    def phase_end(self):
        nc, q, sems = self.nc, self.q, self.sems
        with nc.Block() as block:
            for eng in ENGINES:
                ops = q[eng]
                if not ops:
                    continue

                def body(ectx, ops=ops):
                    for (w, fn, inc, n, drain) in ops:
                        for (s, v) in w:
                            ectx.wait_ge(sems[s], v)
                        instr = fn(ectx)
                        if inc is not None:
                            instr.then_inc(sems[inc], n)
                        if drain:
                            ectx.drain()

                getattr(block, eng)(body)
        self.q = None



def host_consts():
    c = {}
    HEAD_DIM = 64
    inv_freq = 1.0 / (10000.0 ** (np.arange(0, HEAD_DIM, 2, dtype=np.float64) / HEAD_DIM))
    pos = np.arange(T, dtype=np.float64)
    freqs = np.outer(pos, inv_freq)
    emb = np.concatenate([freqs, freqs], -1)
    cosT = np.cos(emb).T.astype(np.float32)
    sinT = np.sin(emb).T.astype(np.float32)
    c["cosT"] = np.ascontiguousarray(np.concatenate([cosT, cosT], 0))
    c["sinT"] = np.ascontiguousarray(np.concatenate([sinT, sinT], 0))
    R = np.zeros((64, 64), np.float32)
    for i in range(32):
        R[2 * i, 2 * i + 1] = -1.0
        R[2 * i + 1, 2 * i] = 1.0
    R2T = np.zeros((128, 128), np.float32)
    R2T[0:64, 0:64] = R.T
    R2T[64:128, 64:128] = R.T
    c["R2T"] = R2T
    D = 512
    div = np.exp(np.arange(0, D, 2, dtype=np.float64) * (-np.log(10000.0) / D))
    ang = pos[:, None] * div[None, :]
    pe = np.stack([np.sin(ang), np.cos(ang)], axis=-1).reshape(T, D)
    c["peT"] = np.ascontiguousarray(pe.T.astype(np.float32))
    ii = np.arange(128)[:, None]
    jj = np.arange(128)[None, :]
    c["mask_causal"] = np.where(jj <= ii, 0.0, NEG).astype(np.float32)
    c["mask_upper"] = np.where(ii <= jj, 0.0, NEG).astype(np.float32)
    c["mask_glob"] = np.ascontiguousarray(
        np.broadcast_to(np.where(jj < 4, 0.0, NEG), (128, 128)).astype(np.float32))
    c["triuS"] = np.triu(np.ones((128, 128), np.float32), 1)
    c["ident"] = np.eye(128, dtype=np.float32)
    return c


def host_weights(inp):
    asc = np.ascontiguousarray
    w = {}
    w["WqkvT0"] = asc(inp["L0_attn_qkv_w"].T)
    w["WoT0"] = asc(inp["L0_attn_out_w"].T)
    w["WgT"] = asc(inp["L0_ffn_w_gate"].T)
    w["W1eT"] = asc(inp["L0_ffn_w1_e"].transpose(0, 2, 1))
    w["W3eT"] = asc(inp["L0_ffn_w3_e"].transpose(0, 2, 1))
    w["W2eT"] = asc(inp["L0_ffn_w2_e"].transpose(0, 2, 1))
    w["W1sT"] = asc(inp["L0_ffn_w1_s"].T)
    w["W3sT"] = asc(inp["L0_ffn_w3_s"].T)
    w["W2sT"] = asc(inp["L0_ffn_w2_s"].T)
    w["WqkvT1"] = asc(inp["L1_attn_Wqkv"].T)
    w["WoT1"] = asc(inp["L1_attn_Wo"].T)
    w["Wr"] = asc(inp["L1_ffn_router_w"])
    w["Wrb"] = asc(inp["L1_ffn_router_b"].reshape(1, 8))
    w["W1x"] = asc(inp["L1_ffn_w1"])
    w["B1x"] = asc(inp["L1_ffn_b1"].reshape(8, 32, 128).transpose(0, 2, 1))
    w["W2x"] = asc(inp["L1_ffn_w2"])
    w["B2x"] = asc(inp["L1_ffn_b2"].reshape(8, 1, 1024))
    return w


WSHAPES = {
    "WqkvT0": [1024, 3072], "WoT0": [1024, 1024], "WgT": [1024, 16],
    "W1eT": [16, 1024, 512], "W3eT": [16, 1024, 512], "W2eT": [16, 512, 1024],
    "W1sT": [1024, 1024], "W3sT": [1024, 1024], "W2sT": [1024, 1024],
    "WqkvT1": [1024, 12288], "WoT1": [4096, 1024],
    "Wr": [1024, 8], "Wrb": [1, 8], "W1x": [8, 1024, 4096], "B1x": [8, 128, 32],
    "W2x": [8, 4096, 1024], "B2x": [8, 1, 1024],
    "cosT": [128, T], "sinT": [128, T], "R2T": [128, 128], "peT": [512, T],
    "mask_causal": [128, 128], "mask_upper": [128, 128], "mask_glob": [128, 128],
    "triuS": [128, 128],
    "ident": [128, 128],
}


class BC:

    def __init__(self, nc, ctx):
        self.nc = nc
        self.ctx = ctx
        self.ext = {}
        self.p = Prog(nc, ctx)
        self.tp_war = [[], []]

    def ein(self, name):
        if name not in self.ext:
            self.ext[name] = self.nc.dram_tensor(name, WSHAPES[name], F32,
                                                 kind="ExternalInput").ap()
        return self.ext[name]

    def sb(self, name, shape, dt):
        scope = getattr(self, "scope", None) or self.ctx
        return scope.enter_context(self.nc.sbuf_tensor(name, shape, dt))

    def ps(self, name, shape, dt=F32):
        scope = getattr(self, "scope", None) or self.ctx
        return scope.enter_context(self.nc.psum_tensor(name, shape, dt))



def emit_ln(bc, pref, x_src, identb, tp_ps, xnT_dram=None, xn_dram=None,
            xnT_sb=None, fp32_cb=None, wait0=()):
    p = bc.p
    xbuf = [bc.sb(pref + "xb%d" % i, [128, DM], F32) for i in range(2)]
    xnb = [bc.sb(pref + "xnb%d" % i, [128, DM], BF16) for i in range(2)]
    xn32 = [bc.sb(pref + "xn32%d" % i, [128, DM], F32) for i in range(2)] \
        if fp32_cb is not None else None
    sq = bc.sb(pref + "sq", [128, DM], BF16)
    st = [bc.sb(pref + "st%d" % i, [128, 4], F32) for i in range(2)]
    stg = [bc.sb(pref + "stg%d" % i, [128, DM], BF16) for i in range(2)] \
        if xnT_dram is not None else None

    free = [list(wait0), list(wait0)]
    stg_free = [[], []]
    last = []
    for tc in range(NTC):
        par = tc & 1
        s = st[par]
        ld = p.dma("sync", xbuf[par][:], x_src[tc * 128:(tc + 1) * 128, :],
                   waits=free[par], inc="x%d" % par)
        p.op("vector", lambda v, a=xbuf[par], s=s: v.reduce_sum(
            out=s[:, 0:1], in_=a[:], axis=AX.X), waits=[ld], inc="sv")
        msq = p.op("scalar", lambda sc, a=xbuf[par], s=s: sc.activation(
            out=sq[:], in_=a[:], func=AF.Square, accum_out=s[:, 1:2]),
            waits=[ld], inc="sa")
        p.op("vector", lambda v, s=s: v.tensor_scalar(
            out=s[:, 2:3], in0=s[:, 0:1], scalar1=1.0 / DM, scalar2=None,
            op0=ALU.mult), inc="sv")
        p.op("vector", lambda v, s=s: v.tensor_scalar(
            out=s[:, 1:2], in0=s[:, 1:2], scalar1=1.0 / DM, scalar2=1e-5,
            op0=ALU.mult, op1=ALU.add), waits=[msq], inc="sv")
        p.op("vector", lambda v, s=s: v.tensor_tensor(
            out=s[:, 3:4], in0=s[:, 2:3], in1=s[:, 2:3], op=ALU.mult), inc="sv")
        p.op("vector", lambda v, s=s: v.tensor_tensor(
            out=s[:, 1:2], in0=s[:, 1:2], in1=s[:, 3:4], op=ALU.subtract), inc="sv")
        mv = p.op("vector", lambda v, s=s: v.reciprocal(
            out=s[:, 1:2], in_=s[:, 1:2]), inc="sv")
        mr = p.op("scalar", lambda sc, s=s: sc.activation(
            out=s[:, 3:4], in_=s[:, 1:2], func=AF.Sqrt), waits=[mv], inc="sa")
        mb = p.op("vector", lambda v, a=xbuf[par], o=xnb[par], s=s: v.tensor_scalar(
            out=o[:], in0=a[:], scalar1=s[:, 2:3], scalar2=s[:, 3:4],
            op0=ALU.subtract, op1=ALU.mult), waits=[mr], inc="sv")
        last_use = [mb]
        if fp32_cb is not None:
            m32 = p.op("vector", lambda v, a=xbuf[par], o=xn32[par], s=s: v.tensor_scalar(
                out=o[:], in0=a[:], scalar1=s[:, 2:3], scalar2=s[:, 3:4],
                op0=ALU.subtract, op1=ALU.mult), inc="sv")
            cbm = fp32_cb(tc, xn32[par], m32)
            last_use.extend(cbm)
        if xn_dram is not None:
            md = p.dma("gpsimd", xn_dram[tc * 128:(tc + 1) * 128, :], xnb[par][:],
                       waits=[mb], inc="n%d" % par)
            last_use.append(md)
            last.append(md)
        if xnT_sb is not None:
            tpar = tc & 1
            mts = []
            for fc in range(8):
                mts.append(p.op("tensor", lambda t, a=xnb[par], f=fc, pp=tp_ps[tpar]:
                           t.transpose(out=pp[:, f * 128:(f + 1) * 128],
                                       in_=a[:, f * 128:(f + 1) * 128],
                                       identity=identb[:]),
                           waits=([mb] + bc.tp_war[tpar]) if fc == 0 else [mb],
                           inc="sp"))
            me = None
            for fc in range(8):
                me = p.op("vector", lambda v, d=xnT_sb[fc], tcc=tc, f=fc, pp=tp_ps[tpar]:
                          v.tensor_copy(out=d[:, tcc * 128:(tcc + 1) * 128],
                                        in_=pp[:, f * 128:(f + 1) * 128]),
                          waits=[mts[-1]], inc="sv")
            bc.tp_war[tpar] = [me]
            last_use.append(me)
            last.append(me)
        if xnT_dram is not None:
            tpar = tc & 1
            mts = []
            for fc in range(8):
                mts.append(p.op("tensor", lambda t, a=xnb[par], f=fc, pp=tp_ps[tpar]:
                           t.transpose(out=pp[:, f * 128:(f + 1) * 128],
                                       in_=a[:, f * 128:(f + 1) * 128],
                                       identity=identb[:]),
                           waits=([mb] + stg_free[par] + bc.tp_war[tpar]) if fc == 0 else [mb],
                           inc="sp"))
            me = p.op("vector", lambda v, d=stg[par], pp=tp_ps[tpar]: v.tensor_copy(
                out=d[:], in_=pp[:]), waits=[mts[-1]], inc="sv")
            bc.tp_war[tpar] = [me]
            dst = xnT_dram.rearrange("(c p) t -> p c t", p=128)[
                :, :, tc * 128:(tc + 1) * 128]
            md = p.dma("gpsimd", dst, stg[par][:].rearrange("p (c f) -> p c f", c=8),
                       waits=[me], inc="s%d" % par)
            stg_free[par] = [md]
            last_use.append(md)
            last.append(md)
        free[par] = last_use
    return last



def emit_A(bc, x_src, xo_dram, xnT_dram, wait0=()):
    import os
    A_PAIRS = int(os.environ.get("A_PAIRS", "8"))
    A_SECT = os.environ.get("A_SECT", "all")
    p = bc.p
    SCALE = 0.125

    identb = bc.sb("A_identb", [128, 128], BF16)
    cosT = bc.sb("A_cosT", [128, T], F32)
    sinT = bc.sb("A_sinT", [128, T], F32)
    r2t = bc.sb("A_r2t", [128, 128], BF16)
    mcau = bc.sb("A_mcau", [128, 128], F32)
    mupp = bc.sb("A_mupp", [128, 128], F32)
    mglob = bc.sb("A_mglob", [128, 128], F32)
    wo0 = [bc.sb("A_wo0_%d" % i, [128, 1024], BF16) for i in range(8)]
    tp_ps = [bc.ps("A_tp%d" % i, [128, 1024], BF16) for i in range(2)]

    p.phase_begin()
    p.dma("gpsimd", identb[:], bc.ein("ident"), inc="cgA")
    LVL = int(os.environ.get("A_CONSTLVL", "4"))
    if LVL >= 1:
        p.dma("sync", cosT[:], bc.ein("cosT"), inc="cstA")
        p.dma("sync", sinT[:], bc.ein("sinT"), inc="cstA")
    if LVL >= 2:
        p.dma("sync", mcau[:], bc.ein("mask_causal"), inc="cstA")
        p.dma("sync", mupp[:], bc.ein("mask_upper"), inc="cstA")
        p.dma("sync", mglob[:], bc.ein("mask_glob"), inc="cstA")
    if LVL >= 3:
        p.dma("gpsimd", r2t[:], bc.ein("R2T"), inc="cgA")
    if LVL >= 4:
        for i in range(8):
            p.dma("gpsimd", wo0[i][:], bc.ein("WoT0")[i * 128:(i + 1) * 128, :],
                  inc="cgA")
    cst = [p.mark("cstA"), p.mark("cgA")]
    m_r2t = m_cos = m_sin = m_mc = m_mu = cst[0]
    m_wo = [cst[1]] * 8
    cst_w = cst

    emit_ln(bc, "Aln", x_src, identb, tp_ps, xnT_dram=xnT_dram,
            wait0=list(wait0) + cst_w)
    ln_done = [p.mark("s0"), p.mark("s1"), p.mark("sv"), p.mark("sp")]

    wq = [bc.sb("A_wq%d" % i, [128, 128], BF16) for i in range(8)]
    wk = [bc.sb("A_wk%d" % i, [128, 128], BF16) for i in range(8)]
    wv = [bc.sb("A_wv%d" % i, [128, 128], BF16) for i in range(8)]
    xnt_s = [bc.sb("A_xnt%d" % i, [128, 512], BF16) for i in range(2)]
    qraw = bc.sb("A_qraw", [128, T], BF16)
    kraw = bc.sb("A_kraw", [128, T], BF16)
    qfin = bc.sb("A_qfin", [128, T], BF16)
    kf01 = [bc.sb("A_kf0", [128, T], BF16), bc.sb("A_kf1", [128, T], BF16)]
    vT = bc.sb("A_vT", [128, T], BF16)
    vtok = bc.sb("A_vtok", [128, T], BF16)
    t1b = bc.sb("A_t1", [128, 512], F32)
    t2b = bc.sb("A_t2", [128, 512], F32)
    attnT = [bc.sb("A_attnT%d" % i, [128, T], BF16) for i in range(8)]
    prow = bc.sb("A_prow", [128, 512], BF16)
    ptt3 = bc.sb("A_ptt", [128, 384], BF16)
    rmax = bc.sb("A_rmax", [128, 1], F32)
    nbias = bc.sb("A_nbias", [128, 1], F32)
    rsum = bc.sb("A_rsum", [128, 1], F32)

    qps = bc.ps("A_qps", [128, 512])
    kps = bc.ps("A_kps", [128, 512])
    vps = bc.ps("A_vps", [128, 512])
    rps = bc.ps("A_rps", [128, 512])
    sps = bc.ps("A_sps", [128, 512])
    aps = bc.ps("A_aps", [128, 128])

    if A_SECT == "ln":
        p.phase_end()
        return []
    mz0 = p.op("vector", lambda v: v.memset(kf01[0][64:128, :], 0.0), inc="sv")
    mz1 = p.op("vector", lambda v: v.memset(kf01[1][0:64, :], 0.0), inc="sv")

    wload_free = []
    xnt_free = [[], []]
    ev_free = {"q": [], "k": [], "v": []}
    rope_free = []
    vtok_free = ln_done
    s_free = []
    exp_free = []
    pt_free = [[], [], []]
    ae_free = []
    pair_done = []

    for pp in range(A_PAIRS):
        qc, kc, vc = pp * 128, 1024 + pp * 128, 2048 + pp * 128
        wl = []
        for ic in range(8):
            wl.append(p.dma("gpsimd", wq[ic][:],
                            bc.ein("WqkvT0")[ic * 128:(ic + 1) * 128, qc:qc + 128],
                            waits=wload_free, inc="wlA"))
            wl.append(p.dma("gpsimd", wk[ic][:],
                            bc.ein("WqkvT0")[ic * 128:(ic + 1) * 128, kc:kc + 128],
                            inc="wlA"))
            wl.append(p.dma("gpsimd", wv[ic][:],
                            bc.ein("WqkvT0")[ic * 128:(ic + 1) * 128, vc:vc + 128],
                            inc="wlA"))
        wload_free = []
        mm_marks = []
        for tcb in range(8):
            sl = slice(tcb * 512, (tcb + 1) * 512)
            mq = mk_ = mv_ = None
            for ic in range(8):
                par = ic & 1
                lx = p.dma("sync", xnt_s[par][:],
                           xnT_dram[ic * 128:(ic + 1) * 128, sl],
                           waits=xnt_free[par] + ln_done, inc="xt%d" % par)
                st_ = (ic == 0)
                en_ = (ic == 7)
                mq = p.op("tensor", lambda t, w=wq[ic], x=xnt_s[par], s=st_, e=en_:
                          t.matmul(out=qps[:], lhsT=w[:], rhs=x[:], start=s, stop=e),
                          waits=[lx, wl[-1]] + (ev_free["q"] if ic == 0 else []),
                          inc="sp")
                mk_ = p.op("tensor", lambda t, w=wk[ic], x=xnt_s[par], s=st_, e=en_:
                           t.matmul(out=kps[:], lhsT=w[:], rhs=x[:], start=s, stop=e),
                           waits=[wl[-1]] + (ev_free["k"] if ic == 0 else []),
                           inc="sp")
                mv_ = p.op("tensor", lambda t, w=wv[ic], x=xnt_s[par], s=st_, e=en_:
                           t.matmul(out=vps[:], lhsT=w[:], rhs=x[:], start=s, stop=e),
                           waits=[wl[-1]] + (ev_free["v"] if ic == 0 else []),
                           inc="sp")
                xnt_free[par] = [mv_]
            eq = p.op("vector", lambda v, s=sl: v.tensor_copy(out=qraw[:, s], in_=qps[:]),
                      waits=[mq] + rope_free, inc="sv")
            ek = p.op("scalar", lambda sc, s=sl: sc.copy(out=kraw[:, s], in_=kps[:]),
                      waits=[mk_] + rope_free, inc="sa")
            ev = p.op("vector", lambda v, s=sl: v.tensor_copy(out=vT[:, s], in_=vps[:]),
                      waits=[mv_] + vtok_free, inc="sv")
            ev_free = {"q": [eq], "k": [ek], "v": [ev]}
            mm_marks.append((eq, ek, ev))
        rp_marks = []
        rps_free = []
        for tcb in range(8):
            sl = slice(tcb * 512, (tcb + 1) * 512)
            eq, ek, ev = mm_marks[tcb]
            for which, (raw, mk2) in enumerate(((qraw, eq), (kraw, ek))):
                mr = p.op("tensor", lambda t, r=raw, s=sl: t.matmul(
                    out=rps[:], lhsT=r2t[:], rhs=r[:, s], start=True, stop=True),
                    waits=[mk2] + cst_w + rps_free, inc="sp")
                m1 = p.op("gpsimd", lambda g, r=raw, s=sl: g.tensor_tensor(
                    out=t1b[:], in0=r[:, s], in1=cosT[:, s], op=ALU.mult),
                    waits=[mk2] + cst_w + rp_marks[-1:], inc="gv")
                m2 = p.op("vector", lambda v, s=sl: v.tensor_tensor(
                    out=t2b[:], in0=rps[:], in1=sinT[:, s], op=ALU.mult),
                    waits=[mr] + cst_w, inc="sv")
                rps_free = [m2]
                if which == 0:
                    m3 = p.op("vector", lambda v, s=sl: v.tensor_tensor(
                        out=qfin[:, s], in0=t1b[:], in1=t2b[:], op=ALU.add),
                        waits=[m1, m2], inc="sv")
                    rp_marks.append(m3)
                else:
                    p.op("vector", lambda v, s=sl: v.tensor_tensor(
                        out=kf01[0][0:64, s], in0=t1b[0:64, :], in1=t2b[0:64, :],
                        op=ALU.add), waits=[m1, m2, mz0], inc="sv")
                    m3 = p.op("vector", lambda v, s=sl: v.tensor_tensor(
                        out=kf01[1][64:128, s], in0=t1b[64:128, :], in1=t2b[64:128, :],
                        op=ALU.add), waits=[mz1], inc="sv")
                    rp_marks.append(m3)
        rope_free = rp_marks[-2:]
        rope_all = rp_marks[-1]
        if A_SECT == "qkv":
            rope_free = []
            pair_done.append(mm_marks[-1][0])
            continue
        vt_last = None
        for g in range(4):
            tpar = g & 1
            mts = []
            for j in range(8):
                kt = g * 8 + j
                ev = mm_marks[kt // 4][2]
                mts.append(p.op("tensor", lambda t, k=kt, j_=j, pp_=tp_ps[tpar]:
                           t.transpose(out=pp_[:, j_ * 128:(j_ + 1) * 128],
                                       in_=vT[:, k * 128:(k + 1) * 128],
                                       identity=identb[:]),
                           waits=([ev] + vtok_free + bc.tp_war[tpar]) if j == 0 else [ev],
                           inc="sp"))
            vt_last = p.op("vector", lambda v, g_=g, pp_=tp_ps[tpar]: v.tensor_copy(
                out=vtok[:, g_ * 1024:(g_ + 1) * 1024], in_=pp_[:]),
                waits=[mts[-1]], inc="sv")
            bc.tp_war[tpar] = [vt_last]
        v_all = vt_last
        if A_SECT in ("rope", "vt"):
            pair_done.append(vt_last)
            continue
        for qt in range(NTC):
            pieces = [(0, qt, mcau)]
            if qt >= 1:
                pieces.append((128, qt - 1, mupp if (qt % 2 == 0) else None))
            if qt >= 2:
                pieces.append((256, 0, mglob))
            width = 128 * len(pieces)
            for hh in range(2):
                kfh = kf01[hh]
                sm = []
                for (c0, kt, _msk) in pieces:
                    sm.append(p.op("tensor", lambda t, c=c0, k=kt, f=kfh, q=qt:
                              t.matmul(out=sps[:, c:c + 128],
                                       lhsT=qfin[:, q * 128:(q + 1) * 128],
                                       rhs=f[:, k * 128:(k + 1) * 128],
                                       start=True, stop=True),
                              waits=[rope_all] + s_free, inc="sp"))
                mk3 = sm[-1]
                if os.environ.get("A_DBG") and pp == 0 and qt == 0 and hh == 0:
                    dbgS = bc.nc.dram_tensor("dbgS", [128, 128], F32).ap()
                    dbgM = bc.nc.dram_tensor("dbgM", [128, 128], F32).ap()
                    dcp = p.op("vector", lambda v: v.tensor_copy(out=t1b[:, 0:128], in_=sps[:, 0:128]), waits=sm, inc="sv")
                    d1 = p.dma("gpsimd", dbgS, t1b[:, 0:128], waits=[dcp], inc="n1")
                    d2 = p.dma("gpsimd", dbgM, mcau[:], waits=cst_w, inc="n1")
                    sm = sm + [d1, d2, dcp]
                for i, (c0, kt, msk) in enumerate(pieces):
                    if msk is not None:
                        mk3 = p.op("vector", lambda v, c=c0, m=msk: v.tensor_tensor(
                            out=sps[:, c:c + 128], in0=sps[:, c:c + 128], in1=m[:],
                            op=ALU.add), waits=sm + cst_w, inc="sv")
                m1 = p.op("vector", lambda v, w=width: v.reduce_max(
                    out=rmax[:], in_=sps[:, 0:w], axis=AX.X), waits=[mk3] + sm, inc="sv")
                m2 = p.op("vector", lambda v: v.tensor_scalar(
                    out=nbias[:], in0=rmax[:], scalar1=-SCALE, scalar2=None,
                    op0=ALU.mult), inc="sv")
                m3 = p.op("scalar", lambda sc, w=width: sc.activation(
                    out=prow[:, 0:w], in_=sps[:, 0:w], func=AF.Exp, scale=SCALE,
                    bias=nbias[:], accum_out=rsum[:]), waits=[m2] + exp_free, inc="sa")
                m4 = p.op("vector", lambda v: v.reciprocal(out=rsum[:], in_=rsum[:]),
                          waits=[m3], inc="sv")
                m5 = p.op("vector", lambda v, w=width: v.tensor_scalar(
                    out=prow[:, 0:w], in0=prow[:, 0:w], scalar1=rsum[:], scalar2=None,
                    op0=ALU.mult), waits=[m4], inc="sv")
                s_free = [m5]
                exp_free = [m5]
                tpar = hh
                mts = []
                for i, (c0, kt, _msk) in enumerate(pieces):
                    mts.append(p.op("tensor", lambda t, c=c0, i_=i, pp_=tp_ps[tpar]:
                               t.transpose(out=pp_[:, i_ * 128:(i_ + 1) * 128],
                                           in_=prow[:, c:c + 128], identity=identb[:]),
                               waits=([m5] + pt_free[0] + bc.tp_war[tpar]) if i == 0 else [m5],
                               inc="sp"))
                me = p.op("vector", lambda v, w3=128 * len(pieces), pp_=tp_ps[tpar]:
                          v.tensor_copy(out=ptt3[:, 0:w3], in_=pp_[:, 0:w3]),
                          waits=[mts[-1]], inc="sv")
                bc.tp_war[tpar] = [me]
                pev = [me]
                am = None
                for j, (c0, kt, _msk) in enumerate(pieces):
                    am = p.op("tensor", lambda t, k=kt, i_=j,
                              s=(j == 0), e=(j == len(pieces) - 1):
                              t.matmul(out=aps[:],
                                       lhsT=vtok[:, k * 128:(k + 1) * 128],
                                       rhs=ptt3[:, i_ * 128:(i_ + 1) * 128], start=s, stop=e),
                              waits=pev + [v_all] + (ae_free if j == 0 else []),
                              inc="sp")
                pt_free[0] = [am]
                ae = p.op("vector", lambda v, d=attnT[pp], q=qt, b=hh * 64: v.tensor_copy(
                    out=d[b:b + 64, q * 128:(q + 1) * 128], in_=aps[b:b + 64, :]),
                    waits=[am], inc="sv")
                ae_free = [ae]
                if os.environ.get("A_DBG") and pp == 0 and qt == 0 and hh == 0:
                    dbgP = bc.nc.dram_tensor("dbgP", [128, 128], F32).ap()
                    dbgR = bc.nc.dram_tensor("dbgR", [128, 2], F32).ap()
                    d3 = p.dma("gpsimd", dbgP, prow[:, 0:128], waits=[ae], inc="n1")
                    d4 = p.dma("gpsimd", dbgR[:, 0:1], rmax[:], inc="n1", allow_slow_non_contiguous=True)
                    d5 = p.dma("gpsimd", dbgR[:, 1:2], rsum[:], inc="n1", allow_slow_non_contiguous=True)
                    ae_free = ae_free + [d3, d4, d5]
                    s_free = s_free + [d3, d4, d5]
                    exp_free = exp_free + [d3, d4, d5]
        pair_done.append(ae_free[0])
        vtok_free = [ae_free[0]]

    if A_SECT != "all" or A_PAIRS < 8:
        p.phase_end()
        return []
    ost = [bc.sb("A_ost0", [128, 1024], F32)]
    opsum = [(qps, kps), (vps, rps)]
    ev_last = [[], []]
    acc_marks = []
    for qt in range(NTC):
        par = qt & 1
        ps0, ps1 = opsum[par]
        mm0 = mm1 = None
        for pc in range(8):
            st_ = (pc == 0)
            en_ = (pc == 7)
            w0 = [pair_done[pc]] if qt == 0 else []
            mm0 = p.op("tensor", lambda t, pcc=pc, q=qt, s=st_, e=en_, d=ps0: t.matmul(
                out=d[:], lhsT=attnT[pcc][:, q * 128:(q + 1) * 128],
                rhs=wo0[pcc][:, 0:512], start=s, stop=e),
                waits=w0 + cst_w + (ev_last[par] if pc == 0 else []), inc="sp")
            mm1 = p.op("tensor", lambda t, pcc=pc, q=qt, s=st_, e=en_, d=ps1: t.matmul(
                out=d[:], lhsT=attnT[pcc][:, q * 128:(q + 1) * 128],
                rhs=wo0[pcc][:, 512:1024], start=s, stop=e), inc="sp")
        war = acc_marks[-1:]
        e0 = p.op("vector", lambda v, d=ost[0], sp_=ps0: v.tensor_copy(
            out=d[:, 0:512], in_=sp_[:]), waits=[mm1] + war, inc="sv")
        e1 = p.op("scalar", lambda s, d=ost[0], sp_=ps1: s.copy(
            out=d[:, 512:1024], in_=sp_[:]), waits=[mm1] + war, inc="sa")
        ev_last[par] = [e0, e1]
        md = p.dma("gpsimd", xo_dram[qt * 128:(qt + 1) * 128, :], ost[0][:],
                   waits=list(wait0) + [e0, e1], inc="ac%d" % par, accum_op=ALU.add)
        acc_marks.append(md)
    p.phase_end()
    return [p.mark("ac0"), p.mark("ac1")]




def emit_B(bc, x_src, xo_dram, wait0=()):
    p = bc.p
    identb = bc.sb("B_identb", [128, 128], BF16)
    tp_ps = [bc.ps("B_tp%d" % i, [128, 1024], BF16) for i in range(2)]
    xnT = [bc.sb("B_xnT%d" % i, [128, T], BF16) for i in range(8)]
    wg = [bc.sb("B_wg%d" % i, [128, 16], BF16) for i in range(8)]

    p.phase_begin()
    p.dma("gpsimd", identb[:], bc.ein("ident"), waits=list(wait0), inc="cgB")
    for i in range(8):
        p.dma("gpsimd", wg[i][:], bc.ein("WgT")[i * 128:(i + 1) * 128, :],
              waits=list(wait0), inc="cgB")
    cst = p.mark("cgB")

    emit_ln(bc, "Bln", x_src, identb, tp_ps, xnT_sb=xnT,
            wait0=list(wait0) + [cst])
    ln_done = [p.mark("sv"), p.mark("sp")]

    sc = bc.sb("B_sc", [128, 16], F32)
    ga = bc.sb("B_ga", [128, 4], F32)
    gb = bc.sb("B_gb", [128, 4], F32)
    gc = bc.sb("B_gc", [128, 4], F32)
    gd = bc.sb("B_gd", [128, 4], F32)
    ge = bc.sb("B_ge", [128, 4], F32)
    gf = bc.sb("B_gf", [128, 4], F32)
    top8 = bc.sb("B_top8", [128, 8], F32)
    msk = bc.sb("B_msk", [128, 16], F32)
    msk2 = bc.sb("B_msk2", [128, 16], F32)
    t16 = bc.sb("B_t16", [128, 16], F32)
    w1v = bc.sb("B_w1v", [128, 1], F32)
    comb = bc.sb("B_comb", [128, 16 * NTC], F32)
    lgps = bc.ps("B_lgps", [128, 16])

    TT = ALU
    gate_done = []
    lg_free = []
    for tc in range(NTC):
        tsl = slice(tc * 128, (tc + 1) * 128)
        mlg = None
        for hc in range(8):
            mlg = p.op("tensor", lambda t, h=hc, s=tsl, st=(hc == 0), en=(hc == 7):
                       t.matmul(out=lgps[:], lhsT=xnT[h][:, s], rhs=wg[h][:],
                                start=st, stop=en),
                       waits=ln_done + [cst] + (lg_free if hc == 0 else []), inc="sp")
        msig = p.op("scalar", lambda s: s.activation(out=sc[:], in_=lgps[:],
                    func=AF.Sigmoid), waits=[mlg] + gate_done[-1:], inc="sa")
        lg_free = [msig]
        scv = sc[:].rearrange("p (g e) -> p g e", e=4)
        ops = [
            lambda v: v.tensor_tensor(out=ga[:], in0=scv[:, :, 0:1], in1=scv[:, :, 1:2], op=TT.max),
            lambda v: v.tensor_tensor(out=gb[:], in0=scv[:, :, 0:1], in1=scv[:, :, 1:2], op=TT.min),
            lambda v: v.tensor_tensor(out=gc[:], in0=scv[:, :, 2:3], in1=scv[:, :, 3:4], op=TT.max),
            lambda v: v.tensor_tensor(out=gd[:], in0=scv[:, :, 2:3], in1=scv[:, :, 3:4], op=TT.min),
            lambda v: v.tensor_tensor(out=ge[:], in0=ga[:], in1=gc[:], op=TT.min),
            lambda v: v.tensor_tensor(out=gf[:], in0=gb[:], in1=gd[:], op=TT.max),
            lambda v: v.tensor_tensor(out=ga[:], in0=ga[:], in1=gc[:], op=TT.max),
            lambda v: v.tensor_tensor(out=gb[:], in0=ge[:], in1=gf[:], op=TT.max),
            lambda v: v.tensor_tensor(out=ga[:], in0=ga[:], in1=gb[:], op=TT.add),
            lambda v: v.reduce_max(out=w1v[:], in_=ga[:], axis=AX.X),
            lambda v: v.tensor_scalar(out=gc[:], in0=ga[:], scalar1=w1v[:], scalar2=None, op0=TT.is_equal),
            lambda v: v.memset(gd[:, 0:1], 0.0),
            lambda v: v.tensor_copy(out=gd[:, 1:4], in_=gc[:, 0:3]),
            lambda v: v.tensor_tensor(out=gd[:, 2:4], in0=gd[:, 2:4], in1=gd[:, 0:2], op=TT.add),
            lambda v: v.tensor_scalar(out=ge[:], in0=gd[:], scalar1=0.5, scalar2=None, op0=TT.is_lt),
            lambda v: v.tensor_tensor(out=gc[:], in0=gc[:], in1=ge[:], op=TT.mult),
            lambda v: v.tensor_scalar(out=ge[:], in0=gc[:], scalar1=NEG, scalar2=None, op0=TT.mult),
            lambda v: v.tensor_tensor(out=gd[:], in0=ga[:], in1=ge[:], op=TT.add),
            lambda v: v.reduce_max(out=w1v[:], in_=gd[:], axis=AX.X),
            lambda v: v.tensor_scalar(out=ge[:], in0=gd[:], scalar1=w1v[:], scalar2=None, op0=TT.is_equal),
            lambda v: v.memset(gf[:, 0:1], 0.0),
            lambda v: v.tensor_copy(out=gf[:, 1:4], in_=ge[:, 0:3]),
            lambda v: v.tensor_tensor(out=gf[:, 2:4], in0=gf[:, 2:4], in1=gf[:, 0:2], op=TT.add),
            lambda v: v.tensor_scalar(out=gf[:], in0=gf[:], scalar1=0.5, scalar2=None, op0=TT.is_lt),
            lambda v: v.tensor_tensor(out=ge[:], in0=ge[:], in1=gf[:], op=TT.mult),
            lambda v: v.tensor_tensor(out=ga[:], in0=gc[:], in1=ge[:], op=TT.add),
            lambda v: v.tensor_scalar(out=gb[:], in0=ga[:], scalar1=1.0, scalar2=-NEG, op0=TT.subtract, op1=TT.mult),
        ]
        m = None
        for i, f in enumerate(ops):
            m = p.op("vector", f, waits=[msig] if i == 0 else [], inc="sv")
        for g in range(4):
            m = p.op("vector", lambda v, g_=g: v.tensor_scalar(
                out=msk[:, 4 * g_:4 * g_ + 4], in0=sc[:, 4 * g_:4 * g_ + 4],
                scalar1=gb[:, g_:g_ + 1], scalar2=None, op0=TT.add), inc="sv")
        ops2 = [
            lambda v: v.max(out=top8[:], in_=msk[:]),
            lambda v: v.memset(top8[:, 4:8], NEG),
            lambda v: v.match_replace(out=msk2[:], in_to_replace=top8[:], in_values=msk[:], imm_value=NEG),
            lambda v: v.tensor_tensor(out=t16[:], in0=msk[:], in1=msk2[:], op=TT.is_gt),
            lambda v: v.reduce_sum(out=w1v[:], in_=top8[:, 0:4], axis=AX.X),
            lambda v: v.tensor_scalar(out=w1v[:], in0=w1v[:], scalar1=1e-6, scalar2=None, op0=TT.add),
            lambda v: v.reciprocal(out=w1v[:], in_=w1v[:]),
            lambda v: v.tensor_tensor(out=t16[:], in0=t16[:], in1=sc[:], op=TT.mult),
        ]
        for f in ops2:
            m = p.op("vector", f, inc="sv")
        m = p.op("vector", lambda v, c=tc: v.tensor_scalar(
            out=comb[:, c * 16:(c + 1) * 16], in0=t16[:], scalar1=w1v[:],
            scalar2=None, op0=TT.mult), inc="sv")
        gate_done.append(m)
    gate_all = gate_done[-1]

    w1t = [[bc.sb("B_w1t%d_%d" % (sl, i), [128, 1024], BF16) for i in range(8)] for sl in range(1)]
    w3t = [[bc.sb("B_w3t%d_%d" % (sl, i), [128, 1024], BF16) for i in range(8)] for sl in range(1)]
    w2t = [[bc.sb("B_w2t%d_%d" % (sl, i), [128, 1024], BF16) for i in range(8)] for sl in range(1)]
    gf32 = [bc.sb("B_gf32%d" % i, [128, 512], F32) for i in range(2)]
    uf32 = bc.sb("B_uf32", [128, 512], F32)
    guT = [bc.sb("B_guT%d" % i, [128, 512], BF16) for i in range(8)]
    yacc = [bc.sb("B_yacc%d" % i, [128, 1024], F32) for i in range(4)]
    etmp = bc.sb("B_etmp", [128, 1024], F32)
    gps2 = [bc.ps("B_gps%d" % i, [128, 512]) for i in range(2)]
    ups = bc.ps("B_ups", [128, 512])
    eps0 = bc.ps("B_eps0", [128, 512])
    eps1 = bc.ps("B_eps1", [128, 512])

    ln_all = p.mark("sv")
    groups = [("S",)] + [(2 * i, 2 * i + 1) for i in range(8)]
    slot_free = [[], []]
    NSLOT = 1
    silu_free = [[], []]
    gu_free = [[] for _ in range(8)]
    uev_free = []
    yacc_free = [[] for _ in range(4)]
    yacc_dma = [[] for _ in range(4)]
    etmp_free = []
    eps_free = [[], []]
    acc_n = 0
    for gi, grp in enumerate(groups):
        sl = gi % NSLOT
        wl = []
        if grp[0] == "S":
            for hc in range(8):
                wl.append(p.dma("gpsimd", w1t[sl][hc][:], bc.ein("W1sT")[hc * 128:(hc + 1) * 128, :], waits=slot_free[sl], inc="wlB"))
                wl.append(p.dma("gpsimd", w3t[sl][hc][:], bc.ein("W3sT")[hc * 128:(hc + 1) * 128, :], inc="wlB"))
                wl.append(p.dma("gpsimd", w2t[sl][hc][:], bc.ein("W2sT")[hc * 128:(hc + 1) * 128, :], inc="wlB"))
        else:
            for li, e in enumerate(grp):
                for hc in range(8):
                    wl.append(p.dma("gpsimd", w1t[sl][hc][:, li * 512:(li + 1) * 512], bc.ein("W1eT")[e, hc * 128:(hc + 1) * 128, :], waits=slot_free[sl], inc="wlB"))
                    wl.append(p.dma("gpsimd", w3t[sl][hc][:, li * 512:(li + 1) * 512], bc.ein("W3eT")[e, hc * 128:(hc + 1) * 128, :], inc="wlB"))
                for fc in range(4):
                    wl.append(p.dma("gpsimd", w2t[sl][li * 4 + fc][:], bc.ein("W2eT")[e, fc * 128:(fc + 1) * 128, :], inc="wlB"))
        wld = wl[-1]
        last_mm = []
        for tcb in range(8):
            tsl = slice(tcb * 512, (tcb + 1) * 512)
            for li in range(len(grp)):
                shared = grp[0] == "S"
                nfc = 8 if shared else 4
                fo = 0 if shared else li * 512
                w2b = 0 if shared else li * 4
                gu_marks = []
                for fc in range(nfc):
                    gpar = fc & 1
                    gp = gps2[gpar]
                    mg = mu = None
                    for hc in range(8):
                        st = (hc == 0)
                        en = (hc == 7)
                        mg = p.op("tensor", lambda t, h=hc, f=fc, o=fo, s=tsl, sl_=sl, st_=st, en_=en, g_=gp:
                                  t.matmul(out=g_[:], lhsT=w1t[sl_][h][:, o + f * 128:o + (f + 1) * 128],
                                           rhs=xnT[h][:, s], start=st_, stop=en_),
                                  waits=[wld, ln_all] + (silu_free[gpar] if hc == 0 else []), inc="sp")
                        mu = p.op("tensor", lambda t, h=hc, f=fc, o=fo, s=tsl, sl_=sl, st_=st, en_=en:
                                  t.matmul(out=ups[:], lhsT=w3t[sl_][h][:, o + f * 128:o + (f + 1) * 128],
                                           rhs=xnT[h][:, s], start=st_, stop=en_),
                                  waits=(uev_free if hc == 0 else []), inc="sp")
                    ms = p.op("scalar", lambda s, g_=gp, d=gf32[gpar]: s.activation(
                              out=d[:], in_=g_[:], func=AF.Silu), waits=[mg], inc="sa")
                    mu2 = p.op("vector", lambda v: v.tensor_copy(out=uf32[:], in_=ups[:]),
                               waits=[mu], inc="sv")
                    mgu = p.op("vector", lambda v, f=fc, d=gf32[gpar]: v.tensor_tensor(
                        out=guT[f][:], in0=d[:], in1=uf32[:], op=TT.mult),
                        waits=[ms, mu2] + gu_free[fc], inc="sv")
                    silu_free[gpar] = [mgu]
                    uev_free = [mgu]
                    gu_marks.append(mgu)
                    gu_free[fc] = []
                for ts in range(4):
                    me0 = me1 = None
                    for fc in range(nfc):
                        st = (fc == 0)
                        en = (fc == nfc - 1)
                        me0 = p.op("tensor", lambda t, f=fc, ts_=ts, b=w2b, sl_=sl, st_=st, en_=en:
                                   t.matmul(out=eps0[:], lhsT=guT[f][:, ts_ * 128:(ts_ + 1) * 128],
                                            rhs=w2t[sl_][b + f][:, 0:512], start=st_, stop=en_),
                                   waits=gu_marks + (eps_free[0] if fc == 0 else []), inc="sp")
                        me1 = p.op("tensor", lambda t, f=fc, ts_=ts, b=w2b, sl_=sl, st_=st, en_=en:
                                   t.matmul(out=eps1[:], lhsT=guT[f][:, ts_ * 128:(ts_ + 1) * 128],
                                            rhs=w2t[sl_][b + f][:, 512:1024], start=st_, stop=en_),
                                   waits=(eps_free[1] if fc == 0 else []), inc="sp")
                        gu_free[fc].append(me1)
                    tci = tcb * 4 + ts
                    if shared:
                        ev0 = p.op("vector", lambda v, y=yacc[ts]: v.tensor_copy(
                            out=y[:, 0:512], in_=eps0[:]), waits=[me1] + yacc_dma[ts], inc="sv")
                        ev1 = p.op("scalar", lambda s, y=yacc[ts]: s.copy(
                            out=y[:, 512:1024], in_=eps1[:]), waits=[me1] + yacc_dma[ts], inc="sa")
                    elif li == 0:
                        ev0 = p.op("scalar", lambda s, y=yacc[ts], t_=tci, e=grp[0]: s.activation(
                            out=y[:, 0:512], in_=eps0[:], func=AF.Copy,
                            scale=comb[:, t_ * 16 + e:t_ * 16 + e + 1]),
                            waits=[me1, gate_all] + yacc_dma[ts], inc="sa")
                        ev1 = p.op("scalar", lambda s, y=yacc[ts], t_=tci, e=grp[0]: s.activation(
                            out=y[:, 512:1024], in_=eps1[:], func=AF.Copy,
                            scale=comb[:, t_ * 16 + e:t_ * 16 + e + 1]), inc="sa")
                    else:
                        et0 = p.op("scalar", lambda s, t_=tci, e=grp[1]: s.activation(
                            out=etmp[:, 0:512], in_=eps0[:], func=AF.Copy,
                            scale=comb[:, t_ * 16 + e:t_ * 16 + e + 1]),
                            waits=[me1, gate_all] + etmp_free, inc="sa")
                        et1 = p.op("scalar", lambda s, t_=tci, e=grp[1]: s.activation(
                            out=etmp[:, 512:1024], in_=eps1[:], func=AF.Copy,
                            scale=comb[:, t_ * 16 + e:t_ * 16 + e + 1]), inc="sa")
                        ev0 = p.op("vector", lambda v, y=yacc[ts]: v.tensor_tensor(
                            out=y[:], in0=y[:], in1=etmp[:], op=TT.add),
                            waits=[et0, et1], inc="sv")
                        ev1 = ev0
                        etmp_free = [ev0]
                    eps_free = [[ev0], [ev1]]
                    yacc_free[ts] = [ev0, ev1]
                    if li == len(grp) - 1:
                        md = p.dma("gpsimd", xo_dram[tci * 128:(tci + 1) * 128, :],
                                   yacc[ts][:], waits=list(wait0) + [ev0, ev1],
                                   inc="acB%d" % ts, accum_op=ALU.add)
                        yacc_dma[ts] = [md]
                        acc_n += 1
            last_mm = [me1]
        slot_free[sl] = last_mm
    p.phase_end()
    return [p.mark("acB%d" % i) for i in range(4)]




def emit_C(bc, x_src, xo_dram, xnT_dram, wait0=()):
    p = bc.p
    SC = 1.0 / np.sqrt(512.0)
    identb = bc.sb("C_identb", [128, 128], BF16)
    mcau = bc.sb("C_mcau", [128, 128], F32)
    tp_ps = [bc.ps("C_tp%d" % i, [128, 1024], BF16) for i in range(2)]

    p.phase_begin()
    p.dma("gpsimd", identb[:], bc.ein("ident"), waits=list(wait0), inc="cgC")
    p.dma("sync", mcau[:], bc.ein("mask_causal"), waits=list(wait0), inc="cstC")
    cst = [p.mark("cstC"), p.mark("cgC")]

    emit_ln(bc, "Cln", x_src, identb, tp_ps, xnT_dram=xnT_dram,
            wait0=list(wait0) + cst)
    ln_done = [p.mark("s0"), p.mark("s1"), p.mark("sv"), p.mark("sp")]

    wq = [bc.sb("C_wq%d" % i, [128, 512], BF16) for i in range(8)]
    wk = [bc.sb("C_wk%d" % i, [128, 512], BF16) for i in range(8)]
    wv = [bc.sb("C_wv%d" % i, [128, 512], BF16) for i in range(8)]
    woc = [bc.sb("C_wo%d" % i, [128, 1024], BF16) for i in range(4)]
    xnt_f = [bc.sb("C_xnt%d" % i, [128, 512], BF16) for i in range(8)]
    qT = [bc.sb("C_qT%d" % i, [128, T], BF16) for i in range(4)]
    kT = [bc.sb("C_kT%d" % i, [128, T], BF16) for i in range(4)]
    vC = bc.sb("C_v", [128, 32 * 512], BF16)
    peb = [bc.sb("C_peb%d" % i, [128, 512], F32) for i in range(2)]
    srow = bc.sb("C_srow", [128, T], F32)
    prow = bc.sb("C_prow", [128, T], BF16)
    ptC = bc.sb("C_ptC", [128, T], BF16)
    atm = bc.sb("C_atm", [128, 512], BF16)
    attnT4 = bc.sb("C_attnT", [128, 512], BF16)
    ost = [bc.sb("C_ost%d" % i, [128, 1024], F32) for i in range(2)]
    rmax = bc.sb("C_rmax", [128, 1], F32)
    nbias = bc.sb("C_nbias", [128, 1], F32)
    rsum = bc.sb("C_rsum", [128, 1], F32)

    cps = [bc.ps("C_cps%d" % i, [128, 512]) for i in range(2)]
    sps = bc.ps("C_sps", [128, 512])
    ops = [bc.ps("C_ops%d" % i, [128, 512]) for i in range(2)]

    wl_free = []
    xnt_free = [[] for _ in range(8)]
    pe_free = [[], []]
    qkv_free = {}
    cps_free = [[], []]
    s_free = []
    exp_free = []
    pt_free = []
    atm_free = []
    at_free = [[] for _ in range(4)]
    ost_free = [[], []]
    ops_free = []
    acc_last = [[], []]
    head_prev_total = []
    for h in range(8):
        base = h * 1536
        wl = []
        for ic in range(8):
            wl.append(p.dma("gpsimd", wq[ic][:],
                            bc.ein("WqkvT1")[ic * 128:(ic + 1) * 128, base:base + 512],
                            waits=wl_free, inc="wlC"))
            wl.append(p.dma("gpsimd", wk[ic][:],
                            bc.ein("WqkvT1")[ic * 128:(ic + 1) * 128, base + 512:base + 1024],
                            inc="wlC"))
            wl.append(p.dma("gpsimd", wv[ic][:],
                            bc.ein("WqkvT1")[ic * 128:(ic + 1) * 128, base + 1024:base + 1536],
                            inc="wlC"))
        for dc in range(4):
            wl.append(p.dma("gpsimd", woc[dc][:],
                            bc.ein("WoT1")[h * 512 + dc * 128:h * 512 + (dc + 1) * 128, :],
                            inc="wlC"))
        wld = wl[-1]
        for tcb in range(8):
            tsl = slice(tcb * 512, (tcb + 1) * 512)
            lds = []
            for ic in range(8):
                lds.append(p.dma("sync", xnt_f[ic][:],
                                 xnT_dram[ic * 128:(ic + 1) * 128, tsl],
                                 waits=xnt_free[ic] + ln_done, inc="xtC%d" % ic))
            for which, (wt, dst) in enumerate(((wq, qT), (wk, kT))):
                for oc in range(4):
                    par = oc & 1
                    pl = p.dma("sync", peb[par][:],
                               bc.ein("peT")[oc * 128:(oc + 1) * 128, tsl],
                               waits=pe_free[par], inc="peC%d" % par)
                    mm = None
                    for ic in range(8):
                        mm = p.op("tensor", lambda t, i=ic, o=oc, s=tsl, w=wt, par_=par,
                                  st=(ic == 0), en=(ic == 7):
                                  t.matmul(out=cps[par_][:], lhsT=w[i][:, o * 128:(o + 1) * 128],
                                           rhs=xnt_f[i][:], start=st, stop=en),
                                  waits=[wld, lds[ic]] + (cps_free[par] if ic == 0 else []),
                                  inc="sp")
                        xnt_free[ic] = [mm]
                    ev = p.op("vector", lambda v, d=dst[oc], s=tsl, par_=par: v.tensor_tensor(
                        out=d[:, s], in0=cps[par_][:], in1=peb[par_][:], op=ALU.add),
                        waits=[mm, pl], inc="sv")
                    pe_free[par] = [ev]
                    cps_free[par] = [ev]
                    qkv_free[(which, oc)] = ev
            for j in range(4):
                par = j & 1
                mm = None
                for ic in range(8):
                    mm = p.op("tensor", lambda t, i=ic, j_=j, st=(ic == 0), en=(ic == 7), par_=par:
                              t.matmul(out=cps[par_][:], lhsT=xnt_f[i][:, j_ * 128:(j_ + 1) * 128],
                                       rhs=wv[i][:], start=st, stop=en),
                              waits=(cps_free[par] if ic == 0 else []), inc="sp")
                    xnt_free[ic] = [mm]
                tj = tcb * 4 + j
                ev = p.op("vector", lambda v, t_=tj, par_=par: v.tensor_copy(
                    out=vC[:, t_ * 512:(t_ + 1) * 512], in_=cps[par_][:]),
                    waits=[mm], inc="sv")
                cps_free[par] = [ev]
                qkv_free[("v", tj)] = ev
        qkv_all = [qkv_free[("v", 31)], qkv_free[(0, 3)], qkv_free[(1, 3)]]
        for qt in range(NTC):
            kext = (qt + 1) * 128
            nkcb = (kext + 511) // 512
            sevs = []
            for kcb in range(nkcb):
                wdt = min(512, kext - kcb * 512)
                mm = None
                for dc in range(4):
                    mm = p.op("tensor", lambda t, d=dc, q=qt, k=kcb, w=wdt,
                              st=(dc == 0), en=(dc == 3):
                              t.matmul(out=sps[:, 0:w], lhsT=qT[d][:, q * 128:(q + 1) * 128],
                                       rhs=kT[d][:, k * 512:k * 512 + w], start=st, stop=en),
                              waits=qkv_all + s_free, inc="sp")
                ev = p.op("scalar", lambda s, k=kcb, w=wdt: s.copy(
                    out=srow[:, k * 512:k * 512 + w], in_=sps[:, 0:w]),
                    waits=[mm] + exp_free, inc="sa")
                s_free = [ev]
                sevs.append(ev)
            mmask = p.op("vector", lambda v, q=qt: v.tensor_tensor(
                out=srow[:, q * 128:(q + 1) * 128], in0=srow[:, q * 128:(q + 1) * 128],
                in1=mcau[:], op=ALU.add), waits=sevs + cst, inc="sv")
            m1 = p.op("vector", lambda v, w=kext: v.reduce_max(
                out=rmax[:], in_=srow[:, 0:w], axis=AX.X), waits=[mmask] + sevs, inc="sv")
            m2 = p.op("vector", lambda v: v.tensor_scalar(
                out=nbias[:], in0=rmax[:], scalar1=-SC, scalar2=None, op0=ALU.mult), inc="sv")
            m3 = p.op("scalar", lambda s, w=kext: s.activation(
                out=prow[:, 0:w], in_=srow[:, 0:w], func=AF.Exp, scale=SC,
                bias=nbias[:], accum_out=rsum[:]), waits=[m2] + pt_free, inc="sa")
            m4 = p.op("vector", lambda v: v.reciprocal(out=rsum[:], in_=rsum[:]),
                      waits=[m3], inc="sv")
            m5 = p.op("vector", lambda v, w=kext: v.tensor_scalar(
                out=prow[:, 0:w], in0=prow[:, 0:w], scalar1=rsum[:], scalar2=None,
                op0=ALU.mult), waits=[m4], inc="sv")
            exp_free = [m5]
            ptevs = []
            ngr = (qt + 1 + 7) // 8
            for g in range(ngr):
                tpar = g & 1
                lo = g * 8
                hi = min(qt + 1, lo + 8)
                mts = []
                for kt in range(lo, hi):
                    mts.append(p.op("tensor", lambda t, k=kt, j_=kt - lo, pp_=tp_ps[tpar]:
                               t.transpose(out=pp_[:, j_ * 128:(j_ + 1) * 128],
                                           in_=prow[:, k * 128:(k + 1) * 128],
                                           identity=identb[:]),
                               waits=([m5] + bc.tp_war[tpar]) if kt == lo else [m5],
                               inc="sp"))
                me = p.op("vector", lambda v, l=lo, h_=hi, pp_=tp_ps[tpar]: v.tensor_copy(
                    out=ptC[:, l * 128:h_ * 128], in_=pp_[:, 0:(h_ - l) * 128]),
                    waits=[mts[-1]], inc="sv")
                bc.tp_war[tpar] = [me]
                ptevs.append(me)
            pt_free = [ptevs[-1]]
            mm = None
            for kt in range(qt + 1):
                mm = p.op("tensor", lambda t, k=kt, st=(kt == 0), en=(kt == qt):
                          t.matmul(out=cps[0][:], lhsT=ptC[:, k * 128:(k + 1) * 128],
                                   rhs=vC[:, k * 512:(k + 1) * 512], start=st, stop=en),
                          waits=ptevs + ([qkv_free[("v", 31)]] if True else []) +
                                (cps_free[0] if kt == 0 else []), inc="sp")
            mev = p.op("vector", lambda v: v.tensor_copy(out=atm[:], in_=cps[0][:]),
                       waits=[mm] + atm_free, inc="sv")
            cps_free[0] = [mev]
            tpar = qt & 1
            mts2 = []
            for dc in range(4):
                mts2.append(p.op("tensor", lambda t, d=dc, pp_=tp_ps[tpar]: t.transpose(
                    out=pp_[:, d * 128:(d + 1) * 128], in_=atm[:, d * 128:(d + 1) * 128],
                    identity=identb[:]),
                    waits=([mev] + bc.tp_war[tpar]) if dc == 0 else [mev], inc="sp"))
            me = p.op("vector", lambda v, pp_=tp_ps[tpar]: v.tensor_copy(
                out=attnT4[:], in_=pp_[:, 0:512]), waits=[mts2[-1]] + at_free[0], inc="sv")
            bc.tp_war[tpar] = [me]
            aevs = [me]
            atm_free = [me]
            par = qt & 1
            mo = [None, None]
            for half in range(2):
                for dc in range(4):
                    mo[half] = p.op("tensor", lambda t, d=dc, hf=half, st=(dc == 0), en=(dc == 3):
                                    t.matmul(out=ops[hf][:], lhsT=attnT4[:, d * 128:(d + 1) * 128],
                                             rhs=woc[d][:, hf * 512:(hf + 1) * 512],
                                             start=st, stop=en),
                                    waits=aevs + [wld] + (ops_free if dc == 0 else []),
                                    inc="sp")
            at_free[0] = [mo[1]]
            e0 = p.op("vector", lambda v, d=ost[par]: v.tensor_copy(
                out=d[:, 0:512], in_=ops[0][:]), waits=[mo[0], mo[1]] + acc_last[par], inc="sv")
            e1 = p.op("scalar", lambda s, d=ost[par]: s.copy(
                out=d[:, 512:1024], in_=ops[1][:]), waits=[mo[0], mo[1]] + acc_last[par], inc="sa")
            ost_free[par] = [e0, e1]
            ops_free = [e0, e1]
            md = p.dma("gpsimd", xo_dram[qt * 128:(qt + 1) * 128, :], ost[par][:],
                       waits=list(wait0) + [e0, e1] + head_prev_total,
                       inc="acC%d" % par, accum_op=ALU.add)
            acc_last[par] = [md]
        wl_free = [mo[1]]
        head_prev_total = [p.mark("acC0"), p.mark("acC1")]
    p.phase_end()
    return [p.mark("acC0"), p.mark("acC1")]




def emit_D(bc, x_src, xo_dram, wait0=()):
    p = bc.p
    nc = bc.nc
    K1 = 2048
    for _nm in ("saD", "scpD", "gaD0", "gaD1", "gaD2", "gaD3", "cc"):
        p.dma_like.add(_nm)

    xn_dram = nc.dram_tensor("D_xn", [T, DM], BF16).ap()
    sc_loc = nc.dram_tensor("D_scloc", [T, 8], F32).ap()
    sc_full = nc.dram_tensor("D_scfull", [8 * T, 8], F32).ap()
    pairbuf = nc.dram_tensor("D_pairs", [8 * C1, 2], F32).ap()

    identb = bc.sb("D_identb", [128, 128], BF16)
    identf = bc.sb("D_identf", [128, 128], F32)
    triu = bc.sb("D_triu", [128, 128], F32)
    wr = [bc.sb("D_wr%d" % i, [128, 8], F32) for i in range(8)]
    wrb = bc.sb("D_wrb", [1, 8], F32)
    onesf = bc.sb("D_onesf", [1, 128], F32)
    ones128 = bc.sb("D_ones128", [128, 1], F32)
    tp_ps = [bc.ps("D_tp%d" % i, [128, 1024], BF16) for i in range(2)]
    tpf = bc.ps("D_tpf", [128, 128], F32)
    psmall = bc.ps("D_psmall", [128, 512])
    h1ps = bc.ps("D_h1ps", [128, 512])
    eps0 = bc.ps("D_eps0", [128, 512])
    eps1 = bc.ps("D_eps1", [128, 512])
    rps = psmall[:, 0:8]
    tbps = psmall[:, 8:16]
    rkp = psmall[:, 16:48]
    redp = psmall[0:1, 48:88]

    xf32 = [bc.sb("D_xf32_%d" % i, [128, 128], F32) for i in range(8)]
    lgs = bc.sb("D_lgs", [128, 8], F32)
    rmax = bc.sb("D_rmax", [128, 1], F32)
    nbias = bc.sb("D_nbias", [128, 1], F32)
    rsum = bc.sb("D_rsum", [128, 1], F32)
    scores_sb = bc.sb("D_scores", [128, 8 * NTC], F32)

    p.phase_begin()
    p.dma("gpsimd", identb[:], bc.ein("ident"), waits=list(wait0), inc="cgD")
    p.dma("sync", identf[:], bc.ein("ident"), waits=list(wait0), inc="cstD")
    p.dma("sync", triu[:], bc.ein("triuS"), inc="cstD")
    for i in range(8):
        p.dma("sync", wr[i][:], bc.ein("Wr")[i * 128:(i + 1) * 128, :], inc="cstD")
    p.dma("sync", wrb[:], bc.ein("Wrb"), inc="cstD")
    mo1 = p.op("vector", lambda v: v.memset(onesf[:], 1.0), waits=list(wait0), inc="sv")
    mo2 = p.op("vector", lambda v: v.memset(ones128[:], 1.0), inc="sv")
    bc.breg = bc.ctx.enter_context(nc.gpsimd.register("D_bnd"))
    mreg = p.op("gpsimd", lambda g: g.reg_mov(bc.breg, 8 * C1 - 1), inc="gv")
    cst = [p.mark("cstD"), p.mark("cgD"), mo1, mo2]

    rt_state = {"tpf_free": [], "xf_free": [[] for _ in range(8)], "rps_free": [],
                "lg_free": [], "sc_marks": []}

    def router_cb(tc, xn32, mark):
        mes = []
        for hc in range(8):
            mt = p.op("tensor", lambda t, h=hc, a=xn32: t.transpose(
                out=tpf[:], in_=a[:, h * 128:(h + 1) * 128], identity=identf[:]),
                waits=[mark] + cst + rt_state["tpf_free"], inc="sp")
            me = p.op("vector", lambda v, x=xf32[hc]: v.tensor_copy(
                out=x[:], in_=tpf[:]), waits=[mt] + rt_state["xf_free"][hc], inc="sv")
            rt_state["tpf_free"] = [me]
            mes.append(me)
        for hc in range(8):
            mmr = p.op("tensor", lambda t, h=hc, x=xf32[hc], st=(hc == 0):
                       t.matmul(out=rps, lhsT=x[:], rhs=wr[h][:], start=st, stop=False),
                       waits=mes + (rt_state["rps_free"] if hc == 0 else []), inc="sp")
            rt_state["xf_free"][hc] = [mmr]
        mmb = p.op("tensor", lambda t: t.matmul(out=rps, lhsT=onesf[:], rhs=wrb[:],
                   start=False, stop=True), inc="sp")
        ml = p.op("vector", lambda v: v.tensor_copy(out=lgs[:], in_=rps),
                  waits=[mmb] + rt_state["lg_free"], inc="sv")
        rt_state["rps_free"] = [ml]
        m1 = p.op("vector", lambda v: v.reduce_max(out=rmax[:], in_=lgs[:], axis=AX.X), inc="sv")
        m2 = p.op("vector", lambda v: v.tensor_scalar(out=nbias[:], in0=rmax[:],
                  scalar1=-1.0, scalar2=None, op0=ALU.mult), inc="sv")
        m3 = p.op("scalar", lambda s: s.activation(out=lgs[:], in_=lgs[:], func=AF.Exp,
                  bias=nbias[:], accum_out=rsum[:]), waits=[m2], inc="sa")
        m4 = p.op("vector", lambda v: v.reciprocal(out=rsum[:], in_=rsum[:]),
                  waits=[m3], inc="sv")
        m5 = p.op("vector", lambda v, c=tc: v.tensor_scalar(
            out=scores_sb[:, c * 8:(c + 1) * 8], in0=lgs[:], scalar1=rsum[:],
            scalar2=None, op0=ALU.mult), waits=[m4], inc="sv")
        rt_state["lg_free"] = [m5]
        md = p.dma("gpsimd", sc_loc[tc * 128:(tc + 1) * 128, :],
                   scores_sb[:, tc * 8:(tc + 1) * 8], waits=[m5], inc="scD%d" % (tc & 1))
        rt_state["sc_marks"].append(md)
        return [m5]

    emit_ln(bc, "Dln", x_src, identb, tp_ps, xn_dram=xn_dram,
            fp32_cb=router_cb, wait0=list(wait0) + cst)
    sc_done = [p.mark("scD0"), p.mark("scD1")]
    xn_done = [p.mark("n0"), p.mark("n1")]

    mcc = p.op("gpsimd", lambda g: g.collective_compute(
        "AllGather", ALU.bypass, replica_groups=[list(range(8))],
        ins=[sc_loc.opt()], outs=[sc_full.opt()]), waits=sc_done, inc="cc")

    sfb = bc.sb("D_sfb", [128, 2048], F32)
    msf = p.dma("sync", sfb[:], sc_full.rearrange("(p t) e -> p (t e)", p=128),
                waits=[mcc], inc="sfD")

    tlo = bc.sb("D_tlo", [1, 8], F32)
    thi = bc.sb("D_thi", [1, 8], F32)
    tmid = bc.sb("D_tmid", [1, 8], F32)
    ge8 = bc.sb("D_ge8", [1, 8], U32)
    gei = bc.sb("D_gei", [1, 8], U32)
    cnt8 = bc.sb("D_cnt8", [128, 8], F32)
    cmpb = bc.sb("D_cmpb", [128, 256], F32)
    sfv = sfb[:].rearrange("p (t e) -> p t e", e=8)
    mi1 = p.op("vector", lambda v: v.memset(tlo[:], 0.0), inc="sv")
    mi2 = p.op("vector", lambda v: v.memset(thi[:], 2.0), inc="sv")
    mprev = [mi1, mi2, msf]
    cnt_free = []
    for it in range(42):
        ma = p.op("vector", lambda v: v.tensor_tensor(out=tmid[:], in0=tlo[:],
                  in1=thi[:], op=ALU.add), waits=mprev, inc="sv")
        mb = p.op("vector", lambda v: v.tensor_scalar(out=tmid[:], in0=tmid[:],
                  scalar1=0.5, scalar2=None, op0=ALU.mult), inc="sv")
        mt = p.op("tensor", lambda t: t.matmul(out=tbps, lhsT=onesf[:], rhs=tmid[:],
                  start=True, stop=True), waits=[mb] + cst + cnt_free, inc="sp")
        mcs = []
        for e in range(8):
            mcs.append(p.op("vector", lambda v, e_=e: v.tensor_scalar(
                out=cmpb[:], in0=sfv[:, :, e_:e_ + 1], scalar1=tbps[:, e_:e_ + 1],
                scalar2=0.0, op0=ALU.is_ge, op1=ALU.add,
                accum_out=cnt8[:, e_:e_ + 1]), waits=[mt, msf], inc="sv"))
        mrd = p.op("tensor", lambda t: t.matmul(out=redp[:, 0:8], lhsT=ones128[:],
                   rhs=cnt8[:], start=True, stop=True), waits=mcs + cst, inc="sp")
        cnt_free = [mrd]
        mg = p.op("vector", lambda v: v.tensor_scalar(out=ge8[:], in0=redp[0:1, 0:8],
                  scalar1=float(K1) - 0.5, scalar2=None, op0=ALU.is_ge),
                  waits=[mrd], inc="sv")
        ml1 = p.op("vector", lambda v: v.select(out=tlo[:], mask=ge8[:],
                   on_true=tmid[:], on_false=tlo[:], add_drain=True), inc="sv")
        mgi = p.op("vector", lambda v: v.tensor_scalar(out=gei[:], in0=ge8[:],
                   scalar1=0.5, scalar2=None, op0=ALU.is_lt), inc="sv")
        ml2 = p.op("vector", lambda v: v.select(out=thi[:], mask=gei[:],
                   on_true=tmid[:], on_false=thi[:], add_drain=True), inc="sv")
        mprev = [ml1, ml2]
    thr_done = mprev + [p.op("vector", lambda v: v.engine_nop(), waits=mprev, inc="sv")]

    toktab = bc.sb("D_toktab", [128, NTC], F32)
    bigt = bc.sb("D_big", [128, NTC], F32)
    me_ = bc.sb("D_me", [128, NTC], F32)
    minv = bc.sb("D_minv", [128, NTC], U32)
    csl = bc.sb("D_csl", [1, NTC], F32)
    pfa = bc.sb("D_pfa", [1, NTC], F32)
    pfb = bc.sb("D_pfb", [1, NTC], F32)
    rkb = bc.sb("D_rkb", [128, NTC], F32)
    slotu = bc.sb("D_slotu", [128, NTC], U32)
    pft = bc.sb("D_pft", [128, 2 * NTC], F32)
    psrc = [bc.sb("D_psrc%d" % i, [128, 2], F32) for i in range(2)]
    mtk = p.op("gpsimd", lambda g: g.iota(toktab[:], pattern=[[128, NTC]], base=0,
               channel_multiplier=1, allow_small_or_imprecise_dtypes=True), inc="gv")
    mbg = p.op("vector", lambda v: v.memset(bigt[:], 1.0e9), inc="sv")
    pfv = pft[:].rearrange("p (s t) -> p s t", t=2)
    mp1 = p.op("vector", lambda v: v.memset(pfv[:, :, 0:1], float(T)), inc="sv")
    mp2 = p.op("vector", lambda v: v.memset(pfv[:, :, 1:2], 0.0), inc="sv")
    mpf = p.dma("gpsimd", pairbuf.rearrange("(p s) t -> p s t", p=128), pft[:],
                waits=[mp1, mp2], inc="pfD")
    scv = scores_sb[:].rearrange("p (t e) -> p t e", e=8)
    scat_marks = [mpf]
    sl_free = []
    for e in range(8):
        mtb = p.op("tensor", lambda t, e_=e: t.matmul(out=tbps[:, 0:1], lhsT=onesf[:],
                   rhs=tlo[0:1, e_:e_ + 1], start=True, stop=True),
                   waits=thr_done + cnt_free, inc="sp")
        mm1 = p.op("vector", lambda v, e_=e: v.tensor_scalar(
            out=me_[:], in0=scv[:, :, e_:e_ + 1], scalar1=tbps[:, 0:1],
            scalar2=None, op0=ALU.is_ge), waits=[mtb] + sl_free, inc="sv")
        mcs1 = p.op("tensor", lambda t: t.matmul(out=redp[:, 8:40], lhsT=ones128[:],
                    rhs=me_[:], start=True, stop=True), waits=[mm1], inc="sp")
        mcv = p.op("vector", lambda v: v.tensor_copy(out=csl[:], in_=redp[0:1, 8:40]),
                   waits=[mcs1], inc="sv")
        pft_ = [pfa, pfb]
        cur = 0
        mpa = p.op("vector", lambda v, d=pft_[0]: v.memset(d[:, 0:1], 0.0), inc="sv")
        mpb = p.op("vector", lambda v, d=pft_[0]: v.tensor_copy(
            out=d[:, 1:NTC], in_=csl[:, 0:NTC - 1]), inc="sv")
        mst = None
        for k in (1, 2, 4, 8, 16):
            src, dst = pft_[cur], pft_[1 - cur]
            mst = p.op("vector", lambda v, k_=k, s_=src, d_=dst: v.tensor_copy(
                out=d_[:, 0:k_], in_=s_[:, 0:k_]), inc="sv")
            mst = p.op("vector", lambda v, k_=k, s_=src, d_=dst: v.tensor_tensor(
                out=d_[:, k_:NTC], in0=s_[:, k_:NTC], in1=s_[:, 0:NTC - k_],
                op=ALU.add), inc="sv")
            cur = 1 - cur
        mr1 = p.op("tensor", lambda t: t.matmul(out=rkp, lhsT=triu[:], rhs=me_[:],
                   start=True, stop=False), waits=[mm1, mst] + cst, inc="sp")
        mr2 = p.op("tensor", lambda t, a=pft_[cur]: t.matmul(out=rkp, lhsT=onesf[:], rhs=a[:],
                   start=False, stop=True), inc="sp")
        ms1 = p.op("vector", lambda v, e_=e: v.tensor_scalar(
            out=rkb[:], in0=rkp, scalar1=float(e_ * C1), scalar2=None, op0=ALU.add),
            waits=[mr2], inc="sv")
        msi = p.op("vector", lambda v: v.tensor_scalar(out=minv[:], in0=me_[:],
                   scalar1=0.5, scalar2=None, op0=ALU.is_lt), inc="sv")
        ms2 = p.op("vector", lambda v: v.select(out=rkb[:], mask=minv[:],
                   on_true=bigt[:], on_false=rkb[:], add_drain=True), waits=[mbg], inc="sv")
        msl = p.op("vector", lambda v: v.tensor_scalar(out=slotu[:], in0=rkb[:],
                   scalar1=0.0, scalar2=None, op0=ALU.add), inc="sv")
        sl_free = []
        for tc in range(NTC):
            par = tc & 1
            mc1 = p.op("vector", lambda v, c=tc, d=psrc[par]: v.tensor_copy(
                out=d[:, 0:1], in_=toktab[:, c:c + 1]),
                waits=[msl, mtk] + (scat_marks[-2:-1] if len(scat_marks) >= 2 else []), inc="sv")
            mc2 = p.op("vector", lambda v, c=tc, e_=e, d=psrc[par]: v.tensor_copy(
                out=d[:, 1:2], in_=scv[:, c:c + 1, e_:e_ + 1]), inc="sv")
            mdd = p.op("gpsimd", lambda g, c=tc, d=psrc[par]: g.indirect_dma_start(
                out=pairbuf, out_offset=IDX(slotu[:, c:c + 1], 0), in_=d[:],
                in_offset=None, bounds_check=bc.breg, oob_is_err=False),
                waits=[mc1, mc2, mpf, mreg] + scat_marks[-1:],
                inc="scpD", n=16, drain=True)
            scat_marks.append(mdd)
        sl_free = scat_marks[-2:]
    scat_all = [p.mark("scpD")]

    btT = [bc.sb("D_btT%d" % i, [128, 512], BF16) for i in range(8)]
    gbuf = [bc.sb("D_gbuf%d" % i, [128, DM], BF16) for i in range(4)]
    prs = [bc.sb("D_prs%d" % i, [128, 2], F32) for i in range(4)]
    tki = [bc.sb("D_tki%d" % i, [128, 1], U32) for i in range(4)]
    h1T = [bc.sb("D_h1T%d" % i, [128, 512], BF16) for i in range(32)]
    w1s = [bc.sb("D_w1s%d" % i, [128, 1024], BF16) for i in range(2)]
    w2s = [bc.sb("D_w2s%d" % i, [128, 1024], BF16) for i in range(32)]
    b1t = bc.sb("D_b1t", [128, 32], F32)
    b2r = bc.sb("D_b2r", [1, 1024], F32)
    ctb = [bc.sb("D_ctb%d" % i, [128, DM], F32) for i in range(2)]

    mz = [p.op("vector", lambda v, g=gbuf[j]: v.memset(g[:], 0.0), inc="sv")
          for j in range(4)]
    W1 = bc.ein("W1x")
    W2 = bc.ein("W2x")
    gb_free = [[] for _ in range(4)]
    bt_free = [[] for _ in range(8)]
    h1_free = [[] for _ in range(32)]
    w1_free = [[], []]
    h1ps_free = []
    eps_fr = [[], []]
    w2_free = [[] for _ in range(32)]
    b_free = []
    ctb_free = [[], []]
    sa_last = []
    prs_free = [[] for _ in range(4)]
    for e in range(8):
        mb1 = p.dma("sync", b1t[:], bc.ein("B1x")[e], waits=b_free, inc="b1D")
        mb2 = p.dma("sync", b2r[:], bc.ein("B2x")[e], waits=b_free, inc="b2D")
        gevs = []
        for j in range(4):
            mpr = p.dma("sync", prs[j][:],
                        pairbuf[e * C1 + j * 128:e * C1 + (j + 1) * 128, :],
                        waits=scat_all + prs_free[j], inc="prD%d" % j)
            mti = p.op("vector", lambda v, j_=j: v.tensor_scalar(out=tki[j_][:],
                       in0=prs[j_][:, 0:1], scalar1=0.0, scalar2=None,
                       op0=ALU.add), waits=[mpr], inc="sv")
            assert T - 1 == 8 * C1 - 1
            mga = p.op("gpsimd", lambda g, j_=j: g.indirect_dma_start(
                out=gbuf[j_][:], out_offset=None, in_=xn_dram,
                in_offset=IDX(tki[j_][:, 0:1], 0), bounds_check=bc.breg,
                oob_is_err=False), waits=[mti, mz[j], mreg] + xn_done + gb_free[j],
                inc="gaD%d" % j, n=16, drain=True)
            tpar = j & 1
            mts = []
            for hc in range(8):
                mts.append(p.op("tensor", lambda t, j_=j, h=hc, pp_=tp_ps[tpar]:
                           t.transpose(out=pp_[:, h * 128:(h + 1) * 128],
                                       in_=gbuf[j_][:, h * 128:(h + 1) * 128],
                                       identity=identb[:]),
                           waits=([mga] + bc.tp_war[tpar]) if hc == 0 else [mga],
                           inc="sp"))
            me2 = None
            for hc in range(8):
                me2 = p.op("vector", lambda v, j_=j, h=hc, pp_=tp_ps[tpar]: v.tensor_copy(
                    out=btT[h][:, j_ * 128:(j_ + 1) * 128],
                    in_=pp_[:, h * 128:(h + 1) * 128]),
                    waits=[mts[-1]] + bt_free[hc], inc="sv")
            bc.tp_war[tpar] = [me2]
            gevs.append(me2)
        for fc in range(32):
            par = fc & 1
            dram_ap = bass.AP(W1.tensor, W1.offset + e * (1024 * 4096) + fc * 128,
                              [[4096, 128], [128 * 4096, 8], [1, 128]])
            mw = p.dma("gpsimd", w1s[par][:].rearrange("p (c f) -> p c f", c=8),
                       dram_ap, waits=w1_free[par], inc="w1D%d" % par)
            mm = None
            for hc in range(8):
                mm = p.op("tensor", lambda t, h=hc, par_=par, st=(hc == 0), en=(hc == 7):
                          t.matmul(out=h1ps[:], lhsT=w1s[par_][:, h * 128:(h + 1) * 128],
                                   rhs=btT[h][:], start=st, stop=en),
                          waits=[mw] + gevs + ((h1_free[fc] + h1ps_free) if hc == 0 else []),
                          inc="sp")
            w1_free[par] = [mm]
            for hc in range(8):
                bt_free[hc] = [mm]
            mh = p.op("scalar", lambda s, f=fc: s.activation(
                out=h1T[f][:], in_=h1ps[:], func=AF.Relu, bias=b1t[:, f:f + 1]),
                waits=[mm, mb1], inc="sa")
            h1_free[fc] = [mh]
            h1ps_free = [mh]
        w2l = []
        for fc in range(32):
            w2l.append(p.dma("gpsimd", w2s[fc][:],
                             W2[e * 4096 + fc * 128:e * 4096 + (fc + 1) * 128, :] if False else
                             bass.AP(W2.tensor, W2.offset + e * (4096 * 1024) + fc * 128 * 1024,
                                     [[1024, 128], [1, 1024]]),
                             waits=w2_free[fc], inc="w2D"))
        w2_all = p.mark("w2D")
        for ts in range(4):
            cpar = ts & 1
            me0 = me1 = None
            for half, epp in ((0, eps0), (1, eps1)):
                mm = None
                for fc in range(32):
                    mm = p.op("tensor", lambda t, f=fc, ts_=ts, hf=half, ep=epp,
                              st=(fc == 0):
                              t.matmul(out=ep[:], lhsT=h1T[f][:, ts_ * 128:(ts_ + 1) * 128],
                                       rhs=w2s[f][:, hf * 512:(hf + 1) * 512],
                                       start=st, stop=False),
                              waits=(h1_free[fc] or [None]) + [w2_all] +
                                    ((ctb_free[cpar] + eps_fr[half]) if fc == 0 else []),
                              inc="sp")
                mm = p.op("tensor", lambda t, hf=half, ep=epp: t.matmul(
                    out=ep[:], lhsT=onesf[:], rhs=b2r[0:1, hf * 512:(hf + 1) * 512],
                    start=False, stop=True), waits=[mb2], inc="sp")
                if half == 0:
                    me0 = mm
                else:
                    me1 = mm
            ct0 = p.op("scalar", lambda s, ts_=ts, c=ctb[cpar]: s.activation(
                out=c[:, 0:512], in_=eps0[:], func=AF.Copy,
                scale=prs[ts_][:, 1:2]), waits=[me0, me1], inc="sa")
            ct1 = p.op("scalar", lambda s, ts_=ts, c=ctb[cpar]: s.activation(
                out=c[:, 512:1024], in_=eps1[:], func=AF.Copy,
                scale=prs[ts_][:, 1:2]), inc="sa")
            eps_fr = [[ct0], [ct1]]

            msa = p.op("gpsimd", lambda g, ts_=ts, c=ctb[cpar]: g.indirect_dma_start(
                out=xo_dram, out_offset=IDX(tki[ts_][:, 0:1], 0), in_=c[:],
                in_offset=None, compute_op=ALU.add),
                waits=list(wait0) + [ct0, ct1] + sa_last, inc="saD", n=16, drain=True)
            sa_last = [msa]
            ctb_free[cpar] = [msa]
            gb_free[ts] = [msa]
            prs_free[ts] = [msa]
        b_free = [mh, ct1]
        for fc in range(32):
            w2_free[fc] = [me1]
    p.phase_end()
    return [p.mark("saD")]



def build_program(phases):
    nc = bass.Bass("TRN2", target_bir_lowering=False)
    with ExitStack() as ctx:
        bc = BC(nc, ctx)
        p = bc.p
        x_in = nc.dram_tensor("x_in", [T, DM], F32, kind="ExternalInput").ap()
        out = nc.dram_tensor("out", [T, DM], F32, kind="ExternalOutput").ap()
        x_dram = nc.dram_tensor("x_work", [T + 1, DM], F32).ap()
        xnT_dram = nc.dram_tensor("xnT", [DM, T], BF16).ap()

        p.phase_begin()
        m0 = p.dma("sync", x_dram[0:T, :], x_in, inc="dma")
        p.phase_end()

        prev = [m0]
        for ph in phases:
            bc.scope = ExitStack()
            if ph == "A":
                prev = emit_A(bc, x_dram[0:T, :], x_dram, xnT_dram, wait0=prev)
            elif ph == "B":
                prev = emit_B(bc, x_dram[0:T, :], x_dram, wait0=prev)
            elif ph == "C":
                prev = emit_C(bc, x_dram[0:T, :], x_dram, xnT_dram, wait0=prev)
            elif ph == "D":
                prev = emit_D(bc, x_dram[0:T, :], x_dram, wait0=prev)
            else:
                raise ValueError(ph)
            bc.scope.close()
            bc.scope = None

        p.phase_begin()
        waits = [p.mark(sm) for sm in list(p.cnt)]
        fin = p.dma("sync", out, x_dram[0:T, :], waits=waits, inc="dma")
        p.op("sync", lambda e: e.nop(), waits=[fin])
        p.phase_end()
    return bc


_CACHE = {}


def _get_program(phases):
    key = tuple(phases)
    if key not in _CACHE:
        _CACHE[key] = build_program(phases)
    return _CACHE[key]


def run_phases(phases, x_full, host_w, host_c):
    bc = _get_program(phases)
    tables = dict(host_w)
    tables.update(host_c)
    in_maps = []
    for c in range(8):
        m = {"x_in": np.ascontiguousarray(x_full[c])}
        for name in bc.ext:
            m[name] = tables[name]
        in_maps.append(m)
    res = run_bass_kernel_spmd(bc.nc, in_maps, core_ids=list(range(8)))
    return np.stack([r["out"] for r in res.results], 0)


def kernel(**inp):
    host_w = host_weights(inp)
    host_c = host_consts()
    x = np.ascontiguousarray(inp["L0_attn_x"].astype(np.float32, copy=False))
    import os
    groups = os.environ.get("K_GROUPS", "ABCD")
    for group in groups.split(","):
        x = run_phases(list(group), x, host_w, host_c)
    return x


# version 04
# speedup vs baseline: 1.3763x; 1.0140x over previous
import numpy as np
from contextlib import ExitStack

import concourse.bass as bass
import concourse.mybir as mybir
from concourse.bass_utils import run_bass_kernel_spmd

F32 = mybir.dt.float32
BF16 = mybir.dt.bfloat16
U32 = mybir.dt.uint32
AF = mybir.ActivationFunctionType
ALU = mybir.AluOpType
AX = mybir.AxisListType
IDX = bass.IndirectOffsetOnAxis

T = 4096
NTC = T // 128
DM = 1024
NEG = -1.0e30
C1 = 512
C0 = 1408

ENGINES = ("sync", "gpsimd", "tensor", "vector", "scalar")


class Prog:

    def __init__(self, nc, ctx):
        self.nc = nc
        self.ctx = ctx
        self.sems = {}
        self.cnt = {}
        self.dma_like = set()
        self.q = None
        self.hwm = None

    def semh(self, name):
        if name not in self.sems:
            self.sems[name] = self.ctx.enter_context(self.nc.semaphore(name))
            self.cnt[name] = 0
        return self.sems[name]

    def phase_begin(self):
        self.q = {e: [] for e in ENGINES}
        self.hwm = {e: {} for e in ENGINES}

    def op(self, eng, fn, waits=(), inc=None, n=1, drain=True):
        wmax = {}
        for item in waits:
            if item is None:
                continue
            s, v = item[0], item[1]
            if s not in self.dma_like and len(item) > 2 and item[2] == eng:
                continue
            if v > wmax.get(s, -1):
                wmax[s] = v
        w = []
        for s, v in wmax.items():
            if self.hwm[eng].get(s, 0) >= v:
                continue
            self.hwm[eng][s] = v
            w.append((s, v))
        mark = None
        if inc is not None:
            self.semh(inc)
            self.cnt[inc] += n
            mark = (inc, self.cnt[inc], eng)
        do_drain = drain and eng in ("vector", "scalar", "gpsimd")
        self.q[eng].append((w, fn, inc, n, do_drain))
        return mark

    def dma(self, eng, out, in_, waits=(), inc="dma", **kw):
        self.dma_like.add(inc)
        return self.op(eng, lambda e, o=out, i=in_, k=dict(kw): e.dma_start(out=o, in_=i, **k),
                       waits=waits, inc=inc, n=16, drain=False)

    def mark(self, inc):
        self.semh(inc)
        return (inc, self.cnt[inc], None)

    def phase_end(self):
        nc, q, sems = self.nc, self.q, self.sems
        with nc.Block() as block:
            for eng in ENGINES:
                ops = q[eng]
                if not ops:
                    continue

                def body(ectx, ops=ops):
                    for (w, fn, inc, n, drain) in ops:
                        for (s, v) in w:
                            ectx.wait_ge(sems[s], v)
                        instr = fn(ectx)
                        if inc is not None:
                            instr.then_inc(sems[inc], n)
                        if drain:
                            ectx.drain()

                getattr(block, eng)(body)
        self.q = None



def host_consts():
    c = {}
    HEAD_DIM = 64
    inv_freq = 1.0 / (10000.0 ** (np.arange(0, HEAD_DIM, 2, dtype=np.float64) / HEAD_DIM))
    pos = np.arange(T, dtype=np.float64)
    freqs = np.outer(pos, inv_freq)
    emb = np.concatenate([freqs, freqs], -1)
    cosT = np.cos(emb).T.astype(np.float32)
    sinT = np.sin(emb).T.astype(np.float32)
    c["cosT"] = np.ascontiguousarray(np.concatenate([cosT, cosT], 0))
    c["sinT"] = np.ascontiguousarray(np.concatenate([sinT, sinT], 0))
    R = np.zeros((64, 64), np.float32)
    for i in range(32):
        R[2 * i, 2 * i + 1] = -1.0
        R[2 * i + 1, 2 * i] = 1.0
    R2T = np.zeros((128, 128), np.float32)
    R2T[0:64, 0:64] = R.T
    R2T[64:128, 64:128] = R.T
    c["R2T"] = R2T
    D = 512
    div = np.exp(np.arange(0, D, 2, dtype=np.float64) * (-np.log(10000.0) / D))
    ang = pos[:, None] * div[None, :]
    pe = np.stack([np.sin(ang), np.cos(ang)], axis=-1).reshape(T, D)
    c["peT"] = np.ascontiguousarray(pe.T.astype(np.float32))
    ii = np.arange(128)[:, None]
    jj = np.arange(128)[None, :]
    c["mask_causal"] = np.where(jj <= ii, 0.0, NEG).astype(np.float32)
    c["mask_upper"] = np.where(ii <= jj, 0.0, NEG).astype(np.float32)
    c["mask_glob"] = np.ascontiguousarray(
        np.broadcast_to(np.where(jj < 4, 0.0, NEG), (128, 128)).astype(np.float32))
    c["triuS"] = np.triu(np.ones((128, 128), np.float32), 1)
    c["ident"] = np.eye(128, dtype=np.float32)
    return c


def host_weights(inp):
    asc = np.ascontiguousarray
    w = {}
    w["WqkvT0"] = asc(inp["L0_attn_qkv_w"].T)
    w["WoT0"] = asc(inp["L0_attn_out_w"].T)
    w["WgT"] = asc(inp["L0_ffn_w_gate"].T)
    w["W1eT"] = asc(inp["L0_ffn_w1_e"].transpose(0, 2, 1))
    w["W3eT"] = asc(inp["L0_ffn_w3_e"].transpose(0, 2, 1))
    w["W2eT"] = asc(inp["L0_ffn_w2_e"].transpose(0, 2, 1))
    w["W1sT"] = asc(inp["L0_ffn_w1_s"].T)
    w["W3sT"] = asc(inp["L0_ffn_w3_s"].T)
    w["W2sT"] = asc(inp["L0_ffn_w2_s"].T)
    w["WqkvT1"] = asc(inp["L1_attn_Wqkv"].T)
    w["WoT1"] = asc(inp["L1_attn_Wo"].T)
    w["Wr"] = asc(inp["L1_ffn_router_w"])
    w["Wrb"] = asc(inp["L1_ffn_router_b"].reshape(1, 8))
    w["W1x"] = asc(inp["L1_ffn_w1"])
    w["B1x"] = asc(inp["L1_ffn_b1"].reshape(8, 32, 128).transpose(0, 2, 1))
    w["W2x"] = asc(inp["L1_ffn_w2"])
    w["B2x"] = asc(inp["L1_ffn_b2"].reshape(8, 1, 1024))
    return w


WSHAPES = {
    "WqkvT0": [1024, 3072], "WoT0": [1024, 1024], "WgT": [1024, 16],
    "W1eT": [16, 1024, 512], "W3eT": [16, 1024, 512], "W2eT": [16, 512, 1024],
    "W1sT": [1024, 1024], "W3sT": [1024, 1024], "W2sT": [1024, 1024],
    "WqkvT1": [1024, 12288], "WoT1": [4096, 1024],
    "Wr": [1024, 8], "Wrb": [1, 8], "W1x": [8, 1024, 4096], "B1x": [8, 128, 32],
    "W2x": [8, 4096, 1024], "B2x": [8, 1, 1024],
    "cosT": [128, T], "sinT": [128, T], "R2T": [128, 128], "peT": [512, T],
    "mask_causal": [128, 128], "mask_upper": [128, 128], "mask_glob": [128, 128],
    "triuS": [128, 128],
    "ident": [128, 128],
}


class BC:

    def __init__(self, nc, ctx):
        self.nc = nc
        self.ctx = ctx
        self.ext = {}
        self.p = Prog(nc, ctx)
        self.tp_war = [[], []]

    def ein(self, name):
        if name not in self.ext:
            self.ext[name] = self.nc.dram_tensor(name, WSHAPES[name], F32,
                                                 kind="ExternalInput").ap()
        return self.ext[name]

    def sb(self, name, shape, dt):
        scope = getattr(self, "scope", None) or self.ctx
        return scope.enter_context(self.nc.sbuf_tensor(name, shape, dt))

    def ps(self, name, shape, dt=F32):
        scope = getattr(self, "scope", None) or self.ctx
        return scope.enter_context(self.nc.psum_tensor(name, shape, dt))



def emit_ln(bc, pref, x_src, identb, tp_ps, xnT_dram=None, xn_dram=None,
            xnT_sb=None, fp32_cb=None, wait0=()):
    p = bc.p
    xbuf = [bc.sb(pref + "xb%d" % i, [128, DM], F32) for i in range(2)]
    xnb = [bc.sb(pref + "xnb%d" % i, [128, DM], BF16) for i in range(2)]
    xn32 = [bc.sb(pref + "xn32%d" % i, [128, DM], F32) for i in range(2)] \
        if fp32_cb is not None else None
    sq = bc.sb(pref + "sq", [128, DM], BF16)
    st = [bc.sb(pref + "st%d" % i, [128, 4], F32) for i in range(2)]
    stg = [bc.sb(pref + "stg%d" % i, [128, DM], BF16) for i in range(2)] \
        if xnT_dram is not None else None

    free = [list(wait0), list(wait0)]
    stg_free = [[], []]
    last = []
    for tc in range(NTC):
        par = tc & 1
        s = st[par]
        ld = p.dma("sync", xbuf[par][:], x_src[tc * 128:(tc + 1) * 128, :],
                   waits=free[par], inc="x%d" % par)
        p.op("vector", lambda v, a=xbuf[par], s=s: v.reduce_sum(
            out=s[:, 0:1], in_=a[:], axis=AX.X), waits=[ld], inc="sv")
        msq = p.op("scalar", lambda sc, a=xbuf[par], s=s: sc.activation(
            out=sq[:], in_=a[:], func=AF.Square, accum_out=s[:, 1:2]),
            waits=[ld], inc="sa")
        p.op("vector", lambda v, s=s: v.tensor_scalar(
            out=s[:, 2:3], in0=s[:, 0:1], scalar1=1.0 / DM, scalar2=None,
            op0=ALU.mult), inc="sv")
        p.op("vector", lambda v, s=s: v.tensor_scalar(
            out=s[:, 1:2], in0=s[:, 1:2], scalar1=1.0 / DM, scalar2=1e-5,
            op0=ALU.mult, op1=ALU.add), waits=[msq], inc="sv")
        p.op("vector", lambda v, s=s: v.tensor_tensor(
            out=s[:, 3:4], in0=s[:, 2:3], in1=s[:, 2:3], op=ALU.mult), inc="sv")
        p.op("vector", lambda v, s=s: v.tensor_tensor(
            out=s[:, 1:2], in0=s[:, 1:2], in1=s[:, 3:4], op=ALU.subtract), inc="sv")
        mv = p.op("vector", lambda v, s=s: v.reciprocal(
            out=s[:, 1:2], in_=s[:, 1:2]), inc="sv")
        mr = p.op("scalar", lambda sc, s=s: sc.activation(
            out=s[:, 3:4], in_=s[:, 1:2], func=AF.Sqrt), waits=[mv], inc="sa")
        mb = p.op("vector", lambda v, a=xbuf[par], o=xnb[par], s=s: v.tensor_scalar(
            out=o[:], in0=a[:], scalar1=s[:, 2:3], scalar2=s[:, 3:4],
            op0=ALU.subtract, op1=ALU.mult), waits=[mr], inc="sv")
        last_use = [mb]
        if fp32_cb is not None:
            m32 = p.op("vector", lambda v, a=xbuf[par], o=xn32[par], s=s: v.tensor_scalar(
                out=o[:], in0=a[:], scalar1=s[:, 2:3], scalar2=s[:, 3:4],
                op0=ALU.subtract, op1=ALU.mult), inc="sv")
            cbm = fp32_cb(tc, xn32[par], m32)
            last_use.extend(cbm)
        if xn_dram is not None:
            md = p.dma("gpsimd", xn_dram[tc * 128:(tc + 1) * 128, :], xnb[par][:],
                       waits=[mb], inc="n%d" % par)
            last_use.append(md)
            last.append(md)
        if xnT_sb is not None:
            tpar = tc & 1
            mts = []
            for fc in range(8):
                mts.append(p.op("tensor", lambda t, a=xnb[par], f=fc, pp=tp_ps[tpar]:
                           t.transpose(out=pp[:, f * 128:(f + 1) * 128],
                                       in_=a[:, f * 128:(f + 1) * 128],
                                       identity=identb[:]),
                           waits=([mb] + bc.tp_war[tpar]) if fc == 0 else [mb],
                           inc="sp"))
            me = None
            for fc in range(8):
                me = p.op("vector", lambda v, d=xnT_sb[fc], tcc=tc, f=fc, pp=tp_ps[tpar]:
                          v.tensor_copy(out=d[:, tcc * 128:(tcc + 1) * 128],
                                        in_=pp[:, f * 128:(f + 1) * 128]),
                          waits=[mts[-1]], inc="sv")
            bc.tp_war[tpar] = [me]
            last_use.append(me)
            last.append(me)
        if xnT_dram is not None:
            tpar = tc & 1
            mts = []
            for fc in range(8):
                mts.append(p.op("tensor", lambda t, a=xnb[par], f=fc, pp=tp_ps[tpar]:
                           t.transpose(out=pp[:, f * 128:(f + 1) * 128],
                                       in_=a[:, f * 128:(f + 1) * 128],
                                       identity=identb[:]),
                           waits=([mb] + stg_free[par] + bc.tp_war[tpar]) if fc == 0 else [mb],
                           inc="sp"))
            me = p.op("vector", lambda v, d=stg[par], pp=tp_ps[tpar]: v.tensor_copy(
                out=d[:], in_=pp[:]), waits=[mts[-1]], inc="sv")
            bc.tp_war[tpar] = [me]
            dst = xnT_dram.rearrange("(c p) t -> p c t", p=128)[
                :, :, tc * 128:(tc + 1) * 128]
            md = p.dma("gpsimd", dst, stg[par][:].rearrange("p (c f) -> p c f", c=8),
                       waits=[me], inc="s%d" % par)
            stg_free[par] = [md]
            last_use.append(md)
            last.append(md)
        free[par] = last_use
    return last



def emit_A(bc, x_src, xo_dram, xnT_dram, wait0=()):
    import os
    A_PAIRS = int(os.environ.get("A_PAIRS", "8"))
    A_SECT = os.environ.get("A_SECT", "all")
    p = bc.p
    SCALE = 0.125

    identb = bc.sb("A_identb", [128, 128], BF16)
    cosT = bc.sb("A_cosT", [128, T], F32)
    sinT = bc.sb("A_sinT", [128, T], F32)
    r2t = bc.sb("A_r2t", [128, 128], BF16)
    mcau = bc.sb("A_mcau", [128, 128], F32)
    mupp = bc.sb("A_mupp", [128, 128], F32)
    mglob = bc.sb("A_mglob", [128, 128], F32)
    wo0 = [bc.sb("A_wo0_%d" % i, [128, 1024], BF16) for i in range(8)]
    tp_ps = [bc.ps("A_tp%d" % i, [128, 1024], BF16) for i in range(2)]

    p.phase_begin()
    p.dma("gpsimd", identb[:], bc.ein("ident"), inc="cgA")
    LVL = int(os.environ.get("A_CONSTLVL", "4"))
    if LVL >= 1:
        p.dma("sync", cosT[:], bc.ein("cosT"), inc="cstA")
        p.dma("sync", sinT[:], bc.ein("sinT"), inc="cstA")
    if LVL >= 2:
        p.dma("sync", mcau[:], bc.ein("mask_causal"), inc="cstA")
        p.dma("sync", mupp[:], bc.ein("mask_upper"), inc="cstA")
        p.dma("sync", mglob[:], bc.ein("mask_glob"), inc="cstA")
    if LVL >= 3:
        p.dma("gpsimd", r2t[:], bc.ein("R2T"), inc="cgA")
    if LVL >= 4:
        for i in range(8):
            p.dma("gpsimd", wo0[i][:], bc.ein("WoT0")[i * 128:(i + 1) * 128, :],
                  inc="cgA")
    cst = [p.mark("cstA"), p.mark("cgA")]
    m_r2t = m_cos = m_sin = m_mc = m_mu = cst[0]
    m_wo = [cst[1]] * 8
    cst_w = cst

    emit_ln(bc, "Aln", x_src, identb, tp_ps, xnT_dram=xnT_dram,
            wait0=list(wait0) + cst_w)
    ln_done = [p.mark("s0"), p.mark("s1"), p.mark("sv"), p.mark("sp")]

    wq = [bc.sb("A_wq%d" % i, [128, 128], BF16) for i in range(8)]
    wk = [bc.sb("A_wk%d" % i, [128, 128], BF16) for i in range(8)]
    wv = [bc.sb("A_wv%d" % i, [128, 128], BF16) for i in range(8)]
    xnt_s = [bc.sb("A_xnt%d" % i, [128, 512], BF16) for i in range(2)]
    qraw = bc.sb("A_qraw", [128, T], BF16)
    kraw = bc.sb("A_kraw", [128, T], BF16)
    qfin = bc.sb("A_qfin", [128, T], BF16)
    kf01 = [bc.sb("A_kf0", [128, T], BF16), bc.sb("A_kf1", [128, T], BF16)]
    vT = bc.sb("A_vT", [128, T], BF16)
    vtok = bc.sb("A_vtok", [128, T], BF16)
    t1b = bc.sb("A_t1", [128, 512], F32)
    t2b = bc.sb("A_t2", [128, 512], F32)
    attnT = [bc.sb("A_attnT%d" % i, [128, T], BF16) for i in range(8)]
    prow = bc.sb("A_prow", [128, 512], BF16)
    ptt3 = bc.sb("A_ptt", [128, 384], BF16)
    rmax = bc.sb("A_rmax", [128, 1], F32)
    nbias = bc.sb("A_nbias", [128, 1], F32)
    rsum = bc.sb("A_rsum", [128, 1], F32)

    qps = bc.ps("A_qps", [128, 512])
    kps = bc.ps("A_kps", [128, 512])
    vps = bc.ps("A_vps", [128, 512])
    rps = bc.ps("A_rps", [128, 512])
    sps = bc.ps("A_sps", [128, 512])
    aps = bc.ps("A_aps", [128, 128])

    if A_SECT == "ln":
        p.phase_end()
        return []
    mz0 = p.op("vector", lambda v: v.memset(kf01[0][64:128, :], 0.0), inc="sv")
    mz1 = p.op("vector", lambda v: v.memset(kf01[1][0:64, :], 0.0), inc="sv")

    wload_free = []
    xnt_free = [[], []]
    ev_free = {"q": [], "k": [], "v": []}
    rope_free = []
    vtok_free = ln_done
    s_free = []
    exp_free = []
    pt_free = [[], [], []]
    ae_free = []
    pair_done = []

    for pp in range(A_PAIRS):
        qc, kc, vc = pp * 128, 1024 + pp * 128, 2048 + pp * 128
        wl = []
        for ic in range(8):
            wl.append(p.dma("gpsimd", wq[ic][:],
                            bc.ein("WqkvT0")[ic * 128:(ic + 1) * 128, qc:qc + 128],
                            waits=wload_free, inc="wlA"))
            wl.append(p.dma("gpsimd", wk[ic][:],
                            bc.ein("WqkvT0")[ic * 128:(ic + 1) * 128, kc:kc + 128],
                            inc="wlA"))
            wl.append(p.dma("gpsimd", wv[ic][:],
                            bc.ein("WqkvT0")[ic * 128:(ic + 1) * 128, vc:vc + 128],
                            inc="wlA"))
        wload_free = []
        mm_marks = []
        for tcb in range(8):
            sl = slice(tcb * 512, (tcb + 1) * 512)
            mq = mk_ = mv_ = None
            for ic in range(8):
                par = ic & 1
                lx = p.dma("sync", xnt_s[par][:],
                           xnT_dram[ic * 128:(ic + 1) * 128, sl],
                           waits=xnt_free[par] + ln_done, inc="xt%d" % par)
                st_ = (ic == 0)
                en_ = (ic == 7)
                mq = p.op("tensor", lambda t, w=wq[ic], x=xnt_s[par], s=st_, e=en_:
                          t.matmul(out=qps[:], lhsT=w[:], rhs=x[:], start=s, stop=e),
                          waits=[lx, wl[-1]] + (ev_free["q"] if ic == 0 else []),
                          inc="sp")
                mk_ = p.op("tensor", lambda t, w=wk[ic], x=xnt_s[par], s=st_, e=en_:
                           t.matmul(out=kps[:], lhsT=w[:], rhs=x[:], start=s, stop=e),
                           waits=[wl[-1]] + (ev_free["k"] if ic == 0 else []),
                           inc="sp")
                mv_ = p.op("tensor", lambda t, w=wv[ic], x=xnt_s[par], s=st_, e=en_:
                           t.matmul(out=vps[:], lhsT=w[:], rhs=x[:], start=s, stop=e),
                           waits=[wl[-1]] + (ev_free["v"] if ic == 0 else []),
                           inc="sp")
                xnt_free[par] = [mv_]
            eq = p.op("vector", lambda v, s=sl: v.tensor_copy(out=qraw[:, s], in_=qps[:]),
                      waits=[mq] + rope_free, inc="sv")
            ek = p.op("scalar", lambda sc, s=sl: sc.copy(out=kraw[:, s], in_=kps[:]),
                      waits=[mk_] + rope_free, inc="sa")
            ev = p.op("vector", lambda v, s=sl: v.tensor_copy(out=vT[:, s], in_=vps[:]),
                      waits=[mv_] + vtok_free, inc="sv")
            ev_free = {"q": [eq], "k": [ek], "v": [ev]}
            mm_marks.append((eq, ek, ev))
        rp_marks = []
        rps_free = []
        for tcb in range(8):
            sl = slice(tcb * 512, (tcb + 1) * 512)
            eq, ek, ev = mm_marks[tcb]
            for which, (raw, mk2) in enumerate(((qraw, eq), (kraw, ek))):
                mr = p.op("tensor", lambda t, r=raw, s=sl: t.matmul(
                    out=rps[:], lhsT=r2t[:], rhs=r[:, s], start=True, stop=True),
                    waits=[mk2] + cst_w + rps_free, inc="sp")
                m1 = p.op("gpsimd", lambda g, r=raw, s=sl: g.tensor_tensor(
                    out=t1b[:], in0=r[:, s], in1=cosT[:, s], op=ALU.mult),
                    waits=[mk2] + cst_w + rp_marks[-1:], inc="gv")
                m2 = p.op("vector", lambda v, s=sl: v.tensor_tensor(
                    out=t2b[:], in0=rps[:], in1=sinT[:, s], op=ALU.mult),
                    waits=[mr] + cst_w, inc="sv")
                rps_free = [m2]
                if which == 0:
                    m3 = p.op("vector", lambda v, s=sl: v.tensor_tensor(
                        out=qfin[:, s], in0=t1b[:], in1=t2b[:], op=ALU.add),
                        waits=[m1, m2], inc="sv")
                    rp_marks.append(m3)
                else:
                    p.op("vector", lambda v, s=sl: v.tensor_tensor(
                        out=kf01[0][0:64, s], in0=t1b[0:64, :], in1=t2b[0:64, :],
                        op=ALU.add), waits=[m1, m2, mz0], inc="sv")
                    m3 = p.op("vector", lambda v, s=sl: v.tensor_tensor(
                        out=kf01[1][64:128, s], in0=t1b[64:128, :], in1=t2b[64:128, :],
                        op=ALU.add), waits=[mz1], inc="sv")
                    rp_marks.append(m3)
        rope_free = rp_marks[-2:]
        rope_all = rp_marks[-1]
        if A_SECT == "qkv":
            rope_free = []
            pair_done.append(mm_marks[-1][0])
            continue
        vt_last = None
        for g in range(4):
            tpar = g & 1
            mts = []
            for j in range(8):
                kt = g * 8 + j
                ev = mm_marks[kt // 4][2]
                mts.append(p.op("tensor", lambda t, k=kt, j_=j, pp_=tp_ps[tpar]:
                           t.transpose(out=pp_[:, j_ * 128:(j_ + 1) * 128],
                                       in_=vT[:, k * 128:(k + 1) * 128],
                                       identity=identb[:]),
                           waits=([ev] + vtok_free + bc.tp_war[tpar]) if j == 0 else [ev],
                           inc="sp"))
            vt_last = p.op("vector", lambda v, g_=g, pp_=tp_ps[tpar]: v.tensor_copy(
                out=vtok[:, g_ * 1024:(g_ + 1) * 1024], in_=pp_[:]),
                waits=[mts[-1]], inc="sv")
            bc.tp_war[tpar] = [vt_last]
        v_all = vt_last
        if A_SECT in ("rope", "vt"):
            pair_done.append(vt_last)
            continue
        for qt in range(NTC):
            pieces = [(0, qt, mcau)]
            if qt >= 1:
                pieces.append((128, qt - 1, mupp if (qt % 2 == 0) else None))
            if qt >= 2:
                pieces.append((256, 0, mglob))
            width = 128 * len(pieces)
            for hh in range(2):
                kfh = kf01[hh]
                sm = []
                for (c0, kt, _msk) in pieces:
                    sm.append(p.op("tensor", lambda t, c=c0, k=kt, f=kfh, q=qt:
                              t.matmul(out=sps[:, c:c + 128],
                                       lhsT=qfin[:, q * 128:(q + 1) * 128],
                                       rhs=f[:, k * 128:(k + 1) * 128],
                                       start=True, stop=True),
                              waits=[rope_all] + s_free, inc="sp"))
                mk3 = sm[-1]
                if os.environ.get("A_DBG") and pp == 0 and qt == 0 and hh == 0:
                    dbgS = bc.nc.dram_tensor("dbgS", [128, 128], F32).ap()
                    dbgM = bc.nc.dram_tensor("dbgM", [128, 128], F32).ap()
                    dcp = p.op("vector", lambda v: v.tensor_copy(out=t1b[:, 0:128], in_=sps[:, 0:128]), waits=sm, inc="sv")
                    d1 = p.dma("gpsimd", dbgS, t1b[:, 0:128], waits=[dcp], inc="n1")
                    d2 = p.dma("gpsimd", dbgM, mcau[:], waits=cst_w, inc="n1")
                    sm = sm + [d1, d2, dcp]
                for i, (c0, kt, msk) in enumerate(pieces):
                    if msk is not None:
                        mk3 = p.op("vector", lambda v, c=c0, m=msk: v.tensor_tensor(
                            out=sps[:, c:c + 128], in0=sps[:, c:c + 128], in1=m[:],
                            op=ALU.add), waits=sm + cst_w, inc="sv")
                m1 = p.op("vector", lambda v, w=width: v.reduce_max(
                    out=rmax[:], in_=sps[:, 0:w], axis=AX.X), waits=[mk3] + sm, inc="sv")
                m2 = p.op("vector", lambda v: v.tensor_scalar(
                    out=nbias[:], in0=rmax[:], scalar1=-SCALE, scalar2=None,
                    op0=ALU.mult), inc="sv")
                m3 = p.op("scalar", lambda sc, w=width: sc.activation(
                    out=prow[:, 0:w], in_=sps[:, 0:w], func=AF.Exp, scale=SCALE,
                    bias=nbias[:], accum_out=rsum[:]), waits=[m2] + exp_free, inc="sa")
                m4 = p.op("vector", lambda v: v.reciprocal(out=rsum[:], in_=rsum[:]),
                          waits=[m3], inc="sv")
                m5 = p.op("vector", lambda v, w=width: v.tensor_scalar(
                    out=prow[:, 0:w], in0=prow[:, 0:w], scalar1=rsum[:], scalar2=None,
                    op0=ALU.mult), waits=[m4], inc="sv")
                s_free = [m5]
                exp_free = [m5]
                tpar = hh
                mts = []
                for i, (c0, kt, _msk) in enumerate(pieces):
                    mts.append(p.op("tensor", lambda t, c=c0, i_=i, pp_=tp_ps[tpar]:
                               t.transpose(out=pp_[:, i_ * 128:(i_ + 1) * 128],
                                           in_=prow[:, c:c + 128], identity=identb[:]),
                               waits=([m5] + pt_free[0] + bc.tp_war[tpar]) if i == 0 else [m5],
                               inc="sp"))
                me = p.op("vector", lambda v, w3=128 * len(pieces), pp_=tp_ps[tpar]:
                          v.tensor_copy(out=ptt3[:, 0:w3], in_=pp_[:, 0:w3]),
                          waits=[mts[-1]], inc="sv")
                bc.tp_war[tpar] = [me]
                pev = [me]
                am = None
                for j, (c0, kt, _msk) in enumerate(pieces):
                    am = p.op("tensor", lambda t, k=kt, i_=j,
                              s=(j == 0), e=(j == len(pieces) - 1):
                              t.matmul(out=aps[:],
                                       lhsT=vtok[:, k * 128:(k + 1) * 128],
                                       rhs=ptt3[:, i_ * 128:(i_ + 1) * 128], start=s, stop=e),
                              waits=pev + [v_all] + (ae_free if j == 0 else []),
                              inc="sp")
                pt_free[0] = [am]
                ae = p.op("vector", lambda v, d=attnT[pp], q=qt, b=hh * 64: v.tensor_copy(
                    out=d[b:b + 64, q * 128:(q + 1) * 128], in_=aps[b:b + 64, :]),
                    waits=[am], inc="sv")
                ae_free = [ae]
                if os.environ.get("A_DBG") and pp == 0 and qt == 0 and hh == 0:
                    dbgP = bc.nc.dram_tensor("dbgP", [128, 128], F32).ap()
                    dbgR = bc.nc.dram_tensor("dbgR", [128, 2], F32).ap()
                    d3 = p.dma("gpsimd", dbgP, prow[:, 0:128], waits=[ae], inc="n1")
                    d4 = p.dma("gpsimd", dbgR[:, 0:1], rmax[:], inc="n1", allow_slow_non_contiguous=True)
                    d5 = p.dma("gpsimd", dbgR[:, 1:2], rsum[:], inc="n1", allow_slow_non_contiguous=True)
                    ae_free = ae_free + [d3, d4, d5]
                    s_free = s_free + [d3, d4, d5]
                    exp_free = exp_free + [d3, d4, d5]
        pair_done.append(ae_free[0])
        vtok_free = [ae_free[0]]

    if A_SECT != "all" or A_PAIRS < 8:
        p.phase_end()
        return []
    ost = [bc.sb("A_ost0", [128, 1024], F32)]
    opsum = [(qps, kps), (vps, rps)]
    ev_last = [[], []]
    acc_marks = []
    for qt in range(NTC):
        par = qt & 1
        ps0, ps1 = opsum[par]
        mm0 = mm1 = None
        for pc in range(8):
            st_ = (pc == 0)
            en_ = (pc == 7)
            w0 = [pair_done[pc]] if qt == 0 else []
            mm0 = p.op("tensor", lambda t, pcc=pc, q=qt, s=st_, e=en_, d=ps0: t.matmul(
                out=d[:], lhsT=attnT[pcc][:, q * 128:(q + 1) * 128],
                rhs=wo0[pcc][:, 0:512], start=s, stop=e),
                waits=w0 + cst_w + (ev_last[par] if pc == 0 else []), inc="sp")
            mm1 = p.op("tensor", lambda t, pcc=pc, q=qt, s=st_, e=en_, d=ps1: t.matmul(
                out=d[:], lhsT=attnT[pcc][:, q * 128:(q + 1) * 128],
                rhs=wo0[pcc][:, 512:1024], start=s, stop=e), inc="sp")
        war = acc_marks[-1:]
        e0 = p.op("vector", lambda v, d=ost[0], sp_=ps0: v.tensor_copy(
            out=d[:, 0:512], in_=sp_[:]), waits=[mm1] + war, inc="sv")
        e1 = p.op("scalar", lambda s, d=ost[0], sp_=ps1: s.copy(
            out=d[:, 512:1024], in_=sp_[:]), waits=[mm1] + war, inc="sa")
        ev_last[par] = [e0, e1]
        md = p.dma("gpsimd", xo_dram[qt * 128:(qt + 1) * 128, :], ost[0][:],
                   waits=list(wait0) + [e0, e1], inc="ac%d" % par, accum_op=ALU.add)
        acc_marks.append(md)
    p.phase_end()
    return [p.mark("ac0"), p.mark("ac1")]




def emit_B(bc, x_src, xo_dram, wait0=()):
    p = bc.p
    identb = bc.sb("B_identb", [128, 128], BF16)
    tp_ps = [bc.ps("B_tp%d" % i, [128, 1024], BF16) for i in range(2)]
    xnT = [bc.sb("B_xnT%d" % i, [128, T], BF16) for i in range(8)]
    wg = [bc.sb("B_wg%d" % i, [128, 16], BF16) for i in range(8)]

    p.phase_begin()
    p.dma("gpsimd", identb[:], bc.ein("ident"), waits=list(wait0), inc="cgB")
    for i in range(8):
        p.dma("gpsimd", wg[i][:], bc.ein("WgT")[i * 128:(i + 1) * 128, :],
              waits=list(wait0), inc="cgB")
    cst = p.mark("cgB")

    emit_ln(bc, "Bln", x_src, identb, tp_ps, xnT_sb=xnT,
            wait0=list(wait0) + [cst])
    ln_done = [p.mark("sv"), p.mark("sp")]

    sc = bc.sb("B_sc", [128, 16], F32)
    ga = bc.sb("B_ga", [128, 4], F32)
    gb = bc.sb("B_gb", [128, 4], F32)
    gc = bc.sb("B_gc", [128, 4], F32)
    gd = bc.sb("B_gd", [128, 4], F32)
    ge = bc.sb("B_ge", [128, 4], F32)
    gf = bc.sb("B_gf", [128, 4], F32)
    top8 = bc.sb("B_top8", [128, 8], F32)
    msk = bc.sb("B_msk", [128, 16], F32)
    msk2 = bc.sb("B_msk2", [128, 16], F32)
    t16 = bc.sb("B_t16", [128, 16], F32)
    w1v = bc.sb("B_w1v", [128, 1], F32)
    comb = bc.sb("B_comb", [128, 16 * NTC], F32)
    lgps = bc.ps("B_lgps", [128, 16])

    TT = ALU
    gate_done = []
    lg_free = []
    for tc in range(NTC):
        tsl = slice(tc * 128, (tc + 1) * 128)
        mlg = None
        for hc in range(8):
            mlg = p.op("tensor", lambda t, h=hc, s=tsl, st=(hc == 0), en=(hc == 7):
                       t.matmul(out=lgps[:], lhsT=xnT[h][:, s], rhs=wg[h][:],
                                start=st, stop=en),
                       waits=ln_done + [cst] + (lg_free if hc == 0 else []), inc="sp")
        msig = p.op("scalar", lambda s: s.activation(out=sc[:], in_=lgps[:],
                    func=AF.Sigmoid), waits=[mlg] + gate_done[-1:], inc="sa")
        lg_free = [msig]
        scv = sc[:].rearrange("p (g e) -> p g e", e=4)
        ops = [
            lambda v: v.tensor_tensor(out=ga[:], in0=scv[:, :, 0:1], in1=scv[:, :, 1:2], op=TT.max),
            lambda v: v.tensor_tensor(out=gb[:], in0=scv[:, :, 0:1], in1=scv[:, :, 1:2], op=TT.min),
            lambda v: v.tensor_tensor(out=gc[:], in0=scv[:, :, 2:3], in1=scv[:, :, 3:4], op=TT.max),
            lambda v: v.tensor_tensor(out=gd[:], in0=scv[:, :, 2:3], in1=scv[:, :, 3:4], op=TT.min),
            lambda v: v.tensor_tensor(out=ge[:], in0=ga[:], in1=gc[:], op=TT.min),
            lambda v: v.tensor_tensor(out=gf[:], in0=gb[:], in1=gd[:], op=TT.max),
            lambda v: v.tensor_tensor(out=ga[:], in0=ga[:], in1=gc[:], op=TT.max),
            lambda v: v.tensor_tensor(out=gb[:], in0=ge[:], in1=gf[:], op=TT.max),
            lambda v: v.tensor_tensor(out=ga[:], in0=ga[:], in1=gb[:], op=TT.add),
            lambda v: v.reduce_max(out=w1v[:], in_=ga[:], axis=AX.X),
            lambda v: v.tensor_scalar(out=gc[:], in0=ga[:], scalar1=w1v[:], scalar2=None, op0=TT.is_equal),
            lambda v: v.memset(gd[:, 0:1], 0.0),
            lambda v: v.tensor_copy(out=gd[:, 1:4], in_=gc[:, 0:3]),
            lambda v: v.tensor_tensor(out=gd[:, 2:4], in0=gd[:, 2:4], in1=gd[:, 0:2], op=TT.add),
            lambda v: v.tensor_scalar(out=ge[:], in0=gd[:], scalar1=0.5, scalar2=None, op0=TT.is_lt),
            lambda v: v.tensor_tensor(out=gc[:], in0=gc[:], in1=ge[:], op=TT.mult),
            lambda v: v.tensor_scalar(out=ge[:], in0=gc[:], scalar1=NEG, scalar2=None, op0=TT.mult),
            lambda v: v.tensor_tensor(out=gd[:], in0=ga[:], in1=ge[:], op=TT.add),
            lambda v: v.reduce_max(out=w1v[:], in_=gd[:], axis=AX.X),
            lambda v: v.tensor_scalar(out=ge[:], in0=gd[:], scalar1=w1v[:], scalar2=None, op0=TT.is_equal),
            lambda v: v.memset(gf[:, 0:1], 0.0),
            lambda v: v.tensor_copy(out=gf[:, 1:4], in_=ge[:, 0:3]),
            lambda v: v.tensor_tensor(out=gf[:, 2:4], in0=gf[:, 2:4], in1=gf[:, 0:2], op=TT.add),
            lambda v: v.tensor_scalar(out=gf[:], in0=gf[:], scalar1=0.5, scalar2=None, op0=TT.is_lt),
            lambda v: v.tensor_tensor(out=ge[:], in0=ge[:], in1=gf[:], op=TT.mult),
            lambda v: v.tensor_tensor(out=ga[:], in0=gc[:], in1=ge[:], op=TT.add),
            lambda v: v.tensor_scalar(out=gb[:], in0=ga[:], scalar1=1.0, scalar2=-NEG, op0=TT.subtract, op1=TT.mult),
        ]
        m = None
        for i, f in enumerate(ops):
            m = p.op("vector", f, waits=[msig] if i == 0 else [], inc="sv")
        for g in range(4):
            m = p.op("vector", lambda v, g_=g: v.tensor_scalar(
                out=msk[:, 4 * g_:4 * g_ + 4], in0=sc[:, 4 * g_:4 * g_ + 4],
                scalar1=gb[:, g_:g_ + 1], scalar2=None, op0=TT.add), inc="sv")
        ops2 = [
            lambda v: v.max(out=top8[:], in_=msk[:]),
            lambda v: v.memset(top8[:, 4:8], NEG),
            lambda v: v.match_replace(out=msk2[:], in_to_replace=top8[:], in_values=msk[:], imm_value=NEG),
            lambda v: v.tensor_tensor(out=t16[:], in0=msk[:], in1=msk2[:], op=TT.is_gt),
            lambda v: v.reduce_sum(out=w1v[:], in_=top8[:, 0:4], axis=AX.X),
            lambda v: v.tensor_scalar(out=w1v[:], in0=w1v[:], scalar1=1e-6, scalar2=None, op0=TT.add),
            lambda v: v.reciprocal(out=w1v[:], in_=w1v[:]),
            lambda v: v.tensor_tensor(out=t16[:], in0=t16[:], in1=sc[:], op=TT.mult),
        ]
        for f in ops2:
            m = p.op("vector", f, inc="sv")
        m = p.op("vector", lambda v, c=tc: v.tensor_scalar(
            out=comb[:, c * 16:(c + 1) * 16], in0=t16[:], scalar1=w1v[:],
            scalar2=None, op0=TT.mult), inc="sv")
        gate_done.append(m)
    gate_all = gate_done[-1]

    w1t = [[bc.sb("B_w1t%d_%d" % (sl, i), [128, 1024], BF16) for i in range(8)] for sl in range(1)]
    w3t = [[bc.sb("B_w3t%d_%d" % (sl, i), [128, 1024], BF16) for i in range(8)] for sl in range(1)]
    w2t = [[bc.sb("B_w2t%d_%d" % (sl, i), [128, 1024], BF16) for i in range(8)] for sl in range(1)]
    gf32 = [bc.sb("B_gf32%d" % i, [128, 512], F32) for i in range(2)]
    uf32 = bc.sb("B_uf32", [128, 512], F32)
    guT = [bc.sb("B_guT%d" % i, [128, 512], BF16) for i in range(8)]
    yacc = [bc.sb("B_yacc%d" % i, [128, 1024], F32) for i in range(4)]
    etmp = bc.sb("B_etmp", [128, 1024], F32)
    gps2 = [bc.ps("B_gps%d" % i, [128, 512]) for i in range(2)]
    ups = bc.ps("B_ups", [128, 512])
    eps0 = bc.ps("B_eps0", [128, 512])
    eps1 = bc.ps("B_eps1", [128, 512])

    ln_all = p.mark("sv")
    groups = [("S",)] + [(2 * i, 2 * i + 1) for i in range(8)]
    slot_free = [[], []]
    NSLOT = 1
    silu_free = [[], []]
    gu_free = [[] for _ in range(8)]
    uev_free = []
    yacc_free = [[] for _ in range(4)]
    yacc_dma = [[] for _ in range(4)]
    etmp_free = []
    eps_free = [[], []]
    acc_n = 0
    for gi, grp in enumerate(groups):
        sl = gi % NSLOT
        wl = []
        if grp[0] == "S":
            for hc in range(8):
                wl.append(p.dma("gpsimd", w1t[sl][hc][:], bc.ein("W1sT")[hc * 128:(hc + 1) * 128, :], waits=slot_free[sl], inc="wlB"))
                wl.append(p.dma("gpsimd", w3t[sl][hc][:], bc.ein("W3sT")[hc * 128:(hc + 1) * 128, :], inc="wlB"))
                wl.append(p.dma("gpsimd", w2t[sl][hc][:], bc.ein("W2sT")[hc * 128:(hc + 1) * 128, :], inc="wlB"))
        else:
            for li, e in enumerate(grp):
                for hc in range(8):
                    wl.append(p.dma("gpsimd", w1t[sl][hc][:, li * 512:(li + 1) * 512], bc.ein("W1eT")[e, hc * 128:(hc + 1) * 128, :], waits=slot_free[sl], inc="wlB"))
                    wl.append(p.dma("gpsimd", w3t[sl][hc][:, li * 512:(li + 1) * 512], bc.ein("W3eT")[e, hc * 128:(hc + 1) * 128, :], inc="wlB"))
                for fc in range(4):
                    wl.append(p.dma("gpsimd", w2t[sl][li * 4 + fc][:], bc.ein("W2eT")[e, fc * 128:(fc + 1) * 128, :], inc="wlB"))
        wld = wl[-1]
        last_mm = []
        for tcb in range(8):
            tsl = slice(tcb * 512, (tcb + 1) * 512)
            for li in range(len(grp)):
                shared = grp[0] == "S"
                nfc = 8 if shared else 4
                fo = 0 if shared else li * 512
                w2b = 0 if shared else li * 4
                gu_marks = []
                for fc in range(nfc):
                    gpar = fc & 1
                    gp = gps2[gpar]
                    mg = mu = None
                    for hc in range(8):
                        st = (hc == 0)
                        en = (hc == 7)
                        mg = p.op("tensor", lambda t, h=hc, f=fc, o=fo, s=tsl, sl_=sl, st_=st, en_=en, g_=gp:
                                  t.matmul(out=g_[:], lhsT=w1t[sl_][h][:, o + f * 128:o + (f + 1) * 128],
                                           rhs=xnT[h][:, s], start=st_, stop=en_),
                                  waits=[wld, ln_all] + (silu_free[gpar] if hc == 0 else []), inc="sp")
                        mu = p.op("tensor", lambda t, h=hc, f=fc, o=fo, s=tsl, sl_=sl, st_=st, en_=en:
                                  t.matmul(out=ups[:], lhsT=w3t[sl_][h][:, o + f * 128:o + (f + 1) * 128],
                                           rhs=xnT[h][:, s], start=st_, stop=en_),
                                  waits=(uev_free if hc == 0 else []), inc="sp")
                    ms = p.op("scalar", lambda s, g_=gp, d=gf32[gpar]: s.activation(
                              out=d[:], in_=g_[:], func=AF.Silu), waits=[mg], inc="sa")
                    mu2 = p.op("vector", lambda v: v.tensor_copy(out=uf32[:], in_=ups[:]),
                               waits=[mu], inc="sv")
                    mgu = p.op("vector", lambda v, f=fc, d=gf32[gpar]: v.tensor_tensor(
                        out=guT[f][:], in0=d[:], in1=uf32[:], op=TT.mult),
                        waits=[ms, mu2] + gu_free[fc], inc="sv")
                    silu_free[gpar] = [mgu]
                    uev_free = [mgu]
                    gu_marks.append(mgu)
                    gu_free[fc] = []
                for ts in range(4):
                    me0 = me1 = None
                    for fc in range(nfc):
                        st = (fc == 0)
                        en = (fc == nfc - 1)
                        me0 = p.op("tensor", lambda t, f=fc, ts_=ts, b=w2b, sl_=sl, st_=st, en_=en:
                                   t.matmul(out=eps0[:], lhsT=guT[f][:, ts_ * 128:(ts_ + 1) * 128],
                                            rhs=w2t[sl_][b + f][:, 0:512], start=st_, stop=en_),
                                   waits=gu_marks + (eps_free[0] if fc == 0 else []), inc="sp")
                        me1 = p.op("tensor", lambda t, f=fc, ts_=ts, b=w2b, sl_=sl, st_=st, en_=en:
                                   t.matmul(out=eps1[:], lhsT=guT[f][:, ts_ * 128:(ts_ + 1) * 128],
                                            rhs=w2t[sl_][b + f][:, 512:1024], start=st_, stop=en_),
                                   waits=(eps_free[1] if fc == 0 else []), inc="sp")
                        gu_free[fc].append(me1)
                    tci = tcb * 4 + ts
                    if shared:
                        ev0 = p.op("vector", lambda v, y=yacc[ts]: v.tensor_copy(
                            out=y[:, 0:512], in_=eps0[:]), waits=[me1] + yacc_dma[ts], inc="sv")
                        ev1 = p.op("scalar", lambda s, y=yacc[ts]: s.copy(
                            out=y[:, 512:1024], in_=eps1[:]), waits=[me1] + yacc_dma[ts], inc="sa")
                    elif li == 0:
                        ev0 = p.op("scalar", lambda s, y=yacc[ts], t_=tci, e=grp[0]: s.activation(
                            out=y[:, 0:512], in_=eps0[:], func=AF.Copy,
                            scale=comb[:, t_ * 16 + e:t_ * 16 + e + 1]),
                            waits=[me1, gate_all] + yacc_dma[ts], inc="sa")
                        ev1 = p.op("scalar", lambda s, y=yacc[ts], t_=tci, e=grp[0]: s.activation(
                            out=y[:, 512:1024], in_=eps1[:], func=AF.Copy,
                            scale=comb[:, t_ * 16 + e:t_ * 16 + e + 1]), inc="sa")
                    else:
                        et0 = p.op("scalar", lambda s, t_=tci, e=grp[1]: s.activation(
                            out=etmp[:, 0:512], in_=eps0[:], func=AF.Copy,
                            scale=comb[:, t_ * 16 + e:t_ * 16 + e + 1]),
                            waits=[me1, gate_all] + etmp_free, inc="sa")
                        et1 = p.op("scalar", lambda s, t_=tci, e=grp[1]: s.activation(
                            out=etmp[:, 512:1024], in_=eps1[:], func=AF.Copy,
                            scale=comb[:, t_ * 16 + e:t_ * 16 + e + 1]), inc="sa")
                        ev0 = p.op("vector", lambda v, y=yacc[ts]: v.tensor_tensor(
                            out=y[:], in0=y[:], in1=etmp[:], op=TT.add),
                            waits=[et0, et1], inc="sv")
                        ev1 = ev0
                        etmp_free = [ev0]
                    eps_free = [[ev0], [ev1]]
                    yacc_free[ts] = [ev0, ev1]
                    if li == len(grp) - 1:
                        md = p.dma("gpsimd", xo_dram[tci * 128:(tci + 1) * 128, :],
                                   yacc[ts][:], waits=list(wait0) + [ev0, ev1],
                                   inc="acB%d" % ts, accum_op=ALU.add)
                        yacc_dma[ts] = [md]
                        acc_n += 1
            last_mm = [me1]
        slot_free[sl] = last_mm
    p.phase_end()
    return [p.mark("acB%d" % i) for i in range(4)]




def emit_C(bc, x_src, xo_dram, xnT_dram, wait0=()):
    p = bc.p
    SC = 1.0 / np.sqrt(512.0)
    identb = bc.sb("C_identb", [128, 128], BF16)
    mcau = bc.sb("C_mcau", [128, 128], F32)
    tp_ps = [bc.ps("C_tp%d" % i, [128, 1024], BF16) for i in range(2)]

    p.phase_begin()
    p.dma("gpsimd", identb[:], bc.ein("ident"), waits=list(wait0), inc="cgC")
    p.dma("sync", mcau[:], bc.ein("mask_causal"), waits=list(wait0), inc="cstC")
    cst = [p.mark("cstC"), p.mark("cgC")]

    emit_ln(bc, "Cln", x_src, identb, tp_ps, xnT_dram=xnT_dram,
            wait0=list(wait0) + cst)
    ln_done = [p.mark("s0"), p.mark("s1"), p.mark("sv"), p.mark("sp")]

    wq = [bc.sb("C_wq%d" % i, [128, 512], BF16) for i in range(8)]
    wk = [bc.sb("C_wk%d" % i, [128, 512], BF16) for i in range(8)]
    wv = [bc.sb("C_wv%d" % i, [128, 512], BF16) for i in range(8)]
    woc = [bc.sb("C_wo%d" % i, [128, 1024], BF16) for i in range(4)]
    xnt_f = [bc.sb("C_xnt%d" % i, [128, 512], BF16) for i in range(8)]
    qT = [bc.sb("C_qT%d" % i, [128, T], BF16) for i in range(4)]
    kT = [bc.sb("C_kT%d" % i, [128, T], BF16) for i in range(4)]
    vC = bc.sb("C_v", [128, 32 * 512], BF16)
    peb = [bc.sb("C_peb%d" % i, [128, 512], F32) for i in range(2)]
    srow = bc.sb("C_srow", [128, T], F32)
    prow = bc.sb("C_prow", [128, T], BF16)
    ptC = bc.sb("C_ptC", [128, T], BF16)
    atm = bc.sb("C_atm", [128, 512], BF16)
    attnT4 = bc.sb("C_attnT", [128, 512], BF16)
    ost = [bc.sb("C_ost%d" % i, [128, 1024], F32) for i in range(2)]
    rmax = bc.sb("C_rmax", [128, 1], F32)
    nbias = bc.sb("C_nbias", [128, 1], F32)
    rsum = bc.sb("C_rsum", [128, 1], F32)

    cps = [bc.ps("C_cps%d" % i, [128, 512]) for i in range(2)]
    sps = bc.ps("C_sps", [128, 512])
    ops = [bc.ps("C_ops%d" % i, [128, 512]) for i in range(2)]

    wl_free = []
    xnt_free = [[] for _ in range(8)]
    pe_free = [[], []]
    qkv_free = {}
    cps_free = [[], []]
    s_free = []
    exp_free = []
    pt_free = []
    atm_free = []
    at_free = [[] for _ in range(4)]
    ost_free = [[], []]
    ops_free = []
    acc_last = [[], []]
    head_prev_total = []
    mo_last = None
    for h in range(8):
        base = h * 1536
        wl = []
        for ic in range(8):
            wl.append(p.dma("gpsimd", wq[ic][:],
                            bc.ein("WqkvT1")[ic * 128:(ic + 1) * 128, base:base + 512],
                            waits=wl_free, inc="wlC"))
            wl.append(p.dma("gpsimd", wk[ic][:],
                            bc.ein("WqkvT1")[ic * 128:(ic + 1) * 128, base + 512:base + 1024],
                            inc="wlC"))
            wl.append(p.dma("gpsimd", wv[ic][:],
                            bc.ein("WqkvT1")[ic * 128:(ic + 1) * 128, base + 1024:base + 1536],
                            inc="wlC"))
        for dc in range(4):
            wl.append(p.dma("gpsimd", woc[dc][:],
                            bc.ein("WoT1")[h * 512 + dc * 128:h * 512 + (dc + 1) * 128, :],
                            inc="wlC"))
        wld = wl[-1]
        for tcb in range(8):
            tsl = slice(tcb * 512, (tcb + 1) * 512)
            lds = []
            for ic in range(8):
                lds.append(p.dma("sync", xnt_f[ic][:],
                                 xnT_dram[ic * 128:(ic + 1) * 128, tsl],
                                 waits=xnt_free[ic] + ln_done, inc="xtC%d" % ic))
            for which, (wt, dst) in enumerate(((wq, qT), (wk, kT))):
                for oc in range(4):
                    par = oc & 1
                    pl = p.dma("sync", peb[par][:],
                               bc.ein("peT")[oc * 128:(oc + 1) * 128, tsl],
                               waits=pe_free[par], inc="peC%d" % par)
                    mm = None
                    for ic in range(8):
                        mm = p.op("tensor", lambda t, i=ic, o=oc, s=tsl, w=wt, par_=par,
                                  st=(ic == 0), en=(ic == 7):
                                  t.matmul(out=cps[par_][:], lhsT=w[i][:, o * 128:(o + 1) * 128],
                                           rhs=xnt_f[i][:], start=st, stop=en),
                                  waits=[wld, lds[ic]] + (cps_free[par] if ic == 0 else []),
                                  inc="sp")
                        xnt_free[ic] = [mm]
                    ev = p.op("vector", lambda v, d=dst[oc], s=tsl, par_=par: v.tensor_tensor(
                        out=d[:, s], in0=cps[par_][:], in1=peb[par_][:], op=ALU.add),
                        waits=[mm, pl], inc="sv")
                    pe_free[par] = [ev]
                    cps_free[par] = [ev]
                    qkv_free[(which, oc)] = ev
            for j in range(4):
                par = j & 1
                mm = None
                for ic in range(8):
                    mm = p.op("tensor", lambda t, i=ic, j_=j, st=(ic == 0), en=(ic == 7), par_=par:
                              t.matmul(out=cps[par_][:], lhsT=xnt_f[i][:, j_ * 128:(j_ + 1) * 128],
                                       rhs=wv[i][:], start=st, stop=en),
                              waits=(cps_free[par] if ic == 0 else []), inc="sp")
                    xnt_free[ic] = [mm]
                tj = tcb * 4 + j
                ev = p.op("vector", lambda v, t_=tj, par_=par: v.tensor_copy(
                    out=vC[:, t_ * 512:(t_ + 1) * 512], in_=cps[par_][:]),
                    waits=[mm], inc="sv")
                cps_free[par] = [ev]
                qkv_free[("v", tj)] = ev
        qkv_all = [qkv_free[("v", 31)], qkv_free[(0, 3)], qkv_free[(1, 3)]]
        def emit_tail(st):
            nonlocal pt_free, atm_free, ops_free, mo_last
            qt = st["qt"]
            m5 = st["m5"]
            ptevs = []
            ngr = (qt + 1 + 7) // 8
            for g in range(ngr):
                tpar = g & 1
                lo = g * 8
                hi = min(qt + 1, lo + 8)
                mts = []
                for kt in range(lo, hi):
                    mts.append(p.op("tensor", lambda t, k=kt, j_=kt - lo, pp_=tp_ps[tpar]:
                               t.transpose(out=pp_[:, j_ * 128:(j_ + 1) * 128],
                                           in_=prow[:, k * 128:(k + 1) * 128],
                                           identity=identb[:]),
                               waits=([m5] + bc.tp_war[tpar]) if kt == lo else [m5],
                               inc="sp"))
                me = p.op("vector", lambda v, l=lo, h_=hi, pp_=tp_ps[tpar]: v.tensor_copy(
                    out=ptC[:, l * 128:h_ * 128], in_=pp_[:, 0:(h_ - l) * 128]),
                    waits=[mts[-1]], inc="sv")
                bc.tp_war[tpar] = [me]
                ptevs.append(me)
            pt_free = [ptevs[-1]]
            mm = None
            for kt in range(qt + 1):
                mm = p.op("tensor", lambda t, k=kt, st_=(kt == 0), en=(kt == qt):
                          t.matmul(out=cps[0][:], lhsT=ptC[:, k * 128:(k + 1) * 128],
                                   rhs=vC[:, k * 512:(k + 1) * 512], start=st_, stop=en),
                          waits=ptevs + [qkv_free[("v", 31)]] +
                                (cps_free[0] if kt == 0 else []), inc="sp")
            mev = p.op("vector", lambda v: v.tensor_copy(out=atm[:], in_=cps[0][:]),
                       waits=[mm] + atm_free, inc="sv")
            cps_free[0] = [mev]
            tpar = qt & 1
            mts2 = []
            for dc in range(4):
                mts2.append(p.op("tensor", lambda t, d=dc, pp_=tp_ps[tpar]: t.transpose(
                    out=pp_[:, d * 128:(d + 1) * 128], in_=atm[:, d * 128:(d + 1) * 128],
                    identity=identb[:]),
                    waits=([mev] + bc.tp_war[tpar]) if dc == 0 else [mev], inc="sp"))
            me = p.op("vector", lambda v, pp_=tp_ps[tpar]: v.tensor_copy(
                out=attnT4[:], in_=pp_[:, 0:512]), waits=[mts2[-1]] + at_free[0], inc="sv")
            bc.tp_war[tpar] = [me]
            aevs = [me]
            atm_free = [me]
            par = qt & 1
            mo = [None, None]
            for half in range(2):
                for dc in range(4):
                    mo[half] = p.op("tensor", lambda t, d=dc, hf=half, st_=(dc == 0), en=(dc == 3):
                                    t.matmul(out=ops[hf][:], lhsT=attnT4[:, d * 128:(d + 1) * 128],
                                             rhs=woc[d][:, hf * 512:(hf + 1) * 512],
                                             start=st_, stop=en),
                                    waits=aevs + [wld] + (ops_free if dc == 0 else []),
                                    inc="sp")
            at_free[0] = [mo[1]]
            e0 = p.op("vector", lambda v, d=ost[par]: v.tensor_copy(
                out=d[:, 0:512], in_=ops[0][:]), waits=[mo[0], mo[1]] + acc_last[par], inc="sv")
            e1 = p.op("scalar", lambda s, d=ost[par]: s.copy(
                out=d[:, 512:1024], in_=ops[1][:]), waits=[mo[0], mo[1]] + acc_last[par], inc="sa")
            ost_free[par] = [e0, e1]
            ops_free = [e0, e1]
            md = p.dma("gpsimd", xo_dram[qt * 128:(qt + 1) * 128, :], ost[par][:],
                       waits=list(wait0) + [e0, e1] + head_prev_total,
                       inc="acC%d" % par, accum_op=ALU.add)
            acc_last[par] = [md]
            mo_last = mo[1]

        pending = None
        for qt in range(NTC):
            kext = (qt + 1) * 128
            nkcb = (kext + 511) // 512
            sevs = []
            for kcb in range(nkcb):
                wdt = min(512, kext - kcb * 512)
                mm = None
                for dc in range(4):
                    mm = p.op("tensor", lambda t, d=dc, q=qt, k=kcb, w=wdt,
                              st=(dc == 0), en=(dc == 3):
                              t.matmul(out=sps[:, 0:w], lhsT=qT[d][:, q * 128:(q + 1) * 128],
                                       rhs=kT[d][:, k * 512:k * 512 + w], start=st, stop=en),
                              waits=qkv_all + s_free, inc="sp")
                ev = p.op("scalar", lambda s, k=kcb, w=wdt: s.copy(
                    out=srow[:, k * 512:k * 512 + w], in_=sps[:, 0:w]),
                    waits=[mm] + exp_free, inc="sa")
                s_free = [ev]
                sevs.append(ev)
            mmask = p.op("vector", lambda v, q=qt: v.tensor_tensor(
                out=srow[:, q * 128:(q + 1) * 128], in0=srow[:, q * 128:(q + 1) * 128],
                in1=mcau[:], op=ALU.add), waits=sevs + cst, inc="sv")
            if pending is not None:
                emit_tail(pending)
            m1 = p.op("vector", lambda v, w=kext: v.reduce_max(
                out=rmax[:], in_=srow[:, 0:w], axis=AX.X), waits=[mmask] + sevs, inc="sv")
            m2 = p.op("vector", lambda v: v.tensor_scalar(
                out=nbias[:], in0=rmax[:], scalar1=-SC, scalar2=None, op0=ALU.mult), inc="sv")
            m3 = p.op("scalar", lambda s, w=kext: s.activation(
                out=prow[:, 0:w], in_=srow[:, 0:w], func=AF.Exp, scale=SC,
                bias=nbias[:], accum_out=rsum[:]), waits=[m2] + pt_free, inc="sa")
            m4 = p.op("vector", lambda v: v.reciprocal(out=rsum[:], in_=rsum[:]),
                      waits=[m3], inc="sv")
            m5 = p.op("vector", lambda v, w=kext: v.tensor_scalar(
                out=prow[:, 0:w], in0=prow[:, 0:w], scalar1=rsum[:], scalar2=None,
                op0=ALU.mult), waits=[m4], inc="sv")
            exp_free = [m5]
            pending = {"qt": qt, "m5": m5}
        emit_tail(pending)
        pending = None
        wl_free = [mo_last]
        head_prev_total = [p.mark("acC0"), p.mark("acC1")]
    p.phase_end()
    return [p.mark("acC0"), p.mark("acC1")]




def emit_D(bc, x_src, xo_dram, wait0=()):
    p = bc.p
    nc = bc.nc
    K1 = 2048
    for _nm in ("saD", "scpD", "gaD0", "gaD1", "gaD2", "gaD3", "cc"):
        p.dma_like.add(_nm)

    xn_dram = nc.dram_tensor("D_xn", [T, DM], BF16).ap()
    sc_loc = nc.dram_tensor("D_scloc", [T, 8], F32).ap()
    sc_full = nc.dram_tensor("D_scfull", [8 * T, 8], F32).ap()
    pairbuf = nc.dram_tensor("D_pairs", [8 * C1, 2], F32).ap()

    identb = bc.sb("D_identb", [128, 128], BF16)
    identf = bc.sb("D_identf", [128, 128], F32)
    triu = bc.sb("D_triu", [128, 128], F32)
    wr = [bc.sb("D_wr%d" % i, [128, 8], F32) for i in range(8)]
    wrb = bc.sb("D_wrb", [1, 8], F32)
    onesf = bc.sb("D_onesf", [1, 128], F32)
    ones128 = bc.sb("D_ones128", [128, 1], F32)
    tp_ps = [bc.ps("D_tp%d" % i, [128, 1024], BF16) for i in range(2)]
    tpf = bc.ps("D_tpf", [128, 128], F32)
    psmall = bc.ps("D_psmall", [128, 512])
    h1ps = bc.ps("D_h1ps", [128, 512])
    eps0 = bc.ps("D_eps0", [128, 512])
    eps1 = bc.ps("D_eps1", [128, 512])
    rps = psmall[:, 0:8]
    tbps = psmall[:, 8:16]
    rkp = psmall[:, 16:48]
    redp = psmall[0:1, 48:88]

    xf32 = [bc.sb("D_xf32_%d" % i, [128, 128], F32) for i in range(8)]
    lgs = bc.sb("D_lgs", [128, 8], F32)
    rmax = bc.sb("D_rmax", [128, 1], F32)
    nbias = bc.sb("D_nbias", [128, 1], F32)
    rsum = bc.sb("D_rsum", [128, 1], F32)
    scores_sb = bc.sb("D_scores", [128, 8 * NTC], F32)

    p.phase_begin()
    p.dma("gpsimd", identb[:], bc.ein("ident"), waits=list(wait0), inc="cgD")
    p.dma("sync", identf[:], bc.ein("ident"), waits=list(wait0), inc="cstD")
    p.dma("sync", triu[:], bc.ein("triuS"), inc="cstD")
    for i in range(8):
        p.dma("sync", wr[i][:], bc.ein("Wr")[i * 128:(i + 1) * 128, :], inc="cstD")
    p.dma("sync", wrb[:], bc.ein("Wrb"), inc="cstD")
    mo1 = p.op("vector", lambda v: v.memset(onesf[:], 1.0), waits=list(wait0), inc="sv")
    mo2 = p.op("vector", lambda v: v.memset(ones128[:], 1.0), inc="sv")
    bc.breg = bc.ctx.enter_context(nc.gpsimd.register("D_bnd"))
    mreg = p.op("gpsimd", lambda g: g.reg_mov(bc.breg, 8 * C1 - 1), inc="gv")
    cst = [p.mark("cstD"), p.mark("cgD"), mo1, mo2]

    rt_state = {"tpf_free": [], "xf_free": [[] for _ in range(8)], "rps_free": [],
                "lg_free": [], "sc_marks": []}

    def router_cb(tc, xn32, mark):
        mes = []
        for hc in range(8):
            mt = p.op("tensor", lambda t, h=hc, a=xn32: t.transpose(
                out=tpf[:], in_=a[:, h * 128:(h + 1) * 128], identity=identf[:]),
                waits=[mark] + cst + rt_state["tpf_free"], inc="sp")
            me = p.op("vector", lambda v, x=xf32[hc]: v.tensor_copy(
                out=x[:], in_=tpf[:]), waits=[mt] + rt_state["xf_free"][hc], inc="sv")
            rt_state["tpf_free"] = [me]
            mes.append(me)
        for hc in range(8):
            mmr = p.op("tensor", lambda t, h=hc, x=xf32[hc], st=(hc == 0):
                       t.matmul(out=rps, lhsT=x[:], rhs=wr[h][:], start=st, stop=False),
                       waits=mes + (rt_state["rps_free"] if hc == 0 else []), inc="sp")
            rt_state["xf_free"][hc] = [mmr]
        mmb = p.op("tensor", lambda t: t.matmul(out=rps, lhsT=onesf[:], rhs=wrb[:],
                   start=False, stop=True), inc="sp")
        ml = p.op("vector", lambda v: v.tensor_copy(out=lgs[:], in_=rps),
                  waits=[mmb] + rt_state["lg_free"], inc="sv")
        rt_state["rps_free"] = [ml]
        m1 = p.op("vector", lambda v: v.reduce_max(out=rmax[:], in_=lgs[:], axis=AX.X), inc="sv")
        m2 = p.op("vector", lambda v: v.tensor_scalar(out=nbias[:], in0=rmax[:],
                  scalar1=-1.0, scalar2=None, op0=ALU.mult), inc="sv")
        m3 = p.op("scalar", lambda s: s.activation(out=lgs[:], in_=lgs[:], func=AF.Exp,
                  bias=nbias[:], accum_out=rsum[:]), waits=[m2], inc="sa")
        m4 = p.op("vector", lambda v: v.reciprocal(out=rsum[:], in_=rsum[:]),
                  waits=[m3], inc="sv")
        m5 = p.op("vector", lambda v, c=tc: v.tensor_scalar(
            out=scores_sb[:, c * 8:(c + 1) * 8], in0=lgs[:], scalar1=rsum[:],
            scalar2=None, op0=ALU.mult), waits=[m4], inc="sv")
        rt_state["lg_free"] = [m5]
        md = p.dma("gpsimd", sc_loc[tc * 128:(tc + 1) * 128, :],
                   scores_sb[:, tc * 8:(tc + 1) * 8], waits=[m5], inc="scD%d" % (tc & 1))
        rt_state["sc_marks"].append(md)
        return [m5]

    emit_ln(bc, "Dln", x_src, identb, tp_ps, xn_dram=xn_dram,
            fp32_cb=router_cb, wait0=list(wait0) + cst)
    sc_done = [p.mark("scD0"), p.mark("scD1")]
    xn_done = [p.mark("n0"), p.mark("n1")]

    mcc = p.op("gpsimd", lambda g: g.collective_compute(
        "AllGather", ALU.bypass, replica_groups=[list(range(8))],
        ins=[sc_loc.opt()], outs=[sc_full.opt()]), waits=sc_done, inc="cc")

    sfb = bc.sb("D_sfb", [128, 2048], F32)
    msf = p.dma("sync", sfb[:], sc_full.rearrange("(p t) e -> p (t e)", p=128),
                waits=[mcc], inc="sfD")

    tlo = bc.sb("D_tlo", [1, 8], F32)
    thi = bc.sb("D_thi", [1, 8], F32)
    tmid = bc.sb("D_tmid", [1, 8], F32)
    ge8 = bc.sb("D_ge8", [1, 8], U32)
    gei = bc.sb("D_gei", [1, 8], U32)
    cnt8 = bc.sb("D_cnt8", [128, 8], F32)
    cmpb = bc.sb("D_cmpb", [128, 256], F32)
    sfv = sfb[:].rearrange("p (t e) -> p t e", e=8)
    mi1 = p.op("vector", lambda v: v.memset(tlo[:], 0.0), inc="sv")
    mi2 = p.op("vector", lambda v: v.memset(thi[:], 2.0), inc="sv")
    mprev = [mi1, mi2, msf]
    cnt_free = []
    for it in range(42):
        ma = p.op("vector", lambda v: v.tensor_tensor(out=tmid[:], in0=tlo[:],
                  in1=thi[:], op=ALU.add), waits=mprev, inc="sv")
        mb = p.op("vector", lambda v: v.tensor_scalar(out=tmid[:], in0=tmid[:],
                  scalar1=0.5, scalar2=None, op0=ALU.mult), inc="sv")
        mt = p.op("tensor", lambda t: t.matmul(out=tbps, lhsT=onesf[:], rhs=tmid[:],
                  start=True, stop=True), waits=[mb] + cst + cnt_free, inc="sp")
        mcs = []
        for e in range(8):
            mcs.append(p.op("vector", lambda v, e_=e: v.tensor_scalar(
                out=cmpb[:], in0=sfv[:, :, e_:e_ + 1], scalar1=tbps[:, e_:e_ + 1],
                scalar2=0.0, op0=ALU.is_ge, op1=ALU.add,
                accum_out=cnt8[:, e_:e_ + 1]), waits=[mt, msf], inc="sv"))
        mrd = p.op("tensor", lambda t: t.matmul(out=redp[:, 0:8], lhsT=ones128[:],
                   rhs=cnt8[:], start=True, stop=True), waits=mcs + cst, inc="sp")
        cnt_free = [mrd]
        mg = p.op("vector", lambda v: v.tensor_scalar(out=ge8[:], in0=redp[0:1, 0:8],
                  scalar1=float(K1) - 0.5, scalar2=None, op0=ALU.is_ge),
                  waits=[mrd], inc="sv")
        ml1 = p.op("vector", lambda v: v.select(out=tlo[:], mask=ge8[:],
                   on_true=tmid[:], on_false=tlo[:], add_drain=True), inc="sv")
        mgi = p.op("vector", lambda v: v.tensor_scalar(out=gei[:], in0=ge8[:],
                   scalar1=0.5, scalar2=None, op0=ALU.is_lt), inc="sv")
        ml2 = p.op("vector", lambda v: v.select(out=thi[:], mask=gei[:],
                   on_true=tmid[:], on_false=thi[:], add_drain=True), inc="sv")
        mprev = [ml1, ml2]
    thr_done = mprev + [p.op("vector", lambda v: v.engine_nop(), waits=mprev, inc="sv")]

    toktab = bc.sb("D_toktab", [128, NTC], F32)
    bigt = bc.sb("D_big", [128, NTC], F32)
    me_ = bc.sb("D_me", [128, NTC], F32)
    minv = bc.sb("D_minv", [128, NTC], U32)
    csl = bc.sb("D_csl", [1, NTC], F32)
    pfa = bc.sb("D_pfa", [1, NTC], F32)
    pfb = bc.sb("D_pfb", [1, NTC], F32)
    rkb = bc.sb("D_rkb", [128, NTC], F32)
    slotu = bc.sb("D_slotu", [128, NTC], U32)
    pft = bc.sb("D_pft", [128, 2 * NTC], F32)
    psrc = [bc.sb("D_psrc%d" % i, [128, 2], F32) for i in range(2)]
    mtk = p.op("gpsimd", lambda g: g.iota(toktab[:], pattern=[[128, NTC]], base=0,
               channel_multiplier=1, allow_small_or_imprecise_dtypes=True), inc="gv")
    mbg = p.op("vector", lambda v: v.memset(bigt[:], 1.0e9), inc="sv")
    pfv = pft[:].rearrange("p (s t) -> p s t", t=2)
    mp1 = p.op("vector", lambda v: v.memset(pfv[:, :, 0:1], float(T)), inc="sv")
    mp2 = p.op("vector", lambda v: v.memset(pfv[:, :, 1:2], 0.0), inc="sv")
    mpf = p.dma("gpsimd", pairbuf.rearrange("(p s) t -> p s t", p=128), pft[:],
                waits=[mp1, mp2], inc="pfD")
    scv = scores_sb[:].rearrange("p (t e) -> p t e", e=8)
    scat_marks = [mpf]
    sl_free = []
    for e in range(8):
        mtb = p.op("tensor", lambda t, e_=e: t.matmul(out=tbps[:, 0:1], lhsT=onesf[:],
                   rhs=tlo[0:1, e_:e_ + 1], start=True, stop=True),
                   waits=thr_done + cnt_free, inc="sp")
        mm1 = p.op("vector", lambda v, e_=e: v.tensor_scalar(
            out=me_[:], in0=scv[:, :, e_:e_ + 1], scalar1=tbps[:, 0:1],
            scalar2=None, op0=ALU.is_ge), waits=[mtb] + sl_free, inc="sv")
        mcs1 = p.op("tensor", lambda t: t.matmul(out=redp[:, 8:40], lhsT=ones128[:],
                    rhs=me_[:], start=True, stop=True), waits=[mm1], inc="sp")
        mcv = p.op("vector", lambda v: v.tensor_copy(out=csl[:], in_=redp[0:1, 8:40]),
                   waits=[mcs1], inc="sv")
        pft_ = [pfa, pfb]
        cur = 0
        mpa = p.op("vector", lambda v, d=pft_[0]: v.memset(d[:, 0:1], 0.0), inc="sv")
        mpb = p.op("vector", lambda v, d=pft_[0]: v.tensor_copy(
            out=d[:, 1:NTC], in_=csl[:, 0:NTC - 1]), inc="sv")
        mst = None
        for k in (1, 2, 4, 8, 16):
            src, dst = pft_[cur], pft_[1 - cur]
            mst = p.op("vector", lambda v, k_=k, s_=src, d_=dst: v.tensor_copy(
                out=d_[:, 0:k_], in_=s_[:, 0:k_]), inc="sv")
            mst = p.op("vector", lambda v, k_=k, s_=src, d_=dst: v.tensor_tensor(
                out=d_[:, k_:NTC], in0=s_[:, k_:NTC], in1=s_[:, 0:NTC - k_],
                op=ALU.add), inc="sv")
            cur = 1 - cur
        mr1 = p.op("tensor", lambda t: t.matmul(out=rkp, lhsT=triu[:], rhs=me_[:],
                   start=True, stop=False), waits=[mm1, mst] + cst, inc="sp")
        mr2 = p.op("tensor", lambda t, a=pft_[cur]: t.matmul(out=rkp, lhsT=onesf[:], rhs=a[:],
                   start=False, stop=True), inc="sp")
        ms1 = p.op("vector", lambda v, e_=e: v.tensor_scalar(
            out=rkb[:], in0=rkp, scalar1=float(e_ * C1), scalar2=None, op0=ALU.add),
            waits=[mr2], inc="sv")
        msi = p.op("vector", lambda v: v.tensor_scalar(out=minv[:], in0=me_[:],
                   scalar1=0.5, scalar2=None, op0=ALU.is_lt), inc="sv")
        ms2 = p.op("vector", lambda v: v.select(out=rkb[:], mask=minv[:],
                   on_true=bigt[:], on_false=rkb[:], add_drain=True), waits=[mbg], inc="sv")
        msl = p.op("vector", lambda v: v.tensor_scalar(out=slotu[:], in0=rkb[:],
                   scalar1=0.0, scalar2=None, op0=ALU.add), inc="sv")
        sl_free = []
        for tc in range(NTC):
            par = tc & 1
            mc1 = p.op("vector", lambda v, c=tc, d=psrc[par]: v.tensor_copy(
                out=d[:, 0:1], in_=toktab[:, c:c + 1]),
                waits=[msl, mtk] + (scat_marks[-2:-1] if len(scat_marks) >= 2 else []), inc="sv")
            mc2 = p.op("vector", lambda v, c=tc, e_=e, d=psrc[par]: v.tensor_copy(
                out=d[:, 1:2], in_=scv[:, c:c + 1, e_:e_ + 1]), inc="sv")
            mdd = p.op("gpsimd", lambda g, c=tc, d=psrc[par]: g.indirect_dma_start(
                out=pairbuf, out_offset=IDX(slotu[:, c:c + 1], 0), in_=d[:],
                in_offset=None, bounds_check=bc.breg, oob_is_err=False),
                waits=[mc1, mc2, mpf, mreg] + scat_marks[-1:],
                inc="scpD", n=16, drain=True)
            scat_marks.append(mdd)
        sl_free = scat_marks[-2:]
    scat_all = [p.mark("scpD")]

    btT = [bc.sb("D_btT%d" % i, [128, 512], BF16) for i in range(8)]
    gbuf = [bc.sb("D_gbuf%d" % i, [128, DM], BF16) for i in range(4)]
    prs = [bc.sb("D_prs%d" % i, [128, 2], F32) for i in range(4)]
    tki = [bc.sb("D_tki%d" % i, [128, 1], U32) for i in range(4)]
    h1T = [bc.sb("D_h1T%d" % i, [128, 512], BF16) for i in range(32)]
    w1s = [bc.sb("D_w1s%d" % i, [128, 1024], BF16) for i in range(2)]
    w2s = [bc.sb("D_w2s%d" % i, [128, 1024], BF16) for i in range(32)]
    b1t = bc.sb("D_b1t", [128, 32], F32)
    b2r = bc.sb("D_b2r", [1, 1024], F32)
    ctb = [bc.sb("D_ctb%d" % i, [128, DM], F32) for i in range(2)]

    mz = [p.op("vector", lambda v, g=gbuf[j]: v.memset(g[:], 0.0), inc="sv")
          for j in range(4)]
    W1 = bc.ein("W1x")
    W2 = bc.ein("W2x")
    gb_free = [[] for _ in range(4)]
    bt_free = [[] for _ in range(8)]
    h1_free = [[] for _ in range(32)]
    w1_free = [[], []]
    h1ps_free = []
    eps_fr = [[], []]
    w2_free = [[] for _ in range(32)]
    b_free = []
    ctb_free = [[], []]
    sa_last = []
    prs_free = [[] for _ in range(4)]
    for e in range(8):
        mb1 = p.dma("sync", b1t[:], bc.ein("B1x")[e], waits=b_free, inc="b1D")
        mb2 = p.dma("sync", b2r[:], bc.ein("B2x")[e], waits=b_free, inc="b2D")
        gevs = []
        for j in range(4):
            mpr = p.dma("sync", prs[j][:],
                        pairbuf[e * C1 + j * 128:e * C1 + (j + 1) * 128, :],
                        waits=scat_all + prs_free[j], inc="prD%d" % j)
            mti = p.op("vector", lambda v, j_=j: v.tensor_scalar(out=tki[j_][:],
                       in0=prs[j_][:, 0:1], scalar1=0.0, scalar2=None,
                       op0=ALU.add), waits=[mpr], inc="sv")
            assert T - 1 == 8 * C1 - 1
            mga = p.op("gpsimd", lambda g, j_=j: g.indirect_dma_start(
                out=gbuf[j_][:], out_offset=None, in_=xn_dram,
                in_offset=IDX(tki[j_][:, 0:1], 0), bounds_check=bc.breg,
                oob_is_err=False), waits=[mti, mz[j], mreg] + xn_done + gb_free[j],
                inc="gaD%d" % j, n=16, drain=True)
            tpar = j & 1
            mts = []
            for hc in range(8):
                mts.append(p.op("tensor", lambda t, j_=j, h=hc, pp_=tp_ps[tpar]:
                           t.transpose(out=pp_[:, h * 128:(h + 1) * 128],
                                       in_=gbuf[j_][:, h * 128:(h + 1) * 128],
                                       identity=identb[:]),
                           waits=([mga] + bc.tp_war[tpar]) if hc == 0 else [mga],
                           inc="sp"))
            me2 = None
            for hc in range(8):
                me2 = p.op("vector", lambda v, j_=j, h=hc, pp_=tp_ps[tpar]: v.tensor_copy(
                    out=btT[h][:, j_ * 128:(j_ + 1) * 128],
                    in_=pp_[:, h * 128:(h + 1) * 128]),
                    waits=[mts[-1]] + bt_free[hc], inc="sv")
            bc.tp_war[tpar] = [me2]
            gevs.append(me2)
        for fc in range(32):
            par = fc & 1
            dram_ap = bass.AP(W1.tensor, W1.offset + e * (1024 * 4096) + fc * 128,
                              [[4096, 128], [128 * 4096, 8], [1, 128]])
            mw = p.dma("gpsimd", w1s[par][:].rearrange("p (c f) -> p c f", c=8),
                       dram_ap, waits=w1_free[par], inc="w1D%d" % par)
            mm = None
            for hc in range(8):
                mm = p.op("tensor", lambda t, h=hc, par_=par, st=(hc == 0), en=(hc == 7):
                          t.matmul(out=h1ps[:], lhsT=w1s[par_][:, h * 128:(h + 1) * 128],
                                   rhs=btT[h][:], start=st, stop=en),
                          waits=[mw] + gevs + ((h1_free[fc] + h1ps_free) if hc == 0 else []),
                          inc="sp")
            w1_free[par] = [mm]
            for hc in range(8):
                bt_free[hc] = [mm]
            mh = p.op("scalar", lambda s, f=fc: s.activation(
                out=h1T[f][:], in_=h1ps[:], func=AF.Relu, bias=b1t[:, f:f + 1]),
                waits=[mm, mb1], inc="sa")
            h1_free[fc] = [mh]
            h1ps_free = [mh]
        w2l = []
        for fc in range(32):
            w2l.append(p.dma("gpsimd", w2s[fc][:],
                             W2[e * 4096 + fc * 128:e * 4096 + (fc + 1) * 128, :] if False else
                             bass.AP(W2.tensor, W2.offset + e * (4096 * 1024) + fc * 128 * 1024,
                                     [[1024, 128], [1, 1024]]),
                             waits=w2_free[fc], inc="w2D"))
        w2_all = p.mark("w2D")
        for ts in range(4):
            cpar = ts & 1
            me0 = me1 = None
            for half, epp in ((0, eps0), (1, eps1)):
                mm = None
                for fc in range(32):
                    mm = p.op("tensor", lambda t, f=fc, ts_=ts, hf=half, ep=epp,
                              st=(fc == 0):
                              t.matmul(out=ep[:], lhsT=h1T[f][:, ts_ * 128:(ts_ + 1) * 128],
                                       rhs=w2s[f][:, hf * 512:(hf + 1) * 512],
                                       start=st, stop=False),
                              waits=(h1_free[fc] or [None]) + [w2_all] +
                                    ((ctb_free[cpar] + eps_fr[half]) if fc == 0 else []),
                              inc="sp")
                mm = p.op("tensor", lambda t, hf=half, ep=epp: t.matmul(
                    out=ep[:], lhsT=onesf[:], rhs=b2r[0:1, hf * 512:(hf + 1) * 512],
                    start=False, stop=True), waits=[mb2], inc="sp")
                if half == 0:
                    me0 = mm
                else:
                    me1 = mm
            ct0 = p.op("scalar", lambda s, ts_=ts, c=ctb[cpar]: s.activation(
                out=c[:, 0:512], in_=eps0[:], func=AF.Copy,
                scale=prs[ts_][:, 1:2]), waits=[me0, me1], inc="sa")
            ct1 = p.op("vector", lambda v, ts_=ts, c=ctb[cpar]: v.tensor_scalar(
                out=c[:, 512:1024], in0=eps1[:], scalar1=prs[ts_][:, 1:2],
                scalar2=None, op0=ALU.mult), waits=[me1], inc="sv")
            eps_fr = [[ct0], [ct1]]

            msa = p.op("gpsimd", lambda g, ts_=ts, c=ctb[cpar]: g.indirect_dma_start(
                out=xo_dram, out_offset=IDX(tki[ts_][:, 0:1], 0), in_=c[:],
                in_offset=None, compute_op=ALU.add),
                waits=list(wait0) + [ct0, ct1] + sa_last, inc="saD", n=16, drain=True)
            sa_last = [msa]
            ctb_free[cpar] = [msa]
            gb_free[ts] = [msa]
            prs_free[ts] = [msa]
        b_free = [mh, ct1]
        for fc in range(32):
            w2_free[fc] = [me1]
    p.phase_end()
    return [p.mark("saD")]



def build_program(phases):
    nc = bass.Bass("TRN2", target_bir_lowering=False)
    with ExitStack() as ctx:
        bc = BC(nc, ctx)
        p = bc.p
        x_in = nc.dram_tensor("x_in", [T, DM], F32, kind="ExternalInput").ap()
        out = nc.dram_tensor("out", [T, DM], F32, kind="ExternalOutput").ap()
        x_dram = nc.dram_tensor("x_work", [T + 1, DM], F32).ap()
        xnT_dram = nc.dram_tensor("xnT", [DM, T], BF16).ap()

        p.phase_begin()
        m0 = p.dma("sync", x_dram[0:T, :], x_in, inc="dma")
        p.phase_end()

        prev = [m0]
        for ph in phases:
            bc.scope = ExitStack()
            if ph == "A":
                prev = emit_A(bc, x_dram[0:T, :], x_dram, xnT_dram, wait0=prev)
            elif ph == "B":
                prev = emit_B(bc, x_dram[0:T, :], x_dram, wait0=prev)
            elif ph == "C":
                prev = emit_C(bc, x_dram[0:T, :], x_dram, xnT_dram, wait0=prev)
            elif ph == "D":
                prev = emit_D(bc, x_dram[0:T, :], x_dram, wait0=prev)
            else:
                raise ValueError(ph)
            bc.scope.close()
            bc.scope = None

        p.phase_begin()
        waits = [p.mark(sm) for sm in list(p.cnt)]
        fin = p.dma("sync", out, x_dram[0:T, :], waits=waits, inc="dma")
        p.op("sync", lambda e: e.nop(), waits=[fin])
        p.phase_end()
    return bc


_CACHE = {}


def _get_program(phases):
    key = tuple(phases)
    if key not in _CACHE:
        _CACHE[key] = build_program(phases)
    return _CACHE[key]


def run_phases(phases, x_full, host_w, host_c):
    bc = _get_program(phases)
    tables = dict(host_w)
    tables.update(host_c)
    in_maps = []
    for c in range(8):
        m = {"x_in": np.ascontiguousarray(x_full[c])}
        for name in bc.ext:
            m[name] = tables[name]
        in_maps.append(m)
    res = run_bass_kernel_spmd(bc.nc, in_maps, core_ids=list(range(8)))
    return np.stack([r["out"] for r in res.results], 0)


def kernel(**inp):
    host_w = host_weights(inp)
    host_c = host_consts()
    x = np.ascontiguousarray(inp["L0_attn_x"].astype(np.float32, copy=False))
    import os
    groups = os.environ.get("K_GROUPS", "ABCD")
    for group in groups.split(","):
        x = run_phases(list(group), x, host_w, host_c)
    return x


# version 05 (final)
# speedup vs baseline: 1.4490x; 1.0528x over previous
import numpy as np
from contextlib import ExitStack

import concourse.bass as bass
import concourse.mybir as mybir
from concourse.bass_utils import run_bass_kernel_spmd

F32 = mybir.dt.float32
BF16 = mybir.dt.bfloat16
U32 = mybir.dt.uint32
AF = mybir.ActivationFunctionType
ALU = mybir.AluOpType
AX = mybir.AxisListType
IDX = bass.IndirectOffsetOnAxis

T = 4096
NTC = T // 128
DM = 1024
NEG = -1.0e30
C1 = 512
C0 = 1408

ENGINES = ("sync", "gpsimd", "tensor", "vector", "scalar")


class Prog:

    def __init__(self, nc, ctx):
        self.nc = nc
        self.ctx = ctx
        self.sems = {}
        self.cnt = {}
        self.dma_like = set()
        self.q = None
        self.hwm = None

    def semh(self, name):
        if name not in self.sems:
            self.sems[name] = self.ctx.enter_context(self.nc.semaphore(name))
            self.cnt[name] = 0
        return self.sems[name]

    def phase_begin(self):
        self.q = {e: [] for e in ENGINES}
        self.hwm = {e: {} for e in ENGINES}

    def op(self, eng, fn, waits=(), inc=None, n=1, drain=True):
        wmax = {}
        for item in waits:
            if item is None:
                continue
            s, v = item[0], item[1]
            if s not in self.dma_like and len(item) > 2 and item[2] == eng:
                continue
            if v > wmax.get(s, -1):
                wmax[s] = v
        w = []
        for s, v in wmax.items():
            if self.hwm[eng].get(s, 0) >= v:
                continue
            self.hwm[eng][s] = v
            w.append((s, v))
        mark = None
        if inc is not None:
            self.semh(inc)
            self.cnt[inc] += n
            mark = (inc, self.cnt[inc], eng)
        do_drain = drain and eng in ("vector", "scalar", "gpsimd")
        self.q[eng].append((w, fn, inc, n, do_drain))
        return mark

    def dma(self, eng, out, in_, waits=(), inc="dma", **kw):
        self.dma_like.add(inc)
        return self.op(eng, lambda e, o=out, i=in_, k=dict(kw): e.dma_start(out=o, in_=i, **k),
                       waits=waits, inc=inc, n=16, drain=False)

    def mark(self, inc):
        self.semh(inc)
        return (inc, self.cnt[inc], None)

    def phase_end(self):
        nc, q, sems = self.nc, self.q, self.sems
        with nc.Block() as block:
            for eng in ENGINES:
                ops = q[eng]
                if not ops:
                    continue

                def body(ectx, ops=ops):
                    for (w, fn, inc, n, drain) in ops:
                        for (s, v) in w:
                            ectx.wait_ge(sems[s], v)
                        instr = fn(ectx)
                        if inc is not None:
                            instr.then_inc(sems[inc], n)
                        if drain:
                            ectx.drain()

                getattr(block, eng)(body)
        self.q = None



def host_consts():
    c = {}
    HEAD_DIM = 64
    inv_freq = 1.0 / (10000.0 ** (np.arange(0, HEAD_DIM, 2, dtype=np.float64) / HEAD_DIM))
    pos = np.arange(T, dtype=np.float64)
    freqs = np.outer(pos, inv_freq)
    emb = np.concatenate([freqs, freqs], -1)
    cosT = np.cos(emb).T.astype(np.float32)
    sinT = np.sin(emb).T.astype(np.float32)
    c["cosT"] = np.ascontiguousarray(np.concatenate([cosT, cosT], 0))
    c["sinT"] = np.ascontiguousarray(np.concatenate([sinT, sinT], 0))
    R = np.zeros((64, 64), np.float32)
    for i in range(32):
        R[2 * i, 2 * i + 1] = -1.0
        R[2 * i + 1, 2 * i] = 1.0
    R2T = np.zeros((128, 128), np.float32)
    R2T[0:64, 0:64] = R.T
    R2T[64:128, 64:128] = R.T
    c["R2T"] = R2T
    D = 512
    div = np.exp(np.arange(0, D, 2, dtype=np.float64) * (-np.log(10000.0) / D))
    ang = pos[:, None] * div[None, :]
    pe = np.stack([np.sin(ang), np.cos(ang)], axis=-1).reshape(T, D)
    c["peT"] = np.ascontiguousarray(pe.T.astype(np.float32))
    ii = np.arange(128)[:, None]
    jj = np.arange(128)[None, :]
    c["mask_causal"] = np.where(jj <= ii, 0.0, NEG).astype(np.float32)
    c["mask_upper"] = np.where(ii <= jj, 0.0, NEG).astype(np.float32)
    c["mask_glob"] = np.ascontiguousarray(
        np.broadcast_to(np.where(jj < 4, 0.0, NEG), (128, 128)).astype(np.float32))
    c["triuS"] = np.triu(np.ones((128, 128), np.float32), 1)
    c["ident"] = np.eye(128, dtype=np.float32)
    return c


def host_weights(inp):
    asc = np.ascontiguousarray
    w = {}
    w["WqkvT0"] = asc(inp["L0_attn_qkv_w"].T)
    w["WoT0"] = asc(inp["L0_attn_out_w"].T)
    w["WgT"] = asc(inp["L0_ffn_w_gate"].T)
    w["W1eT"] = asc(inp["L0_ffn_w1_e"].transpose(0, 2, 1))
    w["W3eT"] = asc(inp["L0_ffn_w3_e"].transpose(0, 2, 1))
    w["W2eT"] = asc(inp["L0_ffn_w2_e"].transpose(0, 2, 1))
    w["W1sT"] = asc(inp["L0_ffn_w1_s"].T)
    w["W3sT"] = asc(inp["L0_ffn_w3_s"].T)
    w["W2sT"] = asc(inp["L0_ffn_w2_s"].T)
    w["WqkvT1"] = asc(inp["L1_attn_Wqkv"].T)
    w["WoT1"] = asc(inp["L1_attn_Wo"].T)
    w["Wr"] = asc(inp["L1_ffn_router_w"])
    w["Wrb"] = asc(inp["L1_ffn_router_b"].reshape(1, 8))
    w["W1x"] = asc(inp["L1_ffn_w1"])
    w["B1x"] = asc(inp["L1_ffn_b1"].reshape(8, 32, 128).transpose(0, 2, 1))
    w["W2x"] = asc(inp["L1_ffn_w2"])
    w["B2x"] = asc(inp["L1_ffn_b2"].reshape(8, 1, 1024))
    return w


WSHAPES = {
    "WqkvT0": [1024, 3072], "WoT0": [1024, 1024], "WgT": [1024, 16],
    "W1eT": [16, 1024, 512], "W3eT": [16, 1024, 512], "W2eT": [16, 512, 1024],
    "W1sT": [1024, 1024], "W3sT": [1024, 1024], "W2sT": [1024, 1024],
    "WqkvT1": [1024, 12288], "WoT1": [4096, 1024],
    "Wr": [1024, 8], "Wrb": [1, 8], "W1x": [8, 1024, 4096], "B1x": [8, 128, 32],
    "W2x": [8, 4096, 1024], "B2x": [8, 1, 1024],
    "cosT": [128, T], "sinT": [128, T], "R2T": [128, 128], "peT": [512, T],
    "mask_causal": [128, 128], "mask_upper": [128, 128], "mask_glob": [128, 128],
    "triuS": [128, 128],
    "ident": [128, 128],
}


class BC:

    def __init__(self, nc, ctx):
        self.nc = nc
        self.ctx = ctx
        self.ext = {}
        self.p = Prog(nc, ctx)
        self.tp_war = [[], []]

    def ein(self, name):
        if name not in self.ext:
            self.ext[name] = self.nc.dram_tensor(name, WSHAPES[name], F32,
                                                 kind="ExternalInput").ap()
        return self.ext[name]

    def sb(self, name, shape, dt):
        scope = getattr(self, "scope", None) or self.ctx
        return scope.enter_context(self.nc.sbuf_tensor(name, shape, dt))

    def ps(self, name, shape, dt=F32):
        scope = getattr(self, "scope", None) or self.ctx
        return scope.enter_context(self.nc.psum_tensor(name, shape, dt))



def emit_ln(bc, pref, x_src, identb, tp_ps, xnT_dram=None, xn_dram=None,
            xnT_sb=None, fp32_cb=None, wait0=()):
    p = bc.p
    xbuf = [bc.sb(pref + "xb%d" % i, [128, DM], F32) for i in range(2)]
    xnb = [bc.sb(pref + "xnb%d" % i, [128, DM], BF16) for i in range(2)]
    xn32 = [bc.sb(pref + "xn32%d" % i, [128, DM], F32) for i in range(2)] \
        if fp32_cb is not None else None
    sq = bc.sb(pref + "sq", [128, DM], BF16)
    st = [bc.sb(pref + "st%d" % i, [128, 4], F32) for i in range(2)]
    stg = [bc.sb(pref + "stg%d" % i, [128, DM], BF16) for i in range(2)] \
        if xnT_dram is not None else None

    free = [list(wait0), list(wait0)]
    stg_free = [[], []]
    last = []
    for tc in range(NTC):
        par = tc & 1
        s = st[par]
        ld = p.dma("sync", xbuf[par][:], x_src[tc * 128:(tc + 1) * 128, :],
                   waits=free[par], inc="x%d" % par)
        p.op("vector", lambda v, a=xbuf[par], s=s: v.reduce_sum(
            out=s[:, 0:1], in_=a[:], axis=AX.X), waits=[ld], inc="sv")
        msq = p.op("scalar", lambda sc, a=xbuf[par], s=s: sc.activation(
            out=sq[:], in_=a[:], func=AF.Square, accum_out=s[:, 1:2]),
            waits=[ld], inc="sa")
        p.op("vector", lambda v, s=s: v.tensor_scalar(
            out=s[:, 2:3], in0=s[:, 0:1], scalar1=1.0 / DM, scalar2=None,
            op0=ALU.mult), inc="sv")
        p.op("vector", lambda v, s=s: v.tensor_scalar(
            out=s[:, 1:2], in0=s[:, 1:2], scalar1=1.0 / DM, scalar2=1e-5,
            op0=ALU.mult, op1=ALU.add), waits=[msq], inc="sv")
        p.op("vector", lambda v, s=s: v.tensor_tensor(
            out=s[:, 3:4], in0=s[:, 2:3], in1=s[:, 2:3], op=ALU.mult), inc="sv")
        p.op("vector", lambda v, s=s: v.tensor_tensor(
            out=s[:, 1:2], in0=s[:, 1:2], in1=s[:, 3:4], op=ALU.subtract), inc="sv")
        mv = p.op("vector", lambda v, s=s: v.reciprocal(
            out=s[:, 1:2], in_=s[:, 1:2]), inc="sv")
        mr = p.op("scalar", lambda sc, s=s: sc.activation(
            out=s[:, 3:4], in_=s[:, 1:2], func=AF.Sqrt), waits=[mv], inc="sa")
        mb = p.op("vector", lambda v, a=xbuf[par], o=xnb[par], s=s: v.tensor_scalar(
            out=o[:], in0=a[:], scalar1=s[:, 2:3], scalar2=s[:, 3:4],
            op0=ALU.subtract, op1=ALU.mult), waits=[mr], inc="sv")
        last_use = [mb]
        if fp32_cb is not None:
            m32 = p.op("vector", lambda v, a=xbuf[par], o=xn32[par], s=s: v.tensor_scalar(
                out=o[:], in0=a[:], scalar1=s[:, 2:3], scalar2=s[:, 3:4],
                op0=ALU.subtract, op1=ALU.mult), inc="sv")
            cbm = fp32_cb(tc, xn32[par], m32)
            last_use.extend(cbm)
        if xn_dram is not None:
            md = p.dma("gpsimd", xn_dram[tc * 128:(tc + 1) * 128, :], xnb[par][:],
                       waits=[mb], inc="n%d" % par)
            last_use.append(md)
            last.append(md)
        if xnT_sb is not None:
            tpar = tc & 1
            mts = []
            for fc in range(8):
                mts.append(p.op("tensor", lambda t, a=xnb[par], f=fc, pp=tp_ps[tpar]:
                           t.transpose(out=pp[:, f * 128:(f + 1) * 128],
                                       in_=a[:, f * 128:(f + 1) * 128],
                                       identity=identb[:]),
                           waits=([mb] + bc.tp_war[tpar]) if fc == 0 else [mb],
                           inc="sp"))
            me = None
            for fc in range(8):
                me = p.op("vector", lambda v, d=xnT_sb[fc], tcc=tc, f=fc, pp=tp_ps[tpar]:
                          v.tensor_copy(out=d[:, tcc * 128:(tcc + 1) * 128],
                                        in_=pp[:, f * 128:(f + 1) * 128]),
                          waits=[mts[-1]], inc="sv")
            bc.tp_war[tpar] = [me]
            last_use.append(me)
            last.append(me)
        if xnT_dram is not None:
            tpar = tc & 1
            mts = []
            for fc in range(8):
                mts.append(p.op("tensor", lambda t, a=xnb[par], f=fc, pp=tp_ps[tpar]:
                           t.transpose(out=pp[:, f * 128:(f + 1) * 128],
                                       in_=a[:, f * 128:(f + 1) * 128],
                                       identity=identb[:]),
                           waits=([mb] + stg_free[par] + bc.tp_war[tpar]) if fc == 0 else [mb],
                           inc="sp"))
            me = p.op("vector", lambda v, d=stg[par], pp=tp_ps[tpar]: v.tensor_copy(
                out=d[:], in_=pp[:]), waits=[mts[-1]], inc="sv")
            bc.tp_war[tpar] = [me]
            dst = xnT_dram.rearrange("(c p) t -> p c t", p=128)[
                :, :, tc * 128:(tc + 1) * 128]
            md = p.dma("gpsimd", dst, stg[par][:].rearrange("p (c f) -> p c f", c=8),
                       waits=[me], inc="s%d" % par)
            stg_free[par] = [md]
            last_use.append(md)
            last.append(md)
        free[par] = last_use
    return last



def emit_A(bc, x_src, xo_dram, xnT_dram, wait0=()):
    import os
    A_PAIRS = int(os.environ.get("A_PAIRS", "8"))
    A_SECT = os.environ.get("A_SECT", "all")
    p = bc.p
    SCALE = 0.125

    identb = bc.sb("A_identb", [128, 128], BF16)
    cosT = bc.sb("A_cosT", [128, T], F32)
    sinT = bc.sb("A_sinT", [128, T], F32)
    r2t = bc.sb("A_r2t", [128, 128], BF16)
    mcau = bc.sb("A_mcau", [128, 128], F32)
    mupp = bc.sb("A_mupp", [128, 128], F32)
    mglob = bc.sb("A_mglob", [128, 128], F32)
    wo0 = [bc.sb("A_wo0_%d" % i, [128, 1024], BF16) for i in range(8)]
    tp_ps = [bc.ps("A_tp%d" % i, [128, 1024], BF16) for i in range(2)]

    p.phase_begin()
    p.dma("gpsimd", identb[:], bc.ein("ident"), inc="cgA")
    LVL = int(os.environ.get("A_CONSTLVL", "4"))
    if LVL >= 1:
        p.dma("sync", cosT[:], bc.ein("cosT"), inc="cstA")
        p.dma("sync", sinT[:], bc.ein("sinT"), inc="cstA")
    if LVL >= 2:
        p.dma("sync", mcau[:], bc.ein("mask_causal"), inc="cstA")
        p.dma("sync", mupp[:], bc.ein("mask_upper"), inc="cstA")
        p.dma("sync", mglob[:], bc.ein("mask_glob"), inc="cstA")
    if LVL >= 3:
        p.dma("gpsimd", r2t[:], bc.ein("R2T"), inc="cgA")
    if LVL >= 4:
        for i in range(8):
            p.dma("gpsimd", wo0[i][:], bc.ein("WoT0")[i * 128:(i + 1) * 128, :],
                  inc="cgA")
    cst = [p.mark("cstA"), p.mark("cgA")]
    m_r2t = m_cos = m_sin = m_mc = m_mu = cst[0]
    m_wo = [cst[1]] * 8
    cst_w = cst

    emit_ln(bc, "Aln", x_src, identb, tp_ps, xnT_dram=xnT_dram,
            wait0=list(wait0) + cst_w)
    ln_done = [p.mark("s0"), p.mark("s1"), p.mark("sv"), p.mark("sp")]

    wq = [bc.sb("A_wq%d" % i, [128, 128], BF16) for i in range(8)]
    wk = [bc.sb("A_wk%d" % i, [128, 128], BF16) for i in range(8)]
    wv = [bc.sb("A_wv%d" % i, [128, 128], BF16) for i in range(8)]
    xnt_s = [bc.sb("A_xnt%d" % i, [128, 512], BF16) for i in range(2)]
    qraw = bc.sb("A_qraw", [128, T], BF16)
    kraw = bc.sb("A_kraw", [128, T], BF16)
    qfin = bc.sb("A_qfin", [128, T], BF16)
    kf01 = [bc.sb("A_kf0", [128, T], BF16), bc.sb("A_kf1", [128, T], BF16)]
    vT = bc.sb("A_vT", [128, T], BF16)
    vtok = bc.sb("A_vtok", [128, T], BF16)
    t1b = bc.sb("A_t1", [128, 512], F32)
    t2b = bc.sb("A_t2", [128, 512], F32)
    attnT = [bc.sb("A_attnT%d" % i, [128, T], BF16) for i in range(8)]
    prow = bc.sb("A_prow", [128, 512], BF16)
    ptt3 = bc.sb("A_ptt", [128, 384], BF16)
    rmax = bc.sb("A_rmax", [128, 1], F32)
    nbias = bc.sb("A_nbias", [128, 1], F32)
    rsum = bc.sb("A_rsum", [128, 1], F32)

    qps = bc.ps("A_qps", [128, 512])
    kps = bc.ps("A_kps", [128, 512])
    vps = bc.ps("A_vps", [128, 512])
    rps = bc.ps("A_rps", [128, 512])
    sps = bc.ps("A_sps", [128, 512])
    aps = bc.ps("A_aps", [128, 128])

    if A_SECT == "ln":
        p.phase_end()
        return []
    mz0 = p.op("vector", lambda v: v.memset(kf01[0][64:128, :], 0.0), inc="sv")
    mz1 = p.op("vector", lambda v: v.memset(kf01[1][0:64, :], 0.0), inc="sv")

    wload_free = []
    xnt_free = [[], []]
    ev_free = {"q": [], "k": [], "v": []}
    rope_free = []
    vtok_free = ln_done
    s_free = []
    exp_free = []
    pt_free = [[], [], []]
    ae_free = []
    pair_done = []

    for pp in range(A_PAIRS):
        qc, kc, vc = pp * 128, 1024 + pp * 128, 2048 + pp * 128
        wl = []
        for ic in range(8):
            wl.append(p.dma("gpsimd", wq[ic][:],
                            bc.ein("WqkvT0")[ic * 128:(ic + 1) * 128, qc:qc + 128],
                            waits=wload_free, inc="wlA"))
            wl.append(p.dma("gpsimd", wk[ic][:],
                            bc.ein("WqkvT0")[ic * 128:(ic + 1) * 128, kc:kc + 128],
                            inc="wlA"))
            wl.append(p.dma("gpsimd", wv[ic][:],
                            bc.ein("WqkvT0")[ic * 128:(ic + 1) * 128, vc:vc + 128],
                            inc="wlA"))
        wload_free = []
        mm_marks = []
        for tcb in range(8):
            sl = slice(tcb * 512, (tcb + 1) * 512)
            mq = mk_ = mv_ = None
            for ic in range(8):
                par = ic & 1
                lx = p.dma("sync", xnt_s[par][:],
                           xnT_dram[ic * 128:(ic + 1) * 128, sl],
                           waits=xnt_free[par] + ln_done, inc="xt%d" % par)
                st_ = (ic == 0)
                en_ = (ic == 7)
                mq = p.op("tensor", lambda t, w=wq[ic], x=xnt_s[par], s=st_, e=en_:
                          t.matmul(out=qps[:], lhsT=w[:], rhs=x[:], start=s, stop=e),
                          waits=[lx, wl[-1]] + (ev_free["q"] if ic == 0 else []),
                          inc="sp")
                mk_ = p.op("tensor", lambda t, w=wk[ic], x=xnt_s[par], s=st_, e=en_:
                           t.matmul(out=kps[:], lhsT=w[:], rhs=x[:], start=s, stop=e),
                           waits=[wl[-1]] + (ev_free["k"] if ic == 0 else []),
                           inc="sp")
                mv_ = p.op("tensor", lambda t, w=wv[ic], x=xnt_s[par], s=st_, e=en_:
                           t.matmul(out=vps[:], lhsT=w[:], rhs=x[:], start=s, stop=e),
                           waits=[wl[-1]] + (ev_free["v"] if ic == 0 else []),
                           inc="sp")
                xnt_free[par] = [mv_]
            eq = p.op("vector", lambda v, s=sl: v.tensor_copy(out=qraw[:, s], in_=qps[:]),
                      waits=[mq] + rope_free, inc="sv")
            ek = p.op("scalar", lambda sc, s=sl: sc.copy(out=kraw[:, s], in_=kps[:]),
                      waits=[mk_] + rope_free, inc="sa")
            ev = p.op("vector", lambda v, s=sl: v.tensor_copy(out=vT[:, s], in_=vps[:]),
                      waits=[mv_] + vtok_free, inc="sv")
            ev_free = {"q": [eq], "k": [ek], "v": [ev]}
            mm_marks.append((eq, ek, ev))
        rp_marks = []
        rps_free = []
        for tcb in range(8):
            sl = slice(tcb * 512, (tcb + 1) * 512)
            eq, ek, ev = mm_marks[tcb]
            for which, (raw, mk2) in enumerate(((qraw, eq), (kraw, ek))):
                mr = p.op("tensor", lambda t, r=raw, s=sl: t.matmul(
                    out=rps[:], lhsT=r2t[:], rhs=r[:, s], start=True, stop=True),
                    waits=[mk2] + cst_w + rps_free, inc="sp")
                m1 = p.op("gpsimd", lambda g, r=raw, s=sl: g.tensor_tensor(
                    out=t1b[:], in0=r[:, s], in1=cosT[:, s], op=ALU.mult),
                    waits=[mk2] + cst_w + rp_marks[-1:], inc="gv")
                m2 = p.op("vector", lambda v, s=sl: v.tensor_tensor(
                    out=t2b[:], in0=rps[:], in1=sinT[:, s], op=ALU.mult),
                    waits=[mr] + cst_w, inc="sv")
                rps_free = [m2]
                if which == 0:
                    m3 = p.op("vector", lambda v, s=sl: v.tensor_tensor(
                        out=qfin[:, s], in0=t1b[:], in1=t2b[:], op=ALU.add),
                        waits=[m1, m2], inc="sv")
                    rp_marks.append(m3)
                else:
                    p.op("vector", lambda v, s=sl: v.tensor_tensor(
                        out=kf01[0][0:64, s], in0=t1b[0:64, :], in1=t2b[0:64, :],
                        op=ALU.add), waits=[m1, m2, mz0], inc="sv")
                    m3 = p.op("vector", lambda v, s=sl: v.tensor_tensor(
                        out=kf01[1][64:128, s], in0=t1b[64:128, :], in1=t2b[64:128, :],
                        op=ALU.add), waits=[mz1], inc="sv")
                    rp_marks.append(m3)
        rope_free = rp_marks[-2:]
        rope_all = rp_marks[-1]
        if A_SECT == "qkv":
            rope_free = []
            pair_done.append(mm_marks[-1][0])
            continue
        vt_last = None
        for g in range(4):
            tpar = g & 1
            mts = []
            for j in range(8):
                kt = g * 8 + j
                ev = mm_marks[kt // 4][2]
                mts.append(p.op("tensor", lambda t, k=kt, j_=j, pp_=tp_ps[tpar]:
                           t.transpose(out=pp_[:, j_ * 128:(j_ + 1) * 128],
                                       in_=vT[:, k * 128:(k + 1) * 128],
                                       identity=identb[:]),
                           waits=([ev] + vtok_free + bc.tp_war[tpar]) if j == 0 else [ev],
                           inc="sp"))
            vt_last = p.op("vector", lambda v, g_=g, pp_=tp_ps[tpar]: v.tensor_copy(
                out=vtok[:, g_ * 1024:(g_ + 1) * 1024], in_=pp_[:]),
                waits=[mts[-1]], inc="sv")
            bc.tp_war[tpar] = [vt_last]
        v_all = vt_last
        if A_SECT in ("rope", "vt"):
            pair_done.append(vt_last)
            continue
        for qt in range(NTC):
            pieces = [(0, qt, mcau)]
            if qt >= 1:
                pieces.append((128, qt - 1, mupp if (qt % 2 == 0) else None))
            if qt >= 2:
                pieces.append((256, 0, mglob))
            width = 128 * len(pieces)
            for hh in range(2):
                kfh = kf01[hh]
                sm = []
                for (c0, kt, _msk) in pieces:
                    sm.append(p.op("tensor", lambda t, c=c0, k=kt, f=kfh, q=qt:
                              t.matmul(out=sps[:, c:c + 128],
                                       lhsT=qfin[:, q * 128:(q + 1) * 128],
                                       rhs=f[:, k * 128:(k + 1) * 128],
                                       start=True, stop=True),
                              waits=[rope_all] + s_free, inc="sp"))
                mk3 = sm[-1]
                if os.environ.get("A_DBG") and pp == 0 and qt == 0 and hh == 0:
                    dbgS = bc.nc.dram_tensor("dbgS", [128, 128], F32).ap()
                    dbgM = bc.nc.dram_tensor("dbgM", [128, 128], F32).ap()
                    dcp = p.op("vector", lambda v: v.tensor_copy(out=t1b[:, 0:128], in_=sps[:, 0:128]), waits=sm, inc="sv")
                    d1 = p.dma("gpsimd", dbgS, t1b[:, 0:128], waits=[dcp], inc="n1")
                    d2 = p.dma("gpsimd", dbgM, mcau[:], waits=cst_w, inc="n1")
                    sm = sm + [d1, d2, dcp]
                for i, (c0, kt, msk) in enumerate(pieces):
                    if msk is not None:
                        mk3 = p.op("vector", lambda v, c=c0, m=msk: v.tensor_tensor(
                            out=sps[:, c:c + 128], in0=sps[:, c:c + 128], in1=m[:],
                            op=ALU.add), waits=sm + cst_w, inc="sv")
                m1 = p.op("vector", lambda v, w=width: v.reduce_max(
                    out=rmax[:], in_=sps[:, 0:w], axis=AX.X), waits=[mk3] + sm, inc="sv")
                m2 = p.op("vector", lambda v: v.tensor_scalar(
                    out=nbias[:], in0=rmax[:], scalar1=-SCALE, scalar2=None,
                    op0=ALU.mult), inc="sv")
                m3 = p.op("scalar", lambda sc, w=width: sc.activation(
                    out=prow[:, 0:w], in_=sps[:, 0:w], func=AF.Exp, scale=SCALE,
                    bias=nbias[:], accum_out=rsum[:]), waits=[m2] + exp_free, inc="sa")
                m4 = p.op("vector", lambda v: v.reciprocal(out=rsum[:], in_=rsum[:]),
                          waits=[m3], inc="sv")
                m5 = p.op("vector", lambda v, w=width: v.tensor_scalar(
                    out=prow[:, 0:w], in0=prow[:, 0:w], scalar1=rsum[:], scalar2=None,
                    op0=ALU.mult), waits=[m4], inc="sv")
                s_free = [m5]
                exp_free = [m5]
                tpar = hh
                mts = []
                for i, (c0, kt, _msk) in enumerate(pieces):
                    mts.append(p.op("tensor", lambda t, c=c0, i_=i, pp_=tp_ps[tpar]:
                               t.transpose(out=pp_[:, i_ * 128:(i_ + 1) * 128],
                                           in_=prow[:, c:c + 128], identity=identb[:]),
                               waits=([m5] + pt_free[0] + bc.tp_war[tpar]) if i == 0 else [m5],
                               inc="sp"))
                me = p.op("vector", lambda v, w3=128 * len(pieces), pp_=tp_ps[tpar]:
                          v.tensor_copy(out=ptt3[:, 0:w3], in_=pp_[:, 0:w3]),
                          waits=[mts[-1]], inc="sv")
                bc.tp_war[tpar] = [me]
                pev = [me]
                am = None
                for j, (c0, kt, _msk) in enumerate(pieces):
                    am = p.op("tensor", lambda t, k=kt, i_=j,
                              s=(j == 0), e=(j == len(pieces) - 1):
                              t.matmul(out=aps[:],
                                       lhsT=vtok[:, k * 128:(k + 1) * 128],
                                       rhs=ptt3[:, i_ * 128:(i_ + 1) * 128], start=s, stop=e),
                              waits=pev + [v_all] + (ae_free if j == 0 else []),
                              inc="sp")
                pt_free[0] = [am]
                ae = p.op("vector", lambda v, d=attnT[pp], q=qt, b=hh * 64: v.tensor_copy(
                    out=d[b:b + 64, q * 128:(q + 1) * 128], in_=aps[b:b + 64, :]),
                    waits=[am], inc="sv")
                ae_free = [ae]
                if os.environ.get("A_DBG") and pp == 0 and qt == 0 and hh == 0:
                    dbgP = bc.nc.dram_tensor("dbgP", [128, 128], F32).ap()
                    dbgR = bc.nc.dram_tensor("dbgR", [128, 2], F32).ap()
                    d3 = p.dma("gpsimd", dbgP, prow[:, 0:128], waits=[ae], inc="n1")
                    d4 = p.dma("gpsimd", dbgR[:, 0:1], rmax[:], inc="n1", allow_slow_non_contiguous=True)
                    d5 = p.dma("gpsimd", dbgR[:, 1:2], rsum[:], inc="n1", allow_slow_non_contiguous=True)
                    ae_free = ae_free + [d3, d4, d5]
                    s_free = s_free + [d3, d4, d5]
                    exp_free = exp_free + [d3, d4, d5]
        pair_done.append(ae_free[0])
        vtok_free = [ae_free[0]]

    if A_SECT != "all" or A_PAIRS < 8:
        p.phase_end()
        return []
    ost = [bc.sb("A_ost0", [128, 1024], F32)]
    opsum = [(qps, kps), (vps, rps)]
    ev_last = [[], []]
    acc_marks = []
    for qt in range(NTC):
        par = qt & 1
        ps0, ps1 = opsum[par]
        mm0 = mm1 = None
        for pc in range(8):
            st_ = (pc == 0)
            en_ = (pc == 7)
            w0 = [pair_done[pc]] if qt == 0 else []
            mm0 = p.op("tensor", lambda t, pcc=pc, q=qt, s=st_, e=en_, d=ps0: t.matmul(
                out=d[:], lhsT=attnT[pcc][:, q * 128:(q + 1) * 128],
                rhs=wo0[pcc][:, 0:512], start=s, stop=e),
                waits=w0 + cst_w + (ev_last[par] if pc == 0 else []), inc="sp")
            mm1 = p.op("tensor", lambda t, pcc=pc, q=qt, s=st_, e=en_, d=ps1: t.matmul(
                out=d[:], lhsT=attnT[pcc][:, q * 128:(q + 1) * 128],
                rhs=wo0[pcc][:, 512:1024], start=s, stop=e), inc="sp")
        war = acc_marks[-1:]
        e0 = p.op("vector", lambda v, d=ost[0], sp_=ps0: v.tensor_copy(
            out=d[:, 0:512], in_=sp_[:]), waits=[mm1] + war, inc="sv")
        e1 = p.op("scalar", lambda s, d=ost[0], sp_=ps1: s.copy(
            out=d[:, 512:1024], in_=sp_[:]), waits=[mm1] + war, inc="sa")
        ev_last[par] = [e0, e1]
        md = p.dma("gpsimd", xo_dram[qt * 128:(qt + 1) * 128, :], ost[0][:],
                   waits=list(wait0) + [e0, e1], inc="ac%d" % par, accum_op=ALU.add)
        acc_marks.append(md)
    p.phase_end()
    return [p.mark("ac0"), p.mark("ac1")]




def emit_B(bc, x_src, xo_dram, wait0=()):
    p = bc.p
    identb = bc.sb("B_identb", [128, 128], BF16)
    tp_ps = [bc.ps("B_tp%d" % i, [128, 1024], BF16) for i in range(2)]
    xnT = [bc.sb("B_xnT%d" % i, [128, T], BF16) for i in range(8)]
    wg = [bc.sb("B_wg%d" % i, [128, 16], BF16) for i in range(8)]

    p.phase_begin()
    p.dma("gpsimd", identb[:], bc.ein("ident"), waits=list(wait0), inc="cgB")
    for i in range(8):
        p.dma("gpsimd", wg[i][:], bc.ein("WgT")[i * 128:(i + 1) * 128, :],
              waits=list(wait0), inc="cgB")
    cst = p.mark("cgB")

    emit_ln(bc, "Bln", x_src, identb, tp_ps, xnT_sb=xnT,
            wait0=list(wait0) + [cst])
    ln_done = [p.mark("sv"), p.mark("sp")]

    sc = bc.sb("B_sc", [128, 16], F32)
    ga = bc.sb("B_ga", [128, 4], F32)
    gb = bc.sb("B_gb", [128, 4], F32)
    gc = bc.sb("B_gc", [128, 4], F32)
    gd = bc.sb("B_gd", [128, 4], F32)
    ge = bc.sb("B_ge", [128, 4], F32)
    gf = bc.sb("B_gf", [128, 4], F32)
    top8 = bc.sb("B_top8", [128, 8], F32)
    msk = bc.sb("B_msk", [128, 16], F32)
    msk2 = bc.sb("B_msk2", [128, 16], F32)
    t16 = bc.sb("B_t16", [128, 16], F32)
    w1v = bc.sb("B_w1v", [128, 1], F32)
    comb = bc.sb("B_comb", [128, 16 * NTC], F32)
    lgps = bc.ps("B_lgps", [128, 16])

    TT = ALU
    gate_done = []
    lg_free = []
    for tc in range(NTC):
        tsl = slice(tc * 128, (tc + 1) * 128)
        mlg = None
        for hc in range(8):
            mlg = p.op("tensor", lambda t, h=hc, s=tsl, st=(hc == 0), en=(hc == 7):
                       t.matmul(out=lgps[:], lhsT=xnT[h][:, s], rhs=wg[h][:],
                                start=st, stop=en),
                       waits=ln_done + [cst] + (lg_free if hc == 0 else []), inc="sp")
        msig = p.op("scalar", lambda s: s.activation(out=sc[:], in_=lgps[:],
                    func=AF.Sigmoid), waits=[mlg] + gate_done[-1:], inc="sa")
        lg_free = [msig]
        scv = sc[:].rearrange("p (g e) -> p g e", e=4)
        ops = [
            lambda v: v.tensor_tensor(out=ga[:], in0=scv[:, :, 0:1], in1=scv[:, :, 1:2], op=TT.max),
            lambda v: v.tensor_tensor(out=gb[:], in0=scv[:, :, 0:1], in1=scv[:, :, 1:2], op=TT.min),
            lambda v: v.tensor_tensor(out=gc[:], in0=scv[:, :, 2:3], in1=scv[:, :, 3:4], op=TT.max),
            lambda v: v.tensor_tensor(out=gd[:], in0=scv[:, :, 2:3], in1=scv[:, :, 3:4], op=TT.min),
            lambda v: v.tensor_tensor(out=ge[:], in0=ga[:], in1=gc[:], op=TT.min),
            lambda v: v.tensor_tensor(out=gf[:], in0=gb[:], in1=gd[:], op=TT.max),
            lambda v: v.tensor_tensor(out=ga[:], in0=ga[:], in1=gc[:], op=TT.max),
            lambda v: v.tensor_tensor(out=gb[:], in0=ge[:], in1=gf[:], op=TT.max),
            lambda v: v.tensor_tensor(out=ga[:], in0=ga[:], in1=gb[:], op=TT.add),
            lambda v: v.reduce_max(out=w1v[:], in_=ga[:], axis=AX.X),
            lambda v: v.tensor_scalar(out=gc[:], in0=ga[:], scalar1=w1v[:], scalar2=None, op0=TT.is_equal),
            lambda v: v.memset(gd[:, 0:1], 0.0),
            lambda v: v.tensor_copy(out=gd[:, 1:4], in_=gc[:, 0:3]),
            lambda v: v.tensor_tensor(out=gd[:, 2:4], in0=gd[:, 2:4], in1=gd[:, 0:2], op=TT.add),
            lambda v: v.tensor_scalar(out=ge[:], in0=gd[:], scalar1=0.5, scalar2=None, op0=TT.is_lt),
            lambda v: v.tensor_tensor(out=gc[:], in0=gc[:], in1=ge[:], op=TT.mult),
            lambda v: v.tensor_scalar(out=ge[:], in0=gc[:], scalar1=NEG, scalar2=None, op0=TT.mult),
            lambda v: v.tensor_tensor(out=gd[:], in0=ga[:], in1=ge[:], op=TT.add),
            lambda v: v.reduce_max(out=w1v[:], in_=gd[:], axis=AX.X),
            lambda v: v.tensor_scalar(out=ge[:], in0=gd[:], scalar1=w1v[:], scalar2=None, op0=TT.is_equal),
            lambda v: v.memset(gf[:, 0:1], 0.0),
            lambda v: v.tensor_copy(out=gf[:, 1:4], in_=ge[:, 0:3]),
            lambda v: v.tensor_tensor(out=gf[:, 2:4], in0=gf[:, 2:4], in1=gf[:, 0:2], op=TT.add),
            lambda v: v.tensor_scalar(out=gf[:], in0=gf[:], scalar1=0.5, scalar2=None, op0=TT.is_lt),
            lambda v: v.tensor_tensor(out=ge[:], in0=ge[:], in1=gf[:], op=TT.mult),
            lambda v: v.tensor_tensor(out=ga[:], in0=gc[:], in1=ge[:], op=TT.add),
            lambda v: v.tensor_scalar(out=gb[:], in0=ga[:], scalar1=1.0, scalar2=-NEG, op0=TT.subtract, op1=TT.mult),
        ]
        m = None
        for i, f in enumerate(ops):
            m = p.op("vector", f, waits=[msig] if i == 0 else [], inc="sv")
        for g in range(4):
            m = p.op("vector", lambda v, g_=g: v.tensor_scalar(
                out=msk[:, 4 * g_:4 * g_ + 4], in0=sc[:, 4 * g_:4 * g_ + 4],
                scalar1=gb[:, g_:g_ + 1], scalar2=None, op0=TT.add), inc="sv")
        ops2 = [
            lambda v: v.max(out=top8[:], in_=msk[:]),
            lambda v: v.memset(top8[:, 4:8], NEG),
            lambda v: v.match_replace(out=msk2[:], in_to_replace=top8[:], in_values=msk[:], imm_value=NEG),
            lambda v: v.tensor_tensor(out=t16[:], in0=msk[:], in1=msk2[:], op=TT.is_gt),
            lambda v: v.reduce_sum(out=w1v[:], in_=top8[:, 0:4], axis=AX.X),
            lambda v: v.tensor_scalar(out=w1v[:], in0=w1v[:], scalar1=1e-6, scalar2=None, op0=TT.add),
            lambda v: v.reciprocal(out=w1v[:], in_=w1v[:]),
            lambda v: v.tensor_tensor(out=t16[:], in0=t16[:], in1=sc[:], op=TT.mult),
        ]
        for f in ops2:
            m = p.op("vector", f, inc="sv")
        m = p.op("vector", lambda v, c=tc: v.tensor_scalar(
            out=comb[:, c * 16:(c + 1) * 16], in0=t16[:], scalar1=w1v[:],
            scalar2=None, op0=TT.mult), inc="sv")
        gate_done.append(m)
    gate_all = gate_done[-1]

    w1t = [[bc.sb("B_w1t%d_%d" % (sl, i), [128, 1024], BF16) for i in range(8)] for sl in range(1)]
    w3t = [[bc.sb("B_w3t%d_%d" % (sl, i), [128, 1024], BF16) for i in range(8)] for sl in range(1)]
    w2t = [[bc.sb("B_w2t%d_%d" % (sl, i), [128, 1024], BF16) for i in range(8)] for sl in range(1)]
    gf32 = [bc.sb("B_gf32%d" % i, [128, 512], F32) for i in range(2)]
    uf32 = bc.sb("B_uf32", [128, 512], F32)
    guT = [bc.sb("B_guT%d" % i, [128, 512], BF16) for i in range(8)]
    yacc = [bc.sb("B_yacc%d" % i, [128, 1024], F32) for i in range(4)]
    etmp = bc.sb("B_etmp", [128, 1024], F32)
    gps2 = [bc.ps("B_gps%d" % i, [128, 512]) for i in range(2)]
    ups = bc.ps("B_ups", [128, 512])
    eps0 = bc.ps("B_eps0", [128, 512])
    eps1 = bc.ps("B_eps1", [128, 512])

    ln_all = p.mark("sv")
    groups = [("S",)] + [(2 * i, 2 * i + 1) for i in range(8)]
    slot_free = [[], []]
    NSLOT = 1
    silu_free = [[], []]
    gu_free = [[] for _ in range(8)]
    uev_free = []
    yacc_free = [[] for _ in range(4)]
    yacc_dma = [[] for _ in range(4)]
    etmp_free = []
    eps_free = [[], []]
    acc_n = 0
    for gi, grp in enumerate(groups):
        sl = gi % NSLOT
        wl = []
        if grp[0] == "S":
            for hc in range(8):
                wl.append(p.dma("gpsimd", w1t[sl][hc][:], bc.ein("W1sT")[hc * 128:(hc + 1) * 128, :], waits=slot_free[sl], inc="wlB"))
                wl.append(p.dma("gpsimd", w3t[sl][hc][:], bc.ein("W3sT")[hc * 128:(hc + 1) * 128, :], inc="wlB"))
                wl.append(p.dma("gpsimd", w2t[sl][hc][:], bc.ein("W2sT")[hc * 128:(hc + 1) * 128, :], inc="wlB"))
        else:
            for li, e in enumerate(grp):
                for hc in range(8):
                    wl.append(p.dma("gpsimd", w1t[sl][hc][:, li * 512:(li + 1) * 512], bc.ein("W1eT")[e, hc * 128:(hc + 1) * 128, :], waits=slot_free[sl], inc="wlB"))
                    wl.append(p.dma("gpsimd", w3t[sl][hc][:, li * 512:(li + 1) * 512], bc.ein("W3eT")[e, hc * 128:(hc + 1) * 128, :], inc="wlB"))
                for fc in range(4):
                    wl.append(p.dma("gpsimd", w2t[sl][li * 4 + fc][:], bc.ein("W2eT")[e, fc * 128:(fc + 1) * 128, :], inc="wlB"))
        wld = wl[-1]
        last_mm = []
        for tcb in range(8):
            tsl = slice(tcb * 512, (tcb + 1) * 512)
            for li in range(len(grp)):
                shared = grp[0] == "S"
                nfc = 8 if shared else 4
                fo = 0 if shared else li * 512
                w2b = 0 if shared else li * 4
                gu_marks = []
                for fc in range(nfc):
                    gpar = fc & 1
                    gp = gps2[gpar]
                    mg = mu = None
                    for hc in range(8):
                        st = (hc == 0)
                        en = (hc == 7)
                        mg = p.op("tensor", lambda t, h=hc, f=fc, o=fo, s=tsl, sl_=sl, st_=st, en_=en, g_=gp:
                                  t.matmul(out=g_[:], lhsT=w1t[sl_][h][:, o + f * 128:o + (f + 1) * 128],
                                           rhs=xnT[h][:, s], start=st_, stop=en_),
                                  waits=[wld, ln_all] + (silu_free[gpar] if hc == 0 else []), inc="sp")
                        mu = p.op("tensor", lambda t, h=hc, f=fc, o=fo, s=tsl, sl_=sl, st_=st, en_=en:
                                  t.matmul(out=ups[:], lhsT=w3t[sl_][h][:, o + f * 128:o + (f + 1) * 128],
                                           rhs=xnT[h][:, s], start=st_, stop=en_),
                                  waits=(uev_free if hc == 0 else []), inc="sp")
                    ms = p.op("scalar", lambda s, g_=gp, d=gf32[gpar]: s.activation(
                              out=d[:], in_=g_[:], func=AF.Silu), waits=[mg], inc="sa")
                    mu2 = p.op("vector", lambda v: v.tensor_copy(out=uf32[:], in_=ups[:]),
                               waits=[mu], inc="sv")
                    mgu = p.op("vector", lambda v, f=fc, d=gf32[gpar]: v.tensor_tensor(
                        out=guT[f][:], in0=d[:], in1=uf32[:], op=TT.mult),
                        waits=[ms, mu2] + gu_free[fc], inc="sv")
                    silu_free[gpar] = [mgu]
                    uev_free = [mgu]
                    gu_marks.append(mgu)
                    gu_free[fc] = []
                for ts in range(4):
                    me0 = me1 = None
                    for fc in range(nfc):
                        st = (fc == 0)
                        en = (fc == nfc - 1)
                        me0 = p.op("tensor", lambda t, f=fc, ts_=ts, b=w2b, sl_=sl, st_=st, en_=en:
                                   t.matmul(out=eps0[:], lhsT=guT[f][:, ts_ * 128:(ts_ + 1) * 128],
                                            rhs=w2t[sl_][b + f][:, 0:512], start=st_, stop=en_),
                                   waits=gu_marks + (eps_free[0] if fc == 0 else []), inc="sp")
                        me1 = p.op("tensor", lambda t, f=fc, ts_=ts, b=w2b, sl_=sl, st_=st, en_=en:
                                   t.matmul(out=eps1[:], lhsT=guT[f][:, ts_ * 128:(ts_ + 1) * 128],
                                            rhs=w2t[sl_][b + f][:, 512:1024], start=st_, stop=en_),
                                   waits=(eps_free[1] if fc == 0 else []), inc="sp")
                        gu_free[fc].append(me1)
                    tci = tcb * 4 + ts
                    if shared:
                        ev0 = p.op("vector", lambda v, y=yacc[ts]: v.tensor_copy(
                            out=y[:, 0:512], in_=eps0[:]), waits=[me1] + yacc_dma[ts], inc="sv")
                        ev1 = p.op("scalar", lambda s, y=yacc[ts]: s.copy(
                            out=y[:, 512:1024], in_=eps1[:]), waits=[me1] + yacc_dma[ts], inc="sa")
                    elif li == 0:
                        ev0 = p.op("scalar", lambda s, y=yacc[ts], t_=tci, e=grp[0]: s.activation(
                            out=y[:, 0:512], in_=eps0[:], func=AF.Copy,
                            scale=comb[:, t_ * 16 + e:t_ * 16 + e + 1]),
                            waits=[me1, gate_all] + yacc_dma[ts], inc="sa")
                        ev1 = p.op("scalar", lambda s, y=yacc[ts], t_=tci, e=grp[0]: s.activation(
                            out=y[:, 512:1024], in_=eps1[:], func=AF.Copy,
                            scale=comb[:, t_ * 16 + e:t_ * 16 + e + 1]), inc="sa")
                    else:
                        et0 = p.op("scalar", lambda s, t_=tci, e=grp[1]: s.activation(
                            out=etmp[:, 0:512], in_=eps0[:], func=AF.Copy,
                            scale=comb[:, t_ * 16 + e:t_ * 16 + e + 1]),
                            waits=[me1, gate_all] + etmp_free, inc="sa")
                        et1 = p.op("scalar", lambda s, t_=tci, e=grp[1]: s.activation(
                            out=etmp[:, 512:1024], in_=eps1[:], func=AF.Copy,
                            scale=comb[:, t_ * 16 + e:t_ * 16 + e + 1]), inc="sa")
                        ev0 = p.op("vector", lambda v, y=yacc[ts]: v.tensor_tensor(
                            out=y[:], in0=y[:], in1=etmp[:], op=TT.add),
                            waits=[et0, et1], inc="sv")
                        ev1 = ev0
                        etmp_free = [ev0]
                    eps_free = [[ev0], [ev1]]
                    yacc_free[ts] = [ev0, ev1]
                    if li == len(grp) - 1:
                        md = p.dma("gpsimd", xo_dram[tci * 128:(tci + 1) * 128, :],
                                   yacc[ts][:], waits=list(wait0) + [ev0, ev1],
                                   inc="acB%d" % ts, accum_op=ALU.add)
                        yacc_dma[ts] = [md]
                        acc_n += 1
            last_mm = [me1]
        slot_free[sl] = last_mm
    p.phase_end()
    return [p.mark("acB%d" % i) for i in range(4)]




def emit_C(bc, x_src, xo_dram, xnT_dram, wait0=()):
    p = bc.p
    SC = 1.0 / np.sqrt(512.0)
    identb = bc.sb("C_identb", [128, 128], BF16)
    mcau = bc.sb("C_mcau", [128, 128], F32)
    tp_ps = [bc.ps("C_tp%d" % i, [128, 1024], BF16) for i in range(2)]

    p.phase_begin()
    p.dma("gpsimd", identb[:], bc.ein("ident"), waits=list(wait0), inc="cgC")
    p.dma("sync", mcau[:], bc.ein("mask_causal"), waits=list(wait0), inc="cstC")
    cst = [p.mark("cstC"), p.mark("cgC")]

    emit_ln(bc, "Cln", x_src, identb, tp_ps, xnT_dram=xnT_dram,
            wait0=list(wait0) + cst)
    ln_done = [p.mark("s0"), p.mark("s1"), p.mark("sv"), p.mark("sp")]

    wq = [bc.sb("C_wq%d" % i, [128, 512], BF16) for i in range(8)]
    wk = [bc.sb("C_wk%d" % i, [128, 512], BF16) for i in range(8)]
    wv = [bc.sb("C_wv%d" % i, [128, 512], BF16) for i in range(8)]
    woc = [bc.sb("C_wo%d" % i, [128, 1024], BF16) for i in range(4)]
    xnt_f = [bc.sb("C_xnt%d" % i, [128, 512], BF16) for i in range(8)]
    qT = [bc.sb("C_qT%d" % i, [128, T], BF16) for i in range(4)]
    kT = [bc.sb("C_kT%d" % i, [128, T], BF16) for i in range(4)]
    vC = bc.sb("C_v", [128, 32 * 512], BF16)
    peb = [bc.sb("C_peb%d" % i, [128, 512], F32) for i in range(2)]
    srow = bc.sb("C_srow", [128, T], F32)
    prow = bc.sb("C_prow", [128, T], BF16)
    ptC = bc.sb("C_ptC", [128, T], BF16)
    atm = bc.sb("C_atm", [128, 512], BF16)
    attnT4 = bc.sb("C_attnT", [128, 512], BF16)
    ost = [bc.sb("C_ost%d" % i, [128, 1024], F32) for i in range(2)]
    rmax = bc.sb("C_rmax", [128, 1], F32)
    nbias = bc.sb("C_nbias", [128, 1], F32)
    rsum = bc.sb("C_rsum", [128, 1], F32)

    cps = [bc.ps("C_cps%d" % i, [128, 512]) for i in range(2)]
    sps_t = [bc.ps("C_sps%d" % i, [128, 512]) for i in range(2)]
    ops = [bc.ps("C_ops%d" % i, [128, 512]) for i in range(2)]

    wl_free = []
    xnt_free = [[] for _ in range(8)]
    pe_free = [[], []]
    qkv_free = {}
    cps_free = [[], []]
    s_free = [[], []]
    exp_free = []
    pt_free = []
    atm_free = []
    at_free = [[] for _ in range(4)]
    ost_free = [[], []]
    ops_free = []
    acc_last = [[], []]
    head_prev_total = []
    mo_last = None
    for h in range(8):
        base = h * 1536
        wl = []
        for ic in range(8):
            wl.append(p.dma("gpsimd", wq[ic][:],
                            bc.ein("WqkvT1")[ic * 128:(ic + 1) * 128, base:base + 512],
                            waits=wl_free, inc="wlC"))
            wl.append(p.dma("gpsimd", wk[ic][:],
                            bc.ein("WqkvT1")[ic * 128:(ic + 1) * 128, base + 512:base + 1024],
                            inc="wlC"))
            wl.append(p.dma("gpsimd", wv[ic][:],
                            bc.ein("WqkvT1")[ic * 128:(ic + 1) * 128, base + 1024:base + 1536],
                            inc="wlC"))
        for dc in range(4):
            wl.append(p.dma("gpsimd", woc[dc][:],
                            bc.ein("WoT1")[h * 512 + dc * 128:h * 512 + (dc + 1) * 128, :],
                            inc="wlC"))
        wld = wl[-1]
        for tcb in range(8):
            tsl = slice(tcb * 512, (tcb + 1) * 512)
            lds = []
            for ic in range(8):
                lds.append(p.dma("sync", xnt_f[ic][:],
                                 xnT_dram[ic * 128:(ic + 1) * 128, tsl],
                                 waits=xnt_free[ic] + ln_done, inc="xtC%d" % ic))
            for which, (wt, dst) in enumerate(((wq, qT), (wk, kT))):
                for oc in range(4):
                    par = oc & 1
                    pl = p.dma("sync", peb[par][:],
                               bc.ein("peT")[oc * 128:(oc + 1) * 128, tsl],
                               waits=pe_free[par], inc="peC%d" % par)
                    mm = None
                    for ic in range(8):
                        mm = p.op("tensor", lambda t, i=ic, o=oc, s=tsl, w=wt, par_=par,
                                  st=(ic == 0), en=(ic == 7):
                                  t.matmul(out=cps[par_][:], lhsT=w[i][:, o * 128:(o + 1) * 128],
                                           rhs=xnt_f[i][:], start=st, stop=en),
                                  waits=[wld, lds[ic]] + (cps_free[par] if ic == 0 else []),
                                  inc="sp")
                        xnt_free[ic] = [mm]
                    ev = p.op("vector", lambda v, d=dst[oc], s=tsl, par_=par: v.tensor_tensor(
                        out=d[:, s], in0=cps[par_][:], in1=peb[par_][:], op=ALU.add),
                        waits=[mm, pl], inc="sv")
                    pe_free[par] = [ev]
                    cps_free[par] = [ev]
                    qkv_free[(which, oc)] = ev
            for j in range(4):
                par = j & 1
                mm = None
                for ic in range(8):
                    mm = p.op("tensor", lambda t, i=ic, j_=j, st=(ic == 0), en=(ic == 7), par_=par:
                              t.matmul(out=cps[par_][:], lhsT=xnt_f[i][:, j_ * 128:(j_ + 1) * 128],
                                       rhs=wv[i][:], start=st, stop=en),
                              waits=(cps_free[par] if ic == 0 else []), inc="sp")
                    xnt_free[ic] = [mm]
                tj = tcb * 4 + j
                ev = p.op("vector", lambda v, t_=tj, par_=par: v.tensor_copy(
                    out=vC[:, t_ * 512:(t_ + 1) * 512], in_=cps[par_][:]),
                    waits=[mm], inc="sv")
                cps_free[par] = [ev]
                qkv_free[("v", tj)] = ev
        qkv_all = [qkv_free[("v", 31)], qkv_free[(0, 3)], qkv_free[(1, 3)]]
        def emit_tail(st):
            nonlocal pt_free, atm_free, ops_free, mo_last
            qt = st["qt"]
            m5 = st["m5"]
            ptevs = []
            ngr = (qt + 1 + 7) // 8
            for g in range(ngr):
                tpar = g & 1
                lo = g * 8
                hi = min(qt + 1, lo + 8)
                mts = []
                for kt in range(lo, hi):
                    mts.append(p.op("tensor", lambda t, k=kt, j_=kt - lo, pp_=tp_ps[tpar]:
                               t.transpose(out=pp_[:, j_ * 128:(j_ + 1) * 128],
                                           in_=prow[:, k * 128:(k + 1) * 128],
                                           identity=identb[:]),
                               waits=([m5] + bc.tp_war[tpar]) if kt == lo else [m5],
                               inc="sp"))
                me = p.op("vector", lambda v, l=lo, h_=hi, pp_=tp_ps[tpar]: v.tensor_copy(
                    out=ptC[:, l * 128:h_ * 128], in_=pp_[:, 0:(h_ - l) * 128]),
                    waits=[mts[-1]], inc="sv")
                bc.tp_war[tpar] = [me]
                ptevs.append(me)
            pt_free = [ptevs[-1]]
            mm = None
            for kt in range(qt + 1):
                mm = p.op("tensor", lambda t, k=kt, st_=(kt == 0), en=(kt == qt):
                          t.matmul(out=cps[0][:], lhsT=ptC[:, k * 128:(k + 1) * 128],
                                   rhs=vC[:, k * 512:(k + 1) * 512], start=st_, stop=en),
                          waits=ptevs + [qkv_free[("v", 31)]] +
                                (cps_free[0] if kt == 0 else []), inc="sp")
            mev = p.op("vector", lambda v: v.tensor_copy(out=atm[:], in_=cps[0][:]),
                       waits=[mm] + atm_free, inc="sv")
            cps_free[0] = [mev]
            tpar = qt & 1
            mts2 = []
            for dc in range(4):
                mts2.append(p.op("tensor", lambda t, d=dc, pp_=tp_ps[tpar]: t.transpose(
                    out=pp_[:, d * 128:(d + 1) * 128], in_=atm[:, d * 128:(d + 1) * 128],
                    identity=identb[:]),
                    waits=([mev] + bc.tp_war[tpar]) if dc == 0 else [mev], inc="sp"))
            me = p.op("vector", lambda v, pp_=tp_ps[tpar]: v.tensor_copy(
                out=attnT4[:], in_=pp_[:, 0:512]), waits=[mts2[-1]] + at_free[0], inc="sv")
            bc.tp_war[tpar] = [me]
            aevs = [me]
            atm_free = [me]
            par = qt & 1
            mo = [None, None]
            for half in range(2):
                for dc in range(4):
                    mo[half] = p.op("tensor", lambda t, d=dc, hf=half, st_=(dc == 0), en=(dc == 3):
                                    t.matmul(out=ops[hf][:], lhsT=attnT4[:, d * 128:(d + 1) * 128],
                                             rhs=woc[d][:, hf * 512:(hf + 1) * 512],
                                             start=st_, stop=en),
                                    waits=aevs + [wld] + (ops_free if dc == 0 else []),
                                    inc="sp")
            at_free[0] = [mo[1]]
            e0 = p.op("vector", lambda v, d=ost[par]: v.tensor_copy(
                out=d[:, 0:512], in_=ops[0][:]), waits=[mo[0], mo[1]] + acc_last[par], inc="sv")
            e1 = p.op("scalar", lambda s, d=ost[par]: s.copy(
                out=d[:, 512:1024], in_=ops[1][:]), waits=[mo[0], mo[1]] + acc_last[par], inc="sa")
            ost_free[par] = [e0, e1]
            ops_free = [e0, e1]
            md = p.dma("gpsimd", xo_dram[qt * 128:(qt + 1) * 128, :], ost[par][:],
                       waits=list(wait0) + [e0, e1] + head_prev_total,
                       inc="acC%d" % par, accum_op=ALU.add)
            acc_last[par] = [md]
            mo_last = mo[1]

        pending = None
        for qt in range(NTC):
            kext = (qt + 1) * 128
            nkcb = (kext + 511) // 512
            sevs = []
            for kcb in range(nkcb):
                wdt = min(512, kext - kcb * 512)
                spar = kcb & 1
                sp_ = sps_t[spar]
                mm = None
                for dc in range(4):
                    mm = p.op("tensor", lambda t, d=dc, q=qt, k=kcb, w=wdt, o=sp_,
                              st=(dc == 0), en=(dc == 3):
                              t.matmul(out=o[:, 0:w], lhsT=qT[d][:, q * 128:(q + 1) * 128],
                                       rhs=kT[d][:, k * 512:k * 512 + w], start=st, stop=en),
                              waits=qkv_all + s_free[spar], inc="sp")
                ev = p.op("scalar", lambda s, k=kcb, w=wdt, o=sp_: s.copy(
                    out=srow[:, k * 512:k * 512 + w], in_=o[:, 0:w]),
                    waits=[mm] + exp_free, inc="sa")
                s_free[spar] = [ev]
                sevs.append(ev)
            mmask = p.op("vector", lambda v, q=qt: v.tensor_tensor(
                out=srow[:, q * 128:(q + 1) * 128], in0=srow[:, q * 128:(q + 1) * 128],
                in1=mcau[:], op=ALU.add), waits=sevs + cst, inc="sv")
            if pending is not None:
                emit_tail(pending)
            m1 = p.op("vector", lambda v, w=kext: v.reduce_max(
                out=rmax[:], in_=srow[:, 0:w], axis=AX.X), waits=[mmask] + sevs, inc="sv")
            m2 = p.op("vector", lambda v: v.tensor_scalar(
                out=nbias[:], in0=rmax[:], scalar1=-SC, scalar2=None, op0=ALU.mult), inc="sv")
            m3 = p.op("scalar", lambda s, w=kext: s.activation(
                out=prow[:, 0:w], in_=srow[:, 0:w], func=AF.Exp, scale=SC,
                bias=nbias[:], accum_out=rsum[:]), waits=[m2] + pt_free, inc="sa")
            m4 = p.op("vector", lambda v: v.reciprocal(out=rsum[:], in_=rsum[:]),
                      waits=[m3], inc="sv")
            m5 = p.op("vector", lambda v, w=kext: v.tensor_scalar(
                out=prow[:, 0:w], in0=prow[:, 0:w], scalar1=rsum[:], scalar2=None,
                op0=ALU.mult), waits=[m4], inc="sv")
            exp_free = [m5]
            pending = {"qt": qt, "m5": m5}
        emit_tail(pending)
        pending = None
        wl_free = [mo_last]
        head_prev_total = [p.mark("acC0"), p.mark("acC1")]
    p.phase_end()
    return [p.mark("acC0"), p.mark("acC1")]




def emit_D(bc, x_src, xo_dram, wait0=()):
    p = bc.p
    nc = bc.nc
    K1 = 2048
    for _nm in ("saD", "scpD", "gaD0", "gaD1", "gaD2", "gaD3", "cc"):
        p.dma_like.add(_nm)

    xn_dram = nc.dram_tensor("D_xn", [T, DM], BF16).ap()
    sc_loc = nc.dram_tensor("D_scloc", [T, 8], F32).ap()
    sc_full = nc.dram_tensor("D_scfull", [8 * T, 8], F32).ap()
    pairbuf = nc.dram_tensor("D_pairs", [8 * C1, 2], F32).ap()

    identb = bc.sb("D_identb", [128, 128], BF16)
    identf = bc.sb("D_identf", [128, 128], F32)
    triu = bc.sb("D_triu", [128, 128], F32)
    wr = [bc.sb("D_wr%d" % i, [128, 8], F32) for i in range(8)]
    wrb = bc.sb("D_wrb", [1, 8], F32)
    onesf = bc.sb("D_onesf", [1, 128], F32)
    ones128 = bc.sb("D_ones128", [128, 1], F32)
    tp_ps = [bc.ps("D_tp%d" % i, [128, 1024], BF16) for i in range(2)]
    tpf = bc.ps("D_tpf", [128, 128], F32)
    psmall = bc.ps("D_psmall", [128, 512])
    h1ps = bc.ps("D_h1ps", [128, 512])
    eps0 = bc.ps("D_eps0", [128, 512])
    eps1 = bc.ps("D_eps1", [128, 512])
    rps = psmall[:, 0:8]
    tbps = psmall[:, 8:16]
    rkp = psmall[:, 16:48]
    redp = psmall[0:1, 48:88]

    xf32 = [bc.sb("D_xf32_%d" % i, [128, 128], F32) for i in range(8)]
    lgs = bc.sb("D_lgs", [128, 8], F32)
    rmax = bc.sb("D_rmax", [128, 1], F32)
    nbias = bc.sb("D_nbias", [128, 1], F32)
    rsum = bc.sb("D_rsum", [128, 1], F32)
    scores_sb = bc.sb("D_scores", [128, 8 * NTC], F32)

    p.phase_begin()
    p.dma("gpsimd", identb[:], bc.ein("ident"), waits=list(wait0), inc="cgD")
    p.dma("sync", identf[:], bc.ein("ident"), waits=list(wait0), inc="cstD")
    p.dma("sync", triu[:], bc.ein("triuS"), inc="cstD")
    for i in range(8):
        p.dma("sync", wr[i][:], bc.ein("Wr")[i * 128:(i + 1) * 128, :], inc="cstD")
    p.dma("sync", wrb[:], bc.ein("Wrb"), inc="cstD")
    mo1 = p.op("vector", lambda v: v.memset(onesf[:], 1.0), waits=list(wait0), inc="sv")
    mo2 = p.op("vector", lambda v: v.memset(ones128[:], 1.0), inc="sv")
    bc.breg = bc.ctx.enter_context(nc.gpsimd.register("D_bnd"))
    mreg = p.op("gpsimd", lambda g: g.reg_mov(bc.breg, 8 * C1 - 1), inc="gv")
    cst = [p.mark("cstD"), p.mark("cgD"), mo1, mo2]

    rt_state = {"tpf_free": [], "xf_free": [[] for _ in range(8)], "rps_free": [],
                "lg_free": [], "sc_marks": []}

    def router_cb(tc, xn32, mark):
        mes = []
        for hc in range(8):
            mt = p.op("tensor", lambda t, h=hc, a=xn32: t.transpose(
                out=tpf[:], in_=a[:, h * 128:(h + 1) * 128], identity=identf[:]),
                waits=[mark] + cst + rt_state["tpf_free"], inc="sp")
            me = p.op("vector", lambda v, x=xf32[hc]: v.tensor_copy(
                out=x[:], in_=tpf[:]), waits=[mt] + rt_state["xf_free"][hc], inc="sv")
            rt_state["tpf_free"] = [me]
            mes.append(me)
        for hc in range(8):
            mmr = p.op("tensor", lambda t, h=hc, x=xf32[hc], st=(hc == 0):
                       t.matmul(out=rps, lhsT=x[:], rhs=wr[h][:], start=st, stop=False),
                       waits=mes + (rt_state["rps_free"] if hc == 0 else []), inc="sp")
            rt_state["xf_free"][hc] = [mmr]
        mmb = p.op("tensor", lambda t: t.matmul(out=rps, lhsT=onesf[:], rhs=wrb[:],
                   start=False, stop=True), inc="sp")
        ml = p.op("vector", lambda v: v.tensor_copy(out=lgs[:], in_=rps),
                  waits=[mmb] + rt_state["lg_free"], inc="sv")
        rt_state["rps_free"] = [ml]
        m1 = p.op("vector", lambda v: v.reduce_max(out=rmax[:], in_=lgs[:], axis=AX.X), inc="sv")
        m2 = p.op("vector", lambda v: v.tensor_scalar(out=nbias[:], in0=rmax[:],
                  scalar1=-1.0, scalar2=None, op0=ALU.mult), inc="sv")
        m3 = p.op("scalar", lambda s: s.activation(out=lgs[:], in_=lgs[:], func=AF.Exp,
                  bias=nbias[:], accum_out=rsum[:]), waits=[m2], inc="sa")
        m4 = p.op("vector", lambda v: v.reciprocal(out=rsum[:], in_=rsum[:]),
                  waits=[m3], inc="sv")
        m5 = p.op("vector", lambda v, c=tc: v.tensor_scalar(
            out=scores_sb[:, c * 8:(c + 1) * 8], in0=lgs[:], scalar1=rsum[:],
            scalar2=None, op0=ALU.mult), waits=[m4], inc="sv")
        rt_state["lg_free"] = [m5]
        md = p.dma("gpsimd", sc_loc[tc * 128:(tc + 1) * 128, :],
                   scores_sb[:, tc * 8:(tc + 1) * 8], waits=[m5], inc="scD%d" % (tc & 1))
        rt_state["sc_marks"].append(md)
        return [m5]

    emit_ln(bc, "Dln", x_src, identb, tp_ps, xn_dram=xn_dram,
            fp32_cb=router_cb, wait0=list(wait0) + cst)
    sc_done = [p.mark("scD0"), p.mark("scD1")]
    xn_done = [p.mark("n0"), p.mark("n1")]

    mcc = p.op("gpsimd", lambda g: g.collective_compute(
        "AllGather", ALU.bypass, replica_groups=[list(range(8))],
        ins=[sc_loc.opt()], outs=[sc_full.opt()]), waits=sc_done, inc="cc")

    sfb = bc.sb("D_sfb", [128, 2048], F32)
    msf = p.dma("sync", sfb[:], sc_full.rearrange("(p t) e -> p (t e)", p=128),
                waits=[mcc], inc="sfD")

    tlo = bc.sb("D_tlo", [1, 8], F32)
    thi = bc.sb("D_thi", [1, 8], F32)
    tmid = bc.sb("D_tmid", [1, 8], F32)
    ge8 = bc.sb("D_ge8", [1, 8], U32)
    gei = bc.sb("D_gei", [1, 8], U32)
    cnt8 = bc.sb("D_cnt8", [128, 8], F32)
    cmpb = bc.sb("D_cmpb", [128, 256], F32)
    sfv = sfb[:].rearrange("p (t e) -> p t e", e=8)
    mi1 = p.op("vector", lambda v: v.memset(tlo[:], 0.0), inc="sv")
    mi2 = p.op("vector", lambda v: v.memset(thi[:], 2.0), inc="sv")
    mprev = [mi1, mi2, msf]
    cnt_free = []
    for it in range(42):
        ma = p.op("vector", lambda v: v.tensor_tensor(out=tmid[:], in0=tlo[:],
                  in1=thi[:], op=ALU.add), waits=mprev, inc="sv")
        mb = p.op("vector", lambda v: v.tensor_scalar(out=tmid[:], in0=tmid[:],
                  scalar1=0.5, scalar2=None, op0=ALU.mult), inc="sv")
        mt = p.op("tensor", lambda t: t.matmul(out=tbps, lhsT=onesf[:], rhs=tmid[:],
                  start=True, stop=True), waits=[mb] + cst + cnt_free, inc="sp")
        mcs = []
        for e in range(8):
            mcs.append(p.op("vector", lambda v, e_=e: v.tensor_scalar(
                out=cmpb[:], in0=sfv[:, :, e_:e_ + 1], scalar1=tbps[:, e_:e_ + 1],
                scalar2=0.0, op0=ALU.is_ge, op1=ALU.add,
                accum_out=cnt8[:, e_:e_ + 1]), waits=[mt, msf], inc="sv"))
        mrd = p.op("tensor", lambda t: t.matmul(out=redp[:, 0:8], lhsT=ones128[:],
                   rhs=cnt8[:], start=True, stop=True), waits=mcs + cst, inc="sp")
        cnt_free = [mrd]
        mg = p.op("vector", lambda v: v.tensor_scalar(out=ge8[:], in0=redp[0:1, 0:8],
                  scalar1=float(K1) - 0.5, scalar2=None, op0=ALU.is_ge),
                  waits=[mrd], inc="sv")
        ml1 = p.op("vector", lambda v: v.select(out=tlo[:], mask=ge8[:],
                   on_true=tmid[:], on_false=tlo[:], add_drain=True), inc="sv")
        mgi = p.op("vector", lambda v: v.tensor_scalar(out=gei[:], in0=ge8[:],
                   scalar1=0.5, scalar2=None, op0=ALU.is_lt), inc="sv")
        ml2 = p.op("vector", lambda v: v.select(out=thi[:], mask=gei[:],
                   on_true=tmid[:], on_false=thi[:], add_drain=True), inc="sv")
        mprev = [ml1, ml2]
    thr_done = mprev + [p.op("vector", lambda v: v.engine_nop(), waits=mprev, inc="sv")]

    toktab = bc.sb("D_toktab", [128, NTC], F32)
    bigt = bc.sb("D_big", [128, NTC], F32)
    me_ = bc.sb("D_me", [128, NTC], F32)
    minv = bc.sb("D_minv", [128, NTC], U32)
    csl = bc.sb("D_csl", [1, NTC], F32)
    pfa = bc.sb("D_pfa", [1, NTC], F32)
    pfb = bc.sb("D_pfb", [1, NTC], F32)
    rkb = bc.sb("D_rkb", [128, NTC], F32)
    slotu = bc.sb("D_slotu", [128, NTC], U32)
    pft = bc.sb("D_pft", [128, 2 * NTC], F32)
    psrc = [bc.sb("D_psrc%d" % i, [128, 2], F32) for i in range(2)]
    mtk = p.op("gpsimd", lambda g: g.iota(toktab[:], pattern=[[128, NTC]], base=0,
               channel_multiplier=1, allow_small_or_imprecise_dtypes=True), inc="gv")
    mbg = p.op("vector", lambda v: v.memset(bigt[:], 1.0e9), inc="sv")
    pfv = pft[:].rearrange("p (s t) -> p s t", t=2)
    mp1 = p.op("vector", lambda v: v.memset(pfv[:, :, 0:1], float(T)), inc="sv")
    mp2 = p.op("vector", lambda v: v.memset(pfv[:, :, 1:2], 0.0), inc="sv")
    mpf = p.dma("gpsimd", pairbuf.rearrange("(p s) t -> p s t", p=128), pft[:],
                waits=[mp1, mp2], inc="pfD")
    scv = scores_sb[:].rearrange("p (t e) -> p t e", e=8)
    scat_marks = [mpf]
    sl_free = []
    for e in range(8):
        mtb = p.op("tensor", lambda t, e_=e: t.matmul(out=tbps[:, 0:1], lhsT=onesf[:],
                   rhs=tlo[0:1, e_:e_ + 1], start=True, stop=True),
                   waits=thr_done + cnt_free, inc="sp")
        mm1 = p.op("vector", lambda v, e_=e: v.tensor_scalar(
            out=me_[:], in0=scv[:, :, e_:e_ + 1], scalar1=tbps[:, 0:1],
            scalar2=None, op0=ALU.is_ge), waits=[mtb] + sl_free, inc="sv")
        mcs1 = p.op("tensor", lambda t: t.matmul(out=redp[:, 8:40], lhsT=ones128[:],
                    rhs=me_[:], start=True, stop=True), waits=[mm1], inc="sp")
        mcv = p.op("vector", lambda v: v.tensor_copy(out=csl[:], in_=redp[0:1, 8:40]),
                   waits=[mcs1], inc="sv")
        pft_ = [pfa, pfb]
        cur = 0
        mpa = p.op("vector", lambda v, d=pft_[0]: v.memset(d[:, 0:1], 0.0), inc="sv")
        mpb = p.op("vector", lambda v, d=pft_[0]: v.tensor_copy(
            out=d[:, 1:NTC], in_=csl[:, 0:NTC - 1]), inc="sv")
        mst = None
        for k in (1, 2, 4, 8, 16):
            src, dst = pft_[cur], pft_[1 - cur]
            mst = p.op("vector", lambda v, k_=k, s_=src, d_=dst: v.tensor_copy(
                out=d_[:, 0:k_], in_=s_[:, 0:k_]), inc="sv")
            mst = p.op("vector", lambda v, k_=k, s_=src, d_=dst: v.tensor_tensor(
                out=d_[:, k_:NTC], in0=s_[:, k_:NTC], in1=s_[:, 0:NTC - k_],
                op=ALU.add), inc="sv")
            cur = 1 - cur
        mr1 = p.op("tensor", lambda t: t.matmul(out=rkp, lhsT=triu[:], rhs=me_[:],
                   start=True, stop=False), waits=[mm1, mst] + cst, inc="sp")
        mr2 = p.op("tensor", lambda t, a=pft_[cur]: t.matmul(out=rkp, lhsT=onesf[:], rhs=a[:],
                   start=False, stop=True), inc="sp")
        ms1 = p.op("vector", lambda v, e_=e: v.tensor_scalar(
            out=rkb[:], in0=rkp, scalar1=float(e_ * C1), scalar2=None, op0=ALU.add),
            waits=[mr2], inc="sv")
        msi = p.op("vector", lambda v: v.tensor_scalar(out=minv[:], in0=me_[:],
                   scalar1=0.5, scalar2=None, op0=ALU.is_lt), inc="sv")
        ms2 = p.op("vector", lambda v: v.select(out=rkb[:], mask=minv[:],
                   on_true=bigt[:], on_false=rkb[:], add_drain=True), waits=[mbg], inc="sv")
        msl = p.op("vector", lambda v: v.tensor_scalar(out=slotu[:], in0=rkb[:],
                   scalar1=0.0, scalar2=None, op0=ALU.add), inc="sv")
        sl_free = []
        for tc in range(NTC):
            par = tc & 1
            mc1 = p.op("vector", lambda v, c=tc, d=psrc[par]: v.tensor_copy(
                out=d[:, 0:1], in_=toktab[:, c:c + 1]),
                waits=[msl, mtk] + (scat_marks[-2:-1] if len(scat_marks) >= 2 else []), inc="sv")
            mc2 = p.op("vector", lambda v, c=tc, e_=e, d=psrc[par]: v.tensor_copy(
                out=d[:, 1:2], in_=scv[:, c:c + 1, e_:e_ + 1]), inc="sv")
            mdd = p.op("gpsimd", lambda g, c=tc, d=psrc[par]: g.indirect_dma_start(
                out=pairbuf, out_offset=IDX(slotu[:, c:c + 1], 0), in_=d[:],
                in_offset=None, bounds_check=bc.breg, oob_is_err=False),
                waits=[mc1, mc2, mpf, mreg] + scat_marks[-1:],
                inc="scpD", n=16, drain=True)
            scat_marks.append(mdd)
        sl_free = scat_marks[-2:]
    scat_all = [p.mark("scpD")]

    btT = [bc.sb("D_btT%d" % i, [128, 512], BF16) for i in range(8)]
    gbuf = [bc.sb("D_gbuf%d" % i, [128, DM], BF16) for i in range(4)]
    prs = [bc.sb("D_prs%d" % i, [128, 2], F32) for i in range(4)]
    tki = [bc.sb("D_tki%d" % i, [128, 1], U32) for i in range(4)]
    h1T = [bc.sb("D_h1T%d" % i, [128, 512], BF16) for i in range(32)]
    w1s = [bc.sb("D_w1s%d" % i, [128, 1024], BF16) for i in range(2)]
    w2s = [bc.sb("D_w2s%d" % i, [128, 1024], BF16) for i in range(32)]
    b1t = bc.sb("D_b1t", [128, 32], F32)
    b2r = bc.sb("D_b2r", [1, 1024], F32)
    ctb = [bc.sb("D_ctb%d" % i, [128, DM], F32) for i in range(2)]

    mz = [p.op("vector", lambda v, g=gbuf[j]: v.memset(g[:], 0.0), inc="sv")
          for j in range(4)]
    W1 = bc.ein("W1x")
    W2 = bc.ein("W2x")
    gb_free = [[] for _ in range(4)]
    bt_free = [[] for _ in range(8)]
    h1_free = [[] for _ in range(32)]
    w1_free = [[], []]
    h1ps_free = []
    eps_fr = [[], []]
    w2_free = [[] for _ in range(32)]
    b_free = []
    ctb_free = [[], []]
    sa_last = []
    prs_free = [[] for _ in range(4)]
    for e in range(8):
        mb1 = p.dma("sync", b1t[:], bc.ein("B1x")[e], waits=b_free, inc="b1D")
        mb2 = p.dma("sync", b2r[:], bc.ein("B2x")[e], waits=b_free, inc="b2D")
        gevs = []
        for j in range(4):
            mpr = p.dma("sync", prs[j][:],
                        pairbuf[e * C1 + j * 128:e * C1 + (j + 1) * 128, :],
                        waits=scat_all + prs_free[j], inc="prD%d" % j)
            mti = p.op("vector", lambda v, j_=j: v.tensor_scalar(out=tki[j_][:],
                       in0=prs[j_][:, 0:1], scalar1=0.0, scalar2=None,
                       op0=ALU.add), waits=[mpr], inc="sv")
            assert T - 1 == 8 * C1 - 1
            mga = p.op("gpsimd", lambda g, j_=j: g.indirect_dma_start(
                out=gbuf[j_][:], out_offset=None, in_=xn_dram,
                in_offset=IDX(tki[j_][:, 0:1], 0), bounds_check=bc.breg,
                oob_is_err=False), waits=[mti, mz[j], mreg] + xn_done + gb_free[j],
                inc="gaD%d" % j, n=16, drain=True)
            tpar = j & 1
            mts = []
            for hc in range(8):
                mts.append(p.op("tensor", lambda t, j_=j, h=hc, pp_=tp_ps[tpar]:
                           t.transpose(out=pp_[:, h * 128:(h + 1) * 128],
                                       in_=gbuf[j_][:, h * 128:(h + 1) * 128],
                                       identity=identb[:]),
                           waits=([mga] + bc.tp_war[tpar]) if hc == 0 else [mga],
                           inc="sp"))
            me2 = None
            for hc in range(8):
                me2 = p.op("vector", lambda v, j_=j, h=hc, pp_=tp_ps[tpar]: v.tensor_copy(
                    out=btT[h][:, j_ * 128:(j_ + 1) * 128],
                    in_=pp_[:, h * 128:(h + 1) * 128]),
                    waits=[mts[-1]] + bt_free[hc], inc="sv")
            bc.tp_war[tpar] = [me2]
            gevs.append(me2)
        for fc in range(32):
            par = fc & 1
            dram_ap = bass.AP(W1.tensor, W1.offset + e * (1024 * 4096) + fc * 128,
                              [[4096, 128], [128 * 4096, 8], [1, 128]])
            mw = p.dma("gpsimd", w1s[par][:].rearrange("p (c f) -> p c f", c=8),
                       dram_ap, waits=w1_free[par], inc="w1D%d" % par)
            mm = None
            for hc in range(8):
                mm = p.op("tensor", lambda t, h=hc, par_=par, st=(hc == 0), en=(hc == 7):
                          t.matmul(out=h1ps[:], lhsT=w1s[par_][:, h * 128:(h + 1) * 128],
                                   rhs=btT[h][:], start=st, stop=en),
                          waits=[mw] + gevs + ((h1_free[fc] + h1ps_free) if hc == 0 else []),
                          inc="sp")
            w1_free[par] = [mm]
            for hc in range(8):
                bt_free[hc] = [mm]
            mh = p.op("scalar", lambda s, f=fc: s.activation(
                out=h1T[f][:], in_=h1ps[:], func=AF.Relu, bias=b1t[:, f:f + 1]),
                waits=[mm, mb1], inc="sa")
            h1_free[fc] = [mh]
            h1ps_free = [mh]
        w2l = []
        for fc in range(32):
            w2l.append(p.dma("gpsimd", w2s[fc][:],
                             W2[e * 4096 + fc * 128:e * 4096 + (fc + 1) * 128, :] if False else
                             bass.AP(W2.tensor, W2.offset + e * (4096 * 1024) + fc * 128 * 1024,
                                     [[1024, 128], [1, 1024]]),
                             waits=w2_free[fc], inc="w2D"))
        w2_all = p.mark("w2D")
        for ts in range(4):
            cpar = ts & 1
            me0 = me1 = None
            for half, epp in ((0, eps0), (1, eps1)):
                mm = None
                for fc in range(32):
                    mm = p.op("tensor", lambda t, f=fc, ts_=ts, hf=half, ep=epp,
                              st=(fc == 0):
                              t.matmul(out=ep[:], lhsT=h1T[f][:, ts_ * 128:(ts_ + 1) * 128],
                                       rhs=w2s[f][:, hf * 512:(hf + 1) * 512],
                                       start=st, stop=False),
                              waits=(h1_free[fc] or [None]) + [w2_all] +
                                    ((ctb_free[cpar] + eps_fr[half]) if fc == 0 else []),
                              inc="sp")
                mm = p.op("tensor", lambda t, hf=half, ep=epp: t.matmul(
                    out=ep[:], lhsT=onesf[:], rhs=b2r[0:1, hf * 512:(hf + 1) * 512],
                    start=False, stop=True), waits=[mb2], inc="sp")
                if half == 0:
                    me0 = mm
                else:
                    me1 = mm
            ct0 = p.op("scalar", lambda s, ts_=ts, c=ctb[cpar]: s.activation(
                out=c[:, 0:512], in_=eps0[:], func=AF.Copy,
                scale=prs[ts_][:, 1:2]), waits=[me0, me1], inc="sa")
            ct1 = p.op("vector", lambda v, ts_=ts, c=ctb[cpar]: v.tensor_scalar(
                out=c[:, 512:1024], in0=eps1[:], scalar1=prs[ts_][:, 1:2],
                scalar2=None, op0=ALU.mult), waits=[me1], inc="sv")
            eps_fr = [[ct0], [ct1]]

            msa = p.op("gpsimd", lambda g, ts_=ts, c=ctb[cpar]: g.indirect_dma_start(
                out=xo_dram, out_offset=IDX(tki[ts_][:, 0:1], 0), in_=c[:],
                in_offset=None, compute_op=ALU.add),
                waits=list(wait0) + [ct0, ct1] + sa_last, inc="saD", n=16, drain=True)
            sa_last = [msa]
            ctb_free[cpar] = [msa]
            gb_free[ts] = [msa]
            prs_free[ts] = [msa]
        b_free = [mh, ct1]
        for fc in range(32):
            w2_free[fc] = [me1]
    p.phase_end()
    return [p.mark("saD")]



def build_program(phases):
    nc = bass.Bass("TRN2", target_bir_lowering=False)
    with ExitStack() as ctx:
        bc = BC(nc, ctx)
        p = bc.p
        x_in = nc.dram_tensor("x_in", [T, DM], F32, kind="ExternalInput").ap()
        out = nc.dram_tensor("out", [T, DM], F32, kind="ExternalOutput").ap()
        x_dram = nc.dram_tensor("x_work", [T + 1, DM], F32).ap()
        xnT_dram = nc.dram_tensor("xnT", [DM, T], BF16).ap()

        p.phase_begin()
        m0 = p.dma("sync", x_dram[0:T, :], x_in, inc="dma")
        p.phase_end()

        prev = [m0]
        for ph in phases:
            bc.scope = ExitStack()
            if ph == "A":
                prev = emit_A(bc, x_dram[0:T, :], x_dram, xnT_dram, wait0=prev)
            elif ph == "B":
                prev = emit_B(bc, x_dram[0:T, :], x_dram, wait0=prev)
            elif ph == "C":
                prev = emit_C(bc, x_dram[0:T, :], x_dram, xnT_dram, wait0=prev)
            elif ph == "D":
                prev = emit_D(bc, x_dram[0:T, :], x_dram, wait0=prev)
            else:
                raise ValueError(ph)
            bc.scope.close()
            bc.scope = None

        p.phase_begin()
        waits = [p.mark(sm) for sm in list(p.cnt)]
        fin = p.dma("sync", out, x_dram[0:T, :], waits=waits, inc="dma")
        p.op("sync", lambda e: e.nop(), waits=[fin])
        p.phase_end()
    return bc


_CACHE = {}


def _get_program(phases):
    key = tuple(phases)
    if key not in _CACHE:
        _CACHE[key] = build_program(phases)
    return _CACHE[key]


def run_phases(phases, x_full, host_w, host_c):
    bc = _get_program(phases)
    tables = dict(host_w)
    tables.update(host_c)
    in_maps = []
    for c in range(8):
        m = {"x_in": np.ascontiguousarray(x_full[c])}
        for name in bc.ext:
            m[name] = tables[name]
        in_maps.append(m)
    res = run_bass_kernel_spmd(bc.nc, in_maps, core_ids=list(range(8)))
    return np.stack([r["out"] for r in res.results], 0)


def kernel(**inp):
    host_w = host_weights(inp)
    host_c = host_consts()
    x = np.ascontiguousarray(inp["L0_attn_x"].astype(np.float32, copy=False))
    import os
    groups = os.environ.get("K_GROUPS", "ABCD")
    for group in groups.split(","):
        x = run_phases(list(group), x, host_w, host_c)
    return x
